# version 29
# baseline (speedup 1.0000x reference)
import contextlib

import numpy as np
import ml_dtypes
import concourse.bass as bass
import concourse.mybir as mybir
from concourse.bass_utils import run_bass_kernel_spmd

F32 = mybir.dt.float32
BF16 = mybir.dt.bfloat16
I32 = mybir.dt.int32
AF = mybir.ActivationFunctionType
ALU = mybir.AluOpType
AX = mybir.AxisListType

NCORES = 8
SEQS = 2
L = 4096
D = 1024
HID = 3072
TB = 17
NW = 16
RTS = 32
INV127 = float(np.float32(1.0 / 127.0))
EPS = 1e-5


class Bld:
    def __init__(self, nc, es):
        self.nc = nc
        self.sems = {}
        self.cnt = {}
        self.waited = {}
        self.es = es

    def sem(self, name):
        self.sems[name] = self.es.enter_context(self.nc.semaphore(name))
        self.cnt[name] = 0

    def inc(self, instr, name, by=1):
        instr.then_inc(self.sems[name], by)
        self.cnt[name] += by
        return (name, self.cnt[name])

    def tok(self, name):
        return (name, self.cnt[name])

    def wait(self, eng, engkey, tok):
        if tok is None:
            return
        name, val = tok
        if val <= 0:
            return
        key = (engkey, name)
        if self.waited.get(key, 0) >= val:
            return
        self.waited[key] = val
        eng.wait_ge(self.sems[name], val)


def build_program(debug=False, seqs=SEQS, l=L):
    TB = l // 256 + 1
    NW = l // 256
    RTS = l // 128
    SEQSv, Lv = seqs, l

    def lastwin(gb):
        if gb is None or gb < 0:
            return None
        s, t = gb // TB, gb % TB
        return s * NW + min(t, NW - 1)

    nc = bass.Bass("TRN2", target_bir_lowering=False)

    x_h = nc.dram_tensor("x", [SEQSv, Lv, D], F32, kind="ExternalInput")
    wqkv_h = nc.dram_tensor("wqkv", [D, 3 * D], F32, kind="ExternalInput")
    bqkv_h = nc.dram_tensor("bqkv", [1, 3 * D], F32, kind="ExternalInput")
    wout_h = nc.dram_tensor("wout", [D, D], F32, kind="ExternalInput")
    bout_h = nc.dram_tensor("bout", [1, D], F32, kind="ExternalInput")
    wg_h = nc.dram_tensor("wg", [HID, D], I32, kind="ExternalInput")
    wu_h = nc.dram_tensor("wu", [HID, D], I32, kind="ExternalInput")
    wd_h = nc.dram_tensor("wd", [D, HID], I32, kind="ExternalInput")
    ident_h = nc.dram_tensor("ident", [128, 128], BF16, kind="ExternalInput")
    out_h = nc.dram_tensor("out", [SEQSv, Lv, D], F32, kind="ExternalOutput")
    ykind = "ExternalOutput" if debug else "Internal"
    y1f_h = nc.dram_tensor("y1dbg", [SEQSv, Lv, D], F32, kind=ykind)
    y1fm_h = nc.dram_tensor("y1fm_scr", [SEQSv, Lv // 256, 128, 8, 256], BF16,
                            kind="Internal")
    wgT_h = nc.dram_tensor("wgT_scr", [24, 128, D], BF16, kind="Internal")
    wuT_h = nc.dram_tensor("wuT_scr", [24, 128, D], BF16, kind="Internal")
    wdT_h = nc.dram_tensor("wdT_scr", [24, 128, D], BF16, kind="Internal")

    es_outer = contextlib.ExitStack()
    b = Bld(nc, es_outer)
    for s in ["T", "V", "A", "G",
              "xq0", "xq1", "xq2", "xq3", "xr0", "xr1",
              "wl0", "wl1", "wl2", "wl3", "wl4", "wg0",
              "tx0", "tx1", "tv0", "tv1", "ty0", "ty1", "tw0", "tw1",
              "wc0", "wc1", "wc2", "wc3", "wi0", "pk0", "pk1", "pf0", "pf1", "pw0", "pw1",
              "yf0", "yf1", "yr0", "yr1", "st0", "st1"]:
        b.sem(s)

    SO = lambda n, sh, dt, **kw: es_outer.enter_context(nc.sbuf_tensor(n, sh, dt, **kw))
    lneps = SO("lneps", [128, 1], F32)
    PS = lambda n, sh, dt: es_outer.enter_context(nc.psum_tensor(n, sh, dt))
    psum_qkv = PS("psum_qkv", [128, D], F32)
    psum_S = PS("psum_S", [128, 512], F32)
    psum_TR = PS("psum_TR", [128, 1024], BF16)
    psum_O = PS("psum_O", [128, 2048], F32)

    es_a = contextlib.ExitStack()
    SA = lambda n, sh, dt, **kw: es_a.enter_context(nc.sbuf_tensor(n, sh, dt, **kw))

    es_qkv = contextlib.ExitStack()
    wqkv_bf = es_qkv.enter_context(
        nc.sbuf_tensor("wqkv_bf", [128, 8, 3 * D], BF16, side="right"))
    wgT_holder = [None]
    wout_bf = SA("wout_bf", [128, 8, D], BF16)
    bq_rep = SA("bq_rep", [128, 3 * D], BF16)
    bo_rep = SA("bo_rep", [128, D], BF16)
    ident_sb = SA("ident_sb", [128, 128], BF16)
    ones128 = SA("ones128", [128, 128], BF16)
    ones1 = SA("ones1", [1, 128], BF16)

    xt_bf = SA("xt_bf", [128, 3, D], BF16)
    x_fm = SA("x_fm", [128, 2, 8, 128], BF16)
    q_tm = SA("q_tm", [128, 3, D], BF16)
    k_tm = SA("k_tm", [128, 3, D], BF16)
    v_tmp = SA("v_tmp", [128, 3, D], BF16)
    v_fm = SA("v_fm", [128, 3, 8, 128], BF16)
    attn_out = SA("attn_out", [128, 8, 512], BF16)
    negmax = SA("negmax", [128, 8], F32)
    ssub = SA("ssub", [128, 512], F32)
    aexp = SA("aexp", [128, 512], F32)
    asum = SA("asum", [128, 8], F32)
    arecip = SA("arecip", [128, 8], F32)
    a_bf = SA("a_bf", [128, 512], BF16)
    a_bd = SA("a_bd", [128, 2, 8, 128], BF16)
    at_bf = SA("at_bf", [128, 1024], BF16)
    xr32 = SA("xr32", [128, 2, D], F32)
    y1pre = SA("y1pre", [128, 2, D], F32)
    y1sq = SA("y1sq", [128, D], F32)
    sums = SA("sums", [128, 2, 1], F32)
    sumsq = SA("sumsq", [128, 2, 1], F32)
    lnm = SA("lnm", [128, 2, 1], F32)
    lnmsq = SA("lnmsq", [128, 2, 1], F32)
    lnvpe = SA("lnvpe", [128, 2, 1], F32)
    lnrstd = SA("lnrstd", [128, 2, 1], F32)
    lnln = SA("lnln", [128, 2, 1], F32)
    lnmb = SA("lnmb", [128, 2, 1], F32)
    y1_f32 = SA("y1_f32", [128, 2, D], F32)
    y1_bf = SA("y1_bf", [128, 2, D], BF16)
    y1fm_stage = SA("y1fm_stage", [128, 4, 8, 256], BF16)

    es_w = contextlib.ExitStack()
    SW = lambda n, sh, dt, **kw: es_w.enter_context(nc.sbuf_tensor(n, sh, dt, **kw))
    w_in_bf = SW("w_in_bf", [128, 3, HID], BF16)
    wT_stage = SW("wT_stage", [128, 8, 128], BF16)
    bq_bf = SW("bq_bf", [1, 3 * D], BF16)
    bo_bf = SW("bo_bf", [1, D], BF16)

    with nc.Block() as blk:
        @blk.gpsimd
        def _(g):
            b.inc(g.memset(ones128[:, :], 1.0), "G", 1)
            b.inc(g.memset(ones1[:, :], 1.0), "G", 1)
            b.inc(g.memset(a_bd[:, :, :, :], 0.0), "G", 1)
            b.inc(g.memset(lneps[:, :], float(EPS)), "G", 1)
        memset_tok = b.tok("G")

        @blk.sync
        def _(q):
            b.inc(q.dma_start(ident_sb[:, :], ident_h.ap()), "wl0", 16)

        @blk.gpsimd
        def _(g):
            for c in range(8):
                b.inc(g.dma_start(wqkv_bf[:, c, :], wqkv_h.ap()[128 * c:128 * (c + 1), :]),
                      "wi0", 16)
                b.inc(g.dma_start(wout_bf[:, c, :], wout_h.ap()[128 * c:128 * (c + 1), :]),
                      "wi0", 16)
            b.inc(g.dma_start(bq_bf[:, :], bqkv_h.ap()), "wi0", 16)
            b.inc(g.dma_start(bo_bf[:, :], bout_h.ap()), "wi0", 16)
        wcast_tok = b.tok("wi0")
        prev_cast = wcast_tok

        @blk.vector
        def _(v):
            b.wait(v, "V", wcast_tok)
            b.inc(v.tensor_scalar_mul(bq_bf[:, :], bq_bf[:, :], 1.0 / 128.0), "V", 1)
            b.inc(v.tensor_scalar_mul(bo_bf[:, :], bo_bf[:, :], 1.0 / 128.0), "V", 1)
        bias_bf_tok = b.tok("V")

        brep_tok = None
        for g3 in range(4):
            @blk.tensor
            def _(t_, g3=g3, brep_tok=brep_tok):
                b.wait(t_, "T", bias_bf_tok)
                b.wait(t_, "T", memset_tok)
                b.wait(t_, "T", brep_tok)
                if g3 < 3:
                    r0 = bq_bf[0:1, 1024 * g3:1024 * g3 + 512]
                    r1 = bq_bf[0:1, 1024 * g3 + 512:1024 * (g3 + 1)]
                else:
                    r0, r1 = bo_bf[0:1, 0:512], bo_bf[0:1, 512:1024]
                b.inc(t_.matmul(psum_qkv[:, 0:512], ones1[:, :], r0, start=True, stop=True), "T", 1)
                b.inc(t_.matmul(psum_qkv[:, 512:1024], ones1[:, :], r1, start=True, stop=True), "T", 1)
            mm_tok = b.tok("T")

            @blk.vector
            def _(v, g3=g3, mm_tok=mm_tok):
                b.wait(v, "V", mm_tok)
                dst = bq_rep[:, 1024 * g3:1024 * (g3 + 1)] if g3 < 3 else bo_rep[:, :]
                b.inc(v.tensor_copy(dst, psum_qkv[:, :]), "V", 1)
            brep_tok = b.tok("V")

        @blk.sync
        def _(q):
            b.wait(q, "Q", prev_cast)
            b.wait(q, "Q", brep_tok)

    cast_done, tx_done, qkv_mm_done, evac_done, tv_done = {}, {}, {}, {}, {}
    evac_half, qkv_half = {}, {}
    xfm_done, tvmm_done, tymm_done = {}, {}, {}
    tr_banks = [[0, None], [1, None], [2, None]]
    tr_idx = [0]
    qkv_free = [None]
    tr_noq = [False]

    def claim_tr():
        bk = tr_banks[tr_idx[0]]
        tr_idx[0] = (tr_idx[0] + 1) % 3
        if tr_noq[0] and bk[0] == 2:
            bk = tr_banks[tr_idx[0]]
            tr_idx[0] = (tr_idx[0] + 1) % 3
        extra = qkv_free[0] if bk[0] == 2 else None
        return bk, extra

    def tr_slice(bi, c0, c1):
        if bi == 0:
            return psum_TR[:, c0:c1]
        if bi == 1:
            return psum_S[:, c0 // 2:c1 // 2].bitcast(BF16)
        return psum_qkv[:, c0 // 2:c1 // 2].bitcast(BF16)
    s_mm_done, ssub_done, exp_done, scale_done, bd_done = {}, {}, {}, {}, {}
    at_mm_done, atevac_done, o_mm_done, oevac_done = {}, {}, {}, {}
    proj_done, resid_done, sq_done, apply_done = {}, {}, {}, {}
    ty_done, pk_done, pf_done, xq_done, xr_done = {}, {}, {}, {}, {}
    last_resid = None

    a_join_T = [wcast_tok, memset_tok, brep_tok, ("wl0", 16)]

    with nc.Block() as blk:
        @blk.tensor
        def _(t_):
            for tk in a_join_T:
                b.wait(t_, "T", tk)

        def emit_xload(gt2):
            if gt2 >= SEQSv * TB:
                return
            s2, t2 = gt2 // TB, gt2 % TB
            xsl = gt2 % 3

            @blk.gpsimd
            def _(g, s2=s2, t2=t2, gt2=gt2, xsl=xsl):
                b.wait(g, "G", tx_done.get(gt2 - 3))
                b.inc(g.dma_start(xt_bf[:, xsl, :], x_h.ap()[s2, 128 * t2:128 * (t2 + 1), :]),
                      "xq%d" % xsl, 16)
            cast_done[gt2] = b.tok("xq%d" % xsl)
            xq_done[gt2] = cast_done[gt2]

        def emit_xtr(gt2):
            if gt2 >= SEQSv * TB:
                return
            sl2 = gt2 % 2
            xsl = gt2 % 3
            bk, qtok = claim_tr()
            bi, freetok = bk[0], bk[1]

            @blk.tensor
            def _(t_, gt2=gt2, xsl=xsl, bi=bi, freetok=freetok, qtok=qtok):
                b.wait(t_, "T", cast_done[gt2])
                b.wait(t_, "T", freetok)
                b.wait(t_, "T", qtok)
                for c in range(8):
                    b.inc(t_.transpose(tr_slice(bi, 128 * c, 128 * (c + 1)),
                                       xt_bf[:, xsl, 128 * c:128 * (c + 1)],
                                       ident_sb[:, :]), "T", 1)
            tx_done[gt2] = b.tok("T")

            @blk.vector
            def _(v, gt2=gt2, sl2=sl2, bi=bi):
                b.wait(v, "V", tx_done[gt2])
                b.wait(v, "V", qkv_mm_done.get(gt2 - 2))
                b.inc(v.tensor_copy(x_fm[:, sl2, :, :].rearrange("p a b -> p (a b)"),
                                    tr_slice(bi, 0, 1024)), "V", 1)
            xfm_done[gt2] = b.tok("V")
            bk[1] = xfm_done[gt2]

        pending_typf = [None]

        pf_idx = [0]
        pf_by_idx = {}

        def emit_ty_one(gw, s, n, k2):
            wsl = n % 4
            if True:
                grt = s * RTS + 2 * n + k2
                rsl = grt % 2
                bk, qtok = claim_tr()
                bi, freetok = bk[0], bk[1]

                @blk.tensor
                def _(t_, grt=grt, rsl=rsl, bi=bi, freetok=freetok, qtok=qtok):
                    b.wait(t_, "T", apply_done[grt])
                    b.wait(t_, "T", freetok)
                    b.wait(t_, "T", qtok)
                    for c in range(8):
                        b.inc(t_.transpose(tr_slice(bi, 128 * c, 128 * (c + 1)),
                                           y1_bf[:, rsl, 128 * c:128 * (c + 1)],
                                           ident_sb[:, :]), "T", 1)
                tymm_done[grt] = b.tok("T")

                pp1 = pf_by_idx.get(pf_idx[0] - 1)
                pp2 = pf_by_idx.get(pf_idx[0] - 2)

                @blk.vector
                def _(v, grt=grt, rsl=rsl, k2=k2, wsl=wsl, bi=bi, pp1=pp1, pp2=pp2):
                    b.wait(v, "V", tymm_done[grt])
                    b.wait(v, "V", pp1)
                    b.wait(v, "V", pp2)
                    dst = y1fm_stage[:, wsl, :, 128 * k2:128 * (k2 + 1)]
                    b.inc(v.tensor_copy(dst, tr_slice(bi, 0, 1024)
                                        .rearrange("p (a b) -> p a b", a=8)),
                          "V", 1)
                ty_done[grt] = b.tok("V")
                bk[1] = ty_done[grt]

        def emit_pf(gw, s, n):
            wsl = n % 4
            nw0 = n - wsl
            pi = pf_idx[0]
            pf_idx[0] += 1
            prev2 = pf_by_idx.get(pi - 2)

            @blk.gpsimd
            def _(g, s=s, n=n, gw=gw, wsl=wsl, nw0=nw0, pi=pi, prev2=prev2):
                b.wait(g, "G", prev2)
                b.wait(g, "G", ty_done[s * RTS + 2 * n])
                b.wait(g, "G", ty_done[s * RTS + 2 * n + 1])
                src_ap = y1fm_stage[:, 0:wsl + 1, :, :].rearrange(
                    "p w a b -> p w (a b)")
                b.inc(g.dma_start(
                    y1fm_h.ap()[s, nw0:n + 1].rearrange(
                        "w p a b -> p w (a b)"),
                    src_ap), "pf%d" % (pi % 2), 16)
            pf_done[gw] = b.tok("pf%d" % (pi % 2))
            pf_by_idx[pi] = pf_done[gw]

        def flush_typf(which=2):
            if pending_typf[0] is not None:
                gw0, s0, n0 = pending_typf[0]
                if which >= 1 and pending_typf[0] not in ty_started:
                    ty_started.add(pending_typf[0])
                    emit_ty_one(gw0, s0, n0, 0)
                if which == 2:
                    emit_ty_one(gw0, s0, n0, 1)
                    if (n0 % 4 == 3) or (n0 == NW - 1):
                        emit_pf(gw0, s0, n0)
                    pending_typf[0] = None
        ty_started = set()

        wprep_state = {"stage_free": None}
        deq_prev = {}
        tw_prev = {}

        def prep_load(i):
            load_src, n_el, _batches = wprep_jobs[i]
            sl = i % 3
            sp = i % 2

            @blk.gpsimd
            def _(g, i=i, sl=sl, load_src=load_src, n_el=n_el):
                b.wait(g, "G", tw_prev.get(i - 3))
                b.inc(g.dma_start(w_in_bf[:, sl, 0:n_el], load_src), "wc%d" % sl, 16)
            wc_tok = b.tok("wc%d" % sl)

            deq_prev[i] = wc_tok

        def prep_transpose(i):
            _src, n_el, sub_transposes = wprep_jobs[i]
            sl = i % 3
            sp = i % 2

            @blk.gpsimd
            def _(g, i=i, sl=sl, n_el=n_el):
                b.wait(g, "G", deq_prev[i])
                b.inc(g.tensor_scalar_mul(w_in_bf[:, sl, 0:n_el], w_in_bf[:, sl, 0:n_el],
                                          INV127), "G", 1)
            dq_tok = b.tok("G")
            for batch in sub_transposes:
                bk, qtok = claim_tr()
                bi, tr_free = bk[0], bk[1]

                @blk.tensor
                def _(t_, i=i, sl=sl, batch=batch, bi=bi, tr_free=tr_free,
                      qtok=qtok, dq_tok=dq_tok):
                    b.wait(t_, "T", dq_tok)
                    b.wait(t_, "T", tr_free)
                    b.wait(t_, "T", qtok)
                    for j, (src_sl, _dst, _comb) in enumerate(batch):
                        b.inc(t_.transpose(tr_slice(bi, 128 * j, 128 * (j + 1)),
                                           w_in_bf[:, sl, src_sl[0]:src_sl[1]],
                                           ident_sb[:, :]), "T", 1)
                tw_tok = b.tok("T")
                tw_prev[i] = tw_tok

                @blk.vector
                def _(v, tw_tok=tw_tok, bi=bi, sf=wprep_state["stage_free"]):
                    b.wait(v, "V", tw_tok)
                    b.wait(v, "V", sf)
                    b.inc(v.tensor_copy(wT_stage[:, :, :].rearrange("p a b -> p (a b)"),
                                        tr_slice(bi, 0, 1024)), "V", 1)
                ev_tok = b.tok("V")
                bk[1] = ev_tok

                @blk.gpsimd
                def _(g, batch=batch, ev_tok=ev_tok, sl=sl):
                    b.wait(g, "G", ev_tok)
                    comb, park_tr = batch[0][2], batch[0][1]
                    if park_tr:
                        b.inc(g.dma_start(comb.transpose([1, 0, 2]), wT_stage[:, :, :]),
                              "pw%d" % sp, 16)
                    else:
                        b.inc(g.dma_start(comb,
                                          wT_stage[:, :, :].rearrange("p a b -> p (a b)")),
                              "pw%d" % sp, 16)
                wprep_state["stage_free"] = b.tok("pw%d" % sp)

        wprep_jobs = []
        for src_h, dst_h in [(wg_h, wgT_h), (wu_h, wuT_h)]:
            for hc in range(24):
                comb = dst_h.ap()[hc]
                batch = [((128 * dc, 128 * (dc + 1)), False, comb) for dc in range(8)]
                wprep_jobs.append((src_h.ap()[128 * hc:128 * (hc + 1), :], D, [batch]))
        for dc in range(8):
            batches = []
            for hb in range(3):
                comb = wdT_h.ap()[8 * hb:8 * (hb + 1), :, 128 * dc:128 * (dc + 1)]
                batches.append([((128 * (8 * hb + j), 128 * (8 * hb + j + 1)), True, comb)
                                for j in range(8)])
            wprep_jobs.append((wd_h.ap()[128 * dc:128 * (dc + 1), :], HID, batches))
        wprep_ld = [0]
        wprep_tr = [0]

        def prep_loads(n_load):
            k = 0
            while wprep_ld[0] < len(wprep_jobs) and k < n_load:
                i = wprep_ld[0]
                if i >= 3 and (i - 3) not in tw_prev:
                    break
                prep_load(i)
                wprep_ld[0] += 1
                k += 1

        def prep_trs(n_tr):
            k = 0
            while wprep_tr[0] < wprep_ld[0] and k < n_tr:
                if wprep_tr[0] + 2 > wprep_ld[0]:
                    break
                prep_transpose(wprep_tr[0])
                wprep_tr[0] += 1
                k += 1

        def emit_wg_stream(gt_last):
            es_qkv.close()
            wgT_holder[0] = es_outer.enter_context(
                nc.sbuf_tensor("wgT", [128, 24, D], BF16, side="right"))
            pw_fin = [b.tok("pw0"), b.tok("pw1")]

            @blk.sync
            def _(q, gt_last=gt_last, pw_fin=pw_fin):
                b.wait(q, "Q", qkv_mm_done[(gt_last, 2)])
                for tk in pw_fin:
                    b.wait(q, "Q", tk)
                b.inc(q.dma_start(wgT_holder[0][:, :, :],
                                  wgT_h.ap().transpose([1, 0, 2])), "wg0", 16)

        emit_xload(0)
        emit_xload(1)
        emit_xload(2)
        emit_xtr(0)
        emit_xtr(1)

        for s in range(SEQSv):
            for t in range(TB + 1):
                gt = s * TB + t
                sl = gt % 2
                sl3 = gt % 3
                have_qkv = t < TB
                have_w1 = 1 <= t <= NW
                have_w2 = t >= 2
                abase = [b.cnt["A"]]
                off_exp = 9 if have_w2 else 1

                prep_loads(2)
                if have_qkv:
                    emit_xload(gt + 3)

                def emit_qkv_pass(p, gt=gt, sl=sl):
                    prev = (gt, p - 1) if p > 0 else (gt - 1, 2)
                    qsfree = tr_banks[2][1] if p == 0 else None

                    @blk.tensor
                    def _(t_, gt=gt, sl=sl, p=p, prev=prev, qsfree=qsfree):
                        b.wait(t_, "T", xfm_done[gt])
                        b.wait(t_, "T", evac_half.get((prev, 0)))
                        b.wait(t_, "T", qsfree)
                        for c in range(8):
                            lhsT = x_fm[:, sl, c, :]
                            b.inc(t_.matmul(psum_qkv[:, 0:512],
                                            lhsT, wqkv_bf[:, c, 1024 * p:1024 * p + 512],
                                            start=(c == 0), stop=False), "T", 1)
                            if c == 0:
                                b.wait(t_, "T", evac_half.get((prev, 1)))
                            b.inc(t_.matmul(psum_qkv[:, 512:1024],
                                            lhsT, wqkv_bf[:, c, 1024 * p + 512:1024 * (p + 1)],
                                            start=(c == 0), stop=False), "T", 1)
                        b.inc(t_.matmul(psum_qkv[:, 0:512], ones128[:, :],
                                        bq_rep[:, 1024 * p:1024 * p + 512],
                                        start=False, stop=True), "T", 1)
                        qkv_half[(gt, p, 0)] = b.tok("T")
                        b.inc(t_.matmul(psum_qkv[:, 512:1024], ones128[:, :],
                                        bq_rep[:, 1024 * p + 512:1024 * (p + 1)],
                                        start=False, stop=True), "T", 1)
                    qkv_mm_done[(gt, p)] = b.tok("T")
                    qkv_half[(gt, p, 1)] = qkv_mm_done[(gt, p)]
                    if p == 2:
                        qkv_mm_done[gt] = b.tok("T")

                    @blk.vector
                    def _(v, gt=gt, sl3=sl3, p=p):
                        b.wait(v, "V", qkv_half[(gt, p, 0)])
                        lw = lastwin(gt - 3)
                        if p == 0:
                            b.wait(v, "V", s_mm_done.get(lw))
                            dst = q_tm
                        elif p == 1:
                            b.wait(v, "V", s_mm_done.get(lw))
                            dst = k_tm
                        else:
                            b.wait(v, "V", tv_done.get(gt - 3))
                            dst = v_tmp
                        b.inc(v.tensor_copy(dst[:, sl3, 0:512], psum_qkv[:, 0:512]), "V", 1)
                        evac_half[((gt, p), 0)] = b.tok("V")
                        b.wait(v, "V", qkv_half[(gt, p, 1)])
                        b.inc(v.tensor_copy(dst[:, sl3, 512:1024], psum_qkv[:, 512:1024]),
                              "V", 1)
                    evac_half[((gt, p), 1)] = b.tok("V")
                    evac_done[(gt, p)] = b.tok("V")

                if have_qkv:
                    emit_qkv_pass(0)
                    emit_qkv_pass(1)

                if have_w1:
                    n1 = t - 1
                    gw1 = s * NW + n1
                    s_sl0 = (s * TB + n1) % 3
                    s_sl1 = (s * TB + n1 + 1) % 3
                    gt_n1 = s * TB + n1 + 1
                    sfree = tr_banks[1][1]
                    exp_done[gw1] = ("A", abase[0] + off_exp)

                    @blk.tensor
                    def _(t_, gw1=gw1, s_sl0=s_sl0, s_sl1=s_sl1, gt_n1=gt_n1, sfree=sfree):
                        b.wait(t_, "T", evac_half[((gt_n1, 1), 0)])
                        b.wait(t_, "T", ssub_done.get(gw1 - 1))
                        b.wait(t_, "T", sfree)
                        for hp in range(8):
                            if hp == 4:
                                b.wait(t_, "T", evac_half[((gt_n1, 1), 1)])
                            for par in range(2):
                                h = 2 * hp + par
                                for j, slj in enumerate((s_sl0, s_sl1)):
                                    b.inc(t_.matmul(
                                        psum_S[64 * par:64 * par + 64, 64 * hp:64 * hp + 64],
                                        q_tm[:, slj, 64 * h:64 * h + 64],
                                        k_tm[:, slj, 64 * h:64 * h + 64],
                                        start=(j == 0), stop=(j == 1),
                                        skip_group_check=True), "T", 1)
                    s_mm_done[gw1] = b.tok("T")

                    @blk.vector
                    def _(v, gw1=gw1):
                        b.wait(v, "V", s_mm_done[gw1])
                        b.wait(v, "V", b.tok("V"))
                        b.inc(v.reduce_max(negmax[:, :],
                                           psum_S[:, :].rearrange("p (s x) -> p s x", s=8),
                                           axis=AX.X, negate=True), "V", 1)
                        b.wait(v, "V", b.tok("V"))
                        b.wait(v, "V", exp_done.get(gw1 - 1))
                        nm = negmax[:, :].unsqueeze(2).broadcast_to([128, 8, 64])
                        b.inc(v.tensor_tensor(ssub[:, :].rearrange("p (s x) -> p s x", s=8),
                                              psum_S[:, :].rearrange("p (s x) -> p s x", s=8),
                                              nm, ALU.add), "V", 1)
                    ssub_done[gw1] = b.tok("V")
                    tr_banks[1][1] = ssub_done[gw1]

                if have_qkv:
                    emit_qkv_pass(2)
                    qkv_free[0] = evac_half[((gt, 2), 1)]
                    emit_xtr(gt + 2)

                if s == SEQSv - 1 and t == TB - 1 and wprep_tr[0] >= 24:
                    emit_wg_stream(gt)

                if have_qkv:
                    vbk, vqtok = claim_tr()
                    vbi, vfreetok = vbk[0], vbk[1]

                    @blk.tensor
                    def _(t_, gt=gt, sl3=sl3, vbi=vbi, vfreetok=vfreetok, vqtok=vqtok):
                        b.wait(t_, "T", evac_half[((gt, 2), 0)])
                        b.wait(t_, "T", vfreetok)
                        b.wait(t_, "T", vqtok)
                        for c in range(8):
                            if c == 4:
                                b.wait(t_, "T", evac_half[((gt, 2), 1)])
                            b.inc(t_.transpose(tr_slice(vbi, 128 * c, 128 * (c + 1)),
                                               v_tmp[:, sl3, 128 * c:128 * (c + 1)],
                                               ident_sb[:, :]), "T", 1)
                    tvmm_done[gt] = b.tok("T")

                    @blk.vector
                    def _(v, gt=gt, sl3=sl3, vbi=vbi):
                        b.wait(v, "V", tvmm_done[gt])
                        b.wait(v, "V", o_mm_done.get(lastwin(gt - 3)))
                        b.inc(v.tensor_copy(v_fm[:, sl3, :, :].rearrange("p a b -> p (a b)"),
                                            tr_slice(vbi, 0, 1024)), "V", 1)
                    tv_done[gt] = b.tok("V")
                    vbk[1] = tv_done[gt]

                if have_w2:
                    n = t - 2
                    gw = s * NW + n
                    sl0 = (s * TB + n) % 3
                    sl1 = (s * TB + n + 1) % 3
                    wpar = gw % 2

                    flush_typf(which=1)

                    @blk.tensor
                    def _(t_, gw=gw, wpar=wpar):
                        b.wait(t_, "T", bd_done[gw])
                        b.wait(t_, "T", oevac_done.get(gw - 1))
                        b.wait(t_, "T", resid_done.get(2 * (gw - 1)))
                        at_dst = psum_O[:, 0:512].bitcast(BF16)
                        for hp in range(8):
                            b.inc(t_.transpose(at_dst[0:64, 128 * hp:128 * (hp + 1)],
                                               a_bd[:, wpar, hp, 0:64], ident_sb[:, :]), "T", 1)
                            b.inc(t_.transpose(at_dst[64:128, 128 * hp:128 * (hp + 1)],
                                               a_bd[:, wpar, hp, 64:128], ident_sb[:, :]),
                                  "T", 1)
                    at_mm_done[gw] = b.tok("T")

                    @blk.vector
                    def _(v, gw=gw):
                        b.wait(v, "V", at_mm_done[gw])
                        b.wait(v, "V", b.tok("V"))
                        b.wait(v, "V", o_mm_done.get(gw - 1))
                        b.inc(v.tensor_copy(at_bf[:, :], psum_O[:, 0:512].bitcast(BF16)), "V", 1)
                    atevac_done[gw] = b.tok("V")

                    @blk.tensor
                    def _(t_, gw=gw, s=s, n=n, sl0=sl0, sl1=sl1):
                        b.wait(t_, "T", atevac_done[gw])
                        b.wait(t_, "T", tv_done[s * TB + n])
                        b.wait(t_, "T", tv_done[s * TB + n + 1])
                        b.wait(t_, "T", oevac_done.get(gw - 1))
                        b.wait(t_, "T", resid_done.get(2 * (gw - 1) + 1))
                        for hp in range(8):
                            for c4 in range(4):
                                slj = sl0 if c4 < 2 else sl1
                                off = 64 * (c4 % 2)
                                lhsT = v_fm[:, slj, hp, off:off + 64]
                                dst0 = 512 * (hp // 2) + 256 * (hp % 2) + 64 * c4
                                for par in range(2):
                                    b.inc(t_.matmul(
                                        psum_O[64 * par:64 * par + 64, dst0:dst0 + 64],
                                        lhsT,
                                        at_bf[:, 128 * hp + 64 * par:128 * hp + 64 * par + 64],
                                        start=True, stop=True,
                                        skip_group_check=True), "T", 1)
                    o_mm_done[gw] = b.tok("T")

                    flush_typf(which=2)
                    prep_trs(2)

                    oevac_done[gw] = ("A", abase[0] + 8)
                    lnstart = abase[0] + (10 if have_w1 else 9)

                    for k2 in range(2):
                        rt = 2 * n + k2
                        grt = s * RTS + rt
                        rsl = grt % 2

                        @blk.sync
                        def _(q, s=s, rt=rt, grt=grt, rsl=rsl):
                            b.wait(q, "Q", resid_done.get(grt - 2))
                            b.inc(q.dma_start(xr32[:, rsl, :],
                                              x_h.ap()[s, 128 * rt:128 * (rt + 1), :]),
                                  "xr%d" % rsl, 16)
                        xr_done[grt] = b.tok("xr%d" % rsl)

                        @blk.tensor
                        def _(t_, gw=gw, n=n, k2=k2, grt=grt, s=s):
                            pj = psum_O[:, 1024 * k2:1024 * (k2 + 1)]
                            rb = 256 * (n % 2) + 128 * k2
                            for c in range(8):
                                lhsT = attn_out[:, c, rb:rb + 128]
                                b.wait(t_, "T", ("A", abase[0] + 1 + max(c, 1 + 4 * k2)))
                                b.inc(t_.matmul(pj[:, 0:512], lhsT,
                                                wout_bf[:, c, 0:512],
                                                start=(c == 0), stop=False,
                                                skip_group_check=True), "T", 1)
                                b.wait(t_, "T", ("A", abase[0] + 1 + max(c, 3 + 4 * k2)))
                                b.inc(t_.matmul(pj[:, 512:1024], lhsT,
                                                wout_bf[:, c, 512:1024],
                                                start=(c == 0), stop=False,
                                                skip_group_check=True), "T", 1)
                            b.inc(t_.matmul(pj[:, 0:512], ones128[:, :],
                                            bo_rep[:, 0:512], start=False, stop=True,
                                            skip_group_check=True), "T", 1)
                            b.inc(t_.matmul(pj[:, 512:1024], ones128[:, :],
                                            bo_rep[:, 512:1024], start=False, stop=True,
                                            skip_group_check=True), "T", 1)
                        proj_done[2 * gw + k2] = b.tok("T")

                        @blk.vector
                        def _(v, grt=grt, rsl=rsl, gw=gw, k2=k2):
                            b.wait(v, "V", proj_done[2 * gw + k2])
                            b.wait(v, "V", xr_done[grt])
                            b.wait(v, "V", apply_done.get(grt - 2))
                            b.wait(v, "V", b.tok("V"))
                            b.inc(v.tensor_tensor(y1pre[:, rsl, :],
                                                  psum_O[:, 1024 * k2:1024 * (k2 + 1)],
                                                  xr32[:, rsl, :], ALU.add), "V", 1)
                        resid_done[grt] = b.tok("V")
                        last_resid = resid_done[grt]

                        apply_done[grt] = ("A", lnstart + 11 + 12 * k2)

                        @blk.gpsimd
                        def _(g, s=s, rt=rt, grt=grt, rsl=rsl, lnstart=lnstart):
                            b.wait(g, "G", ("A", lnstart + 10 + 12 * k2))
                            b.inc(g.dma_start(y1f_h.ap()[s, 128 * rt:128 * (rt + 1), :],
                                              y1_f32[:, rsl, :]), "pk%d" % rsl, 16)
                        pk_done[grt] = b.tok("pk%d" % rsl)

                    pending_typf[0] = (gw, s, n)

                if have_w1 or have_w2:
                    w2ctx = (gw, s, n) if have_w2 else None
                    w1gw = gw1 if have_w1 else None

                    @blk.scalar
                    def _(a, w2ctx=w2ctx, w1gw=w1gw, abase=abase):
                        assert b.cnt["A"] == abase[0], (b.cnt["A"], abase[0])
                        if w2ctx is not None:
                            gw, s_, n = w2ctx
                            b.wait(a, "A", o_mm_done[gw])
                            b.wait(a, "A", proj_done.get(2 * (gw - 2) + 1))
                            rb = 256 * (n % 2)
                            for hp in range(8):
                                base = 512 * (hp // 2) + 256 * (hp % 2)
                                b.inc(a.activation(
                                    attn_out[:, hp, rb:rb + 256]
                                    .rearrange("p (d c) -> p d c", c=4),
                                    psum_O[:, base:base + 256]
                                    .rearrange("p (c d) -> p d c", c=4),
                                    AF.Copy), "A", 1)
                        if w1gw is not None:
                            b.wait(a, "A", ssub_done[w1gw])
                            b.wait(a, "A", scale_done.get(w1gw - 1))
                            b.inc(a.activation(aexp[:, :], ssub[:, :], AF.Exp, scale=0.125),
                                  "A", 1)
                        if w2ctx is not None:
                            gw, s_, n = w2ctx
                            for k2 in range(2):
                                grt = s_ * RTS + 2 * n + k2
                                rsl = grt % 2
                                b.wait(a, "A", resid_done[grt])
                                b.wait(a, "A", b.tok("A"))
                                b.inc(a.activation(y1sq[:, :], y1pre[:, rsl, :], AF.Identity,
                                                   accum_out=sums[:, rsl, :]), "A", 1)
                                b.wait(a, "A", b.tok("A"))
                                b.inc(a.activation(y1sq[:, :], y1pre[:, rsl, :], AF.Square,
                                                   accum_out=sumsq[:, rsl, :]), "A", 1)
                                b.wait(a, "A", b.tok("A"))
                                b.inc(a.activation(lnm[:, rsl, :], sums[:, rsl, :], AF.Identity,
                                                   scale=1.0 / D), "A", 1)
                                b.wait(a, "A", b.tok("A"))
                                b.inc(a.activation(lnmsq[:, rsl, :], lnm[:, rsl, :], AF.Square),
                                      "A", 1)
                                b.inc(a.activation(lnvpe[:, rsl, :], sumsq[:, rsl, :],
                                                   AF.Identity, scale=1.0 / D), "A", 1)
                                b.wait(a, "A", b.tok("A"))
                                b.inc(a.activation(lnvpe[:, rsl, :], lnmsq[:, rsl, :],
                                                   AF.Identity,
                                                   scale=-1.0, bias=lnvpe[:, rsl, :]), "A", 1)
                                b.wait(a, "A", b.tok("A"))
                                b.inc(a.activation(lnln[:, rsl, :], lnvpe[:, rsl, :], AF.Ln,
                                                   bias=lneps[:, :]), "A", 1)
                                b.wait(a, "A", b.tok("A"))
                                b.inc(a.activation(lnrstd[:, rsl, :], lnln[:, rsl, :], AF.Exp,
                                                   scale=-0.5), "A", 1)
                                b.wait(a, "A", b.tok("A"))
                                b.inc(a.activation(lnmb[:, rsl, :], lnm[:, rsl, :], AF.Identity,
                                                   scale=lnrstd[:, rsl, :]), "A", 1)
                                b.wait(a, "A", b.tok("A"))
                                b.inc(a.activation(lnmb[:, rsl, :], lnmb[:, rsl, :], AF.Identity,
                                                   scale=-1.0), "A", 1)
                                b.wait(a, "A", b.tok("A"))
                                b.wait(a, "A", pk_done.get(grt - 2))
                                b.wait(a, "A", tymm_done.get(grt - 2))
                                b.inc(a.activation(y1_f32[:, rsl, :], y1pre[:, rsl, :],
                                                   AF.Identity, bias=lnmb[:, rsl, :],
                                                   scale=lnrstd[:, rsl, :]), "A", 1)
                                b.inc(a.activation(y1_bf[:, rsl, :], y1pre[:, rsl, :],
                                                   AF.Identity, bias=lnmb[:, rsl, :],
                                                   scale=lnrstd[:, rsl, :]), "A", 1)

                if have_w1:
                    @blk.vector
                    def _(v, gw1=gw1):
                        b.wait(v, "V", exp_done[gw1])
                        b.wait(v, "V", b.tok("V"))
                        b.inc(v.reduce_sum(asum[:, :],
                                           aexp[:, :].rearrange("p (s x) -> p s x", s=8),
                                           axis=AX.X), "V", 1)
                        b.wait(v, "V", b.tok("V"))
                        b.inc(v.reciprocal(arecip[:, :], asum[:, :]), "V", 1)
                        b.wait(v, "V", b.tok("V"))
                        ar = arecip[:, :].unsqueeze(2).broadcast_to([128, 8, 64])
                        b.inc(v.tensor_tensor(a_bf[:, :].rearrange("p (s x) -> p s x", s=8),
                                              aexp[:, :].rearrange("p (s x) -> p s x", s=8),
                                              ar, ALU.mult), "V", 1)
                    scale_done[gw1] = b.tok("V")

                    @blk.gpsimd
                    def _(g, gw1=gw1):
                        b.wait(g, "G", scale_done[gw1])
                        b.wait(g, "G", at_mm_done.get(gw1 - 2))
                        b.wait(g, "G", memset_tok)
                        wp1 = gw1 % 2
                        b.inc(g.tensor_copy(
                            a_bd[0:64, wp1, :, 0:64],
                            a_bf[0:64, :].rearrange("p (h x) -> p h x", h=8)), "G", 1)
                        b.inc(g.tensor_copy(
                            a_bd[64:128, wp1, :, 64:128],
                            a_bf[64:128, :].rearrange("p (h x) -> p h x", h=8)), "G", 1)
                    bd_done[gw1] = b.tok("G")

        while wprep_tr[0] < len(wprep_jobs):
            while wprep_ld[0] < len(wprep_jobs) and wprep_ld[0] < wprep_tr[0] + 3:
                prep_load(wprep_ld[0])
                wprep_ld[0] += 1
            prep_transpose(wprep_tr[0])
            wprep_tr[0] += 1
        flush_typf()
        if wgT_holder[0] is None:
            emit_wg_stream(SEQSv * TB - 1)
        wg_tok = b.tok("wg0")

        @blk.sync
        def _(q):
            b.wait(q, "Q", b.tok("pk0"))
            b.wait(q, "Q", b.tok("pk1"))
            b.wait(q, "Q", b.tok("pf0"))
            b.wait(q, "Q", b.tok("pf1"))

    a_end = [b.tok(k) for k in ("T", "V", "A", "G")]
    a_end += [b.tok(k) for k in ("pk0", "pk1", "pf0", "pf1",
                                 "xq0", "xq1", "xq2", "xq3", "xr0", "xr1",
                                 "pw0", "pw1", "wc0", "wc1")]

    es_w.close()
    es_a.close()

    es_b = contextlib.ExitStack()
    SBB = lambda n, sh, dt, **kw: es_b.enter_context(nc.sbuf_tensor(n, sh, dt, **kw))
    wgT = wgT_holder[0]
    wuT = SBB("wuT", [128, 24, D], BF16)
    wdT = SBB("wdT", [128, 24, D], BF16)
    y1fm_sb = SBB("y1fm_sb", [128, 2, 8, 256], BF16)
    y1r32 = SBB("y1r32", [128, 2, 2, D], F32)
    sg_bf = SBB("sg_bf", [128, 2, 256], BF16)
    h_bf = SBB("h_bf", [128, 2, 256], BF16)
    y2pre = SBB("y2pre", [128, 2, D], F32)
    y2sq = SBB("y2sq", [128, D], F32)
    y2 = SBB("y2", [128, 2, 2, D], F32)
    s2s = SBB("s2s", [128, 2, 1], F32)
    s2q = SBB("s2q", [128, 2, 1], F32)
    l2m = SBB("l2m", [128, 2, 1], F32)
    l2msq = SBB("l2msq", [128, 2, 1], F32)
    l2vpe = SBB("l2vpe", [128, 2, 1], F32)
    l2rstd = SBB("l2rstd", [128, 2, 1], F32)
    l2ln = SBB("l2ln", [128, 2, 1], F32)
    l2mb = SBB("l2mb", [128, 2, 1], F32)

    gate_ps = [psum_qkv[:, 0:256], psum_qkv[:, 512:768]]
    up_ps = [psum_S[:, 0:256], psum_TR[:, 0:512].bitcast(F32)[:, 0:256]]
    down_ps = psum_O

    gate_done, silu_done, up_done, mul_done, down_done = {}, {}, {}, {}, {}
    apply2_done = {}
    mb2_toks = {}
    pending_ln2 = [None]
    pending_vln2 = [None]
    dload_done, rload_done, resid2_done, sq2_done, y2_done, store_done = {}, {}, {}, {}, {}, {}
    NGg = SEQSv * (Lv // 256)

    with nc.Block() as blk:
        @blk.tensor
        def _(t_):
            for tk in a_end:
                b.wait(t_, "T", tk)

        @blk.vector
        def _(v):
            for tk in a_end:
                b.wait(v, "V", tk)

        @blk.scalar
        def _(a):
            for tk in a_end:
                b.wait(a, "A", tk)

        @blk.gpsimd
        def _(g):
            for tk in a_end:
                b.wait(g, "G", tk)

        @blk.sync
        def _(q):
            for tk in a_end:
                b.wait(q, "Q", tk)

        wu_tok, wd_tok = {}, {}

        def emit_wu_chunk(hc):
            @blk.sync
            def _(q, hc=hc):
                b.wait(q, "Q", up_done.get((0, hc - 2)))
                b.inc(q.dma_start(wuT[:, hc, :], wuT_h.ap()[hc]),
                      "wl%d" % (hc % 2), 16)
            wu_tok[hc] = b.tok("wl%d" % (hc % 2))

        def emit_wd_chunk(hc):
            @blk.sync
            def _(q, hc=hc):
                b.wait(q, "Q", down_done.get((0, hc - 3)))
                b.inc(q.dma_start(wdT[:, hc, :], wdT_h.ap()[hc]),
                      "wl%d" % (2 + hc % 3), 16)
            wd_tok[hc] = b.tok("wl%d" % (2 + hc % 3))

        def emit_gloads(g):
            s = g // (Lv // 256)
            r0 = 256 * (g % (Lv // 256))
            gsl = g % 2

            @blk.sync
            def _(q, g=g, s=s, r0=r0, gsl=gsl):
                b.wait(q, "Q", up_done.get((g - 2, 23)))
                b.inc(q.dma_start(y1fm_sb[:, gsl, :, :],
                                  y1fm_h.ap()[s, g % (Lv // 256)]), "yf%d" % gsl, 16)
            dload_done[g] = b.tok("yf%d" % gsl)

            @blk.sync
            def _(q, g=g, s=s, r0=r0, gsl=gsl):
                b.wait(q, "Q", resid2_done.get((g - 2, 1)))
                for rs in range(2):
                    b.inc(q.dma_start(y1r32[:, gsl, rs, :],
                                      y1f_h.ap()[s, r0 + 128 * rs:r0 + 128 * (rs + 1), :]),
                          "yr%d" % gsl, 16)
            rload_done[g] = b.tok("yr%d" % gsl)

        emit_gloads(0)
        if NGg > 1:
            emit_gloads(1)
        for c0 in (0, 1):
            emit_wu_chunk(c0)
            emit_wd_chunk(c0)
        emit_wd_chunk(2)

        b2bases = {}

        def nb2(g):
            return b2bases[g] + (24 if g == 0 else 40)

        for g in range(NGg):
            s = g // (Lv // 256)
            r0 = 256 * (g % (Lv // 256))
            gsl = g % 2
            b2base = [b.cnt["A"] if g == 0 else nb2(g - 1)]
            b2bases[g] = b2base[0]

            def emit_down(hc, g=g):
                hsl = hc % 2

                @blk.tensor
                def _(t_, g=g, hc=hc, hsl=hsl):
                    b.wait(t_, "T", wd_tok[hc])
                    b.wait(t_, "T", mul_done[(g, hc)])
                    if hc == 0:
                        b.wait(t_, "T", resid2_done.get((g - 1, 1)))
                    for rs in range(2):
                        lhsT = h_bf[:, hsl, 128 * rs:128 * (rs + 1)]
                        b.inc(t_.matmul(down_ps[:, 1024 * rs:1024 * rs + 512],
                                        lhsT, wdT[:, hc, 0:512],
                                        start=(hc == 0), stop=(hc == 23),
                                        skip_group_check=True), "T", 1)
                        b.inc(t_.matmul(down_ps[:, 1024 * rs + 512:1024 * (rs + 1)],
                                        lhsT, wdT[:, hc, 512:1024],
                                        start=(hc == 0), stop=(hc == 23),
                                        skip_group_check=True), "T", 1)
                down_done[(g, hc)] = b.tok("T")

            for hc in range(24):
                hsl = hc % 2

                @blk.tensor
                def _(t_, g=g, gsl=gsl, hc=hc, hsl=hsl):
                    b.wait(t_, "T", wg_tok)
                    b.wait(t_, "T", dload_done[g])
                    b.wait(t_, "T", silu_done.get((g, hc - 2) if hc >= 2 else (g - 1, 22 + hc)))
                    for dc in range(8):
                        b.inc(t_.matmul(gate_ps[hsl], wgT[:, hc, 128 * dc:128 * (dc + 1)],
                                        y1fm_sb[:, gsl, dc, :],
                                        start=(dc == 0), stop=(dc == 7)), "T", 1)
                gate_done[(g, hc)] = b.tok("T")

                @blk.tensor
                def _(t_, g=g, gsl=gsl, hc=hc, hsl=hsl):
                    b.wait(t_, "T", wu_tok[hc])
                    b.wait(t_, "T", mul_done.get((g, hc - 2) if hc >= 2 else (g - 1, 22 + hc)))
                    for dc in range(8):
                        b.inc(t_.matmul(up_ps[hsl], wuT[:, hc, 128 * dc:128 * (dc + 1)],
                                        y1fm_sb[:, gsl, dc, :],
                                        start=(dc == 0), stop=(dc == 7)), "T", 1)
                up_done[(g, hc)] = b.tok("T")

                silu_done[(g, hc)] = ("A", b2base[0] + (
                    1 + hc if hc < 5 or g == 0 else
                    3 + hc if hc < 6 else
                    7 + hc if hc < 7 else
                    9 + hc if hc < 8 else
                    13 + hc if hc < 9 else
                    15 + hc if hc < 11 else
                    16 + hc if hc < 13 else
                    17 + hc))

                @blk.vector
                def _(v, g=g, hc=hc, hsl=hsl):
                    b.wait(v, "V", up_done[(g, hc)])
                    b.wait(v, "V", silu_done[(g, hc)])
                    b.wait(v, "V", down_done.get((g, hc - 2) if hc >= 2 else (g - 1, 22 + hc)))
                    b.wait(v, "V", b.tok("V"))
                    b.inc(v.tensor_tensor(h_bf[:, hsl, :], up_ps[hsl], sg_bf[:, hsl, :],
                                          ALU.mult), "V", 1)
                mul_done[(g, hc)] = b.tok("V")

                if hc == 9 and pending_vln2[0] is not None:
                    pending_vln2[0](0)
                if hc == 10 and pending_vln2[0] is not None:
                    pending_vln2[0](1)
                    pending_vln2[0] = None
                if hc >= 1:
                    emit_down(hc - 1)
                if g == 0 and hc + 2 <= 23:
                    emit_wu_chunk(hc + 2)
                    if hc >= 1:
                        emit_wd_chunk(hc + 2)
            emit_down(23)

            for rs in range(2):
                @blk.vector
                def _(v, g=g, gsl=gsl, rs=rs):
                    b.wait(v, "V", down_done[(g, 23)])
                    b.wait(v, "V", rload_done[g])
                    b.wait(v, "V", apply2_done.get((g - 1, rs)))
                    b.wait(v, "V", b.tok("V"))
                    b.inc(v.tensor_tensor(y2pre[:, rs, :], down_ps[:, 1024 * rs:1024 * (rs + 1)],
                                          y1r32[:, gsl, rs, :], ALU.add), "V", 1)
                resid2_done[(g, rs)] = b.tok("V")

            for rs in range(2):
                apply2_done[(g, rs)] = ("A", nb2(g) + (8 + 8 * rs if g == NGg - 1 else 26 + 3 * rs))

            if g + 2 < NGg:
                emit_gloads(g + 2)

            def emit_vln2(rs, g=g, s=s, r0=r0):
                rstd2_done = ("A", nb2(g) + (7 + 8 * rs if g == NGg - 1 else 22 + rs))

                @blk.vector
                def _(v, g=g, rs=rs, rstd2_done=rstd2_done):
                    b.wait(v, "V", rstd2_done)
                    b.wait(v, "V", b.tok("V"))
                    b.inc(v.reciprocal(l2rstd[:, rs, :], l2ln[:, rs, :]), "V", 1)
                    b.wait(v, "V", b.tok("V"))
                    b.inc(v.scalar_tensor_tensor(l2mb[:, rs, :], l2m[:, rs, :],
                                                 -1.0, l2rstd[:, rs, :],
                                                 ALU.mult, ALU.mult), "V", 1)
                mb2_toks[(g, rs)] = b.tok("V")

                @blk.gpsimd
                def _(g_, g=g, rs=rs, s=s, r0=r0):
                    b.wait(g_, "G", apply2_done[(g, rs)])
                    b.inc(g_.dma_start(out_h.ap()[s, r0 + 128 * rs:r0 + 128 * (rs + 1), :],
                                       y2[:, g % 2, rs, :]), "st%d" % (g % 2), 16)
                store_done[(g, rs)] = b.tok("st%d" % (g % 2))

            pending_vln2[0] = emit_vln2

            @blk.scalar
            def _(a, g=g, b2base=b2base, prev_ln2=pending_ln2[0]):
                assert b.cnt["A"] == b2base[0], (b.cnt["A"], b2base[0])
                for hc in range(24):
                    hsl = hc % 2
                    b.wait(a, "A", gate_done[(g, hc)])
                    b.wait(a, "A", mul_done.get((g, hc - 2) if hc >= 2 else (g - 1, 22 + hc)))
                    b.inc(a.activation(sg_bf[:, hsl, :], gate_ps[hsl], AF.Silu), "A", 1)
                    if prev_ln2 is None:
                        continue
                    gp = prev_ln2
                    if hc in (4, 6):
                        rs = 0 if hc == 4 else 1
                        b.wait(a, "A", resid2_done[(gp, rs)])
                        b.wait(a, "A", b.tok("A"))
                        b.inc(a.activation(y2sq[:, :], y2pre[:, rs, :], AF.Identity,
                                           accum_out=s2s[:, rs, :]), "A", 1)
                        b.wait(a, "A", b.tok("A"))
                        b.inc(a.activation(y2sq[:, :], y2pre[:, rs, :], AF.Square,
                                           accum_out=s2q[:, rs, :]), "A", 1)
                    if hc in (5, 7):
                        rs = 0 if hc == 5 else 1
                        b.wait(a, "A", b.tok("A"))
                        b.inc(a.activation(l2m[:, rs, :], s2s[:, rs, :], AF.Identity,
                                           scale=1.0 / D), "A", 1)
                        b.wait(a, "A", b.tok("A"))
                        b.inc(a.activation(l2msq[:, rs, :], l2m[:, rs, :], AF.Square), "A", 1)
                        b.inc(a.activation(l2vpe[:, rs, :], s2q[:, rs, :], AF.Identity,
                                           scale=1.0 / D), "A", 1)
                        b.wait(a, "A", b.tok("A"))
                        b.inc(a.activation(l2vpe[:, rs, :], l2msq[:, rs, :], AF.Identity,
                                           scale=-1.0, bias=l2vpe[:, rs, :]), "A", 1)
                    if hc == 8:
                        for rs in range(2):
                            b.wait(a, "A", b.tok("A"))
                            b.inc(a.activation(l2ln[:, rs, :], l2vpe[:, rs, :], AF.Sqrt,
                                               bias=lneps[:, :]), "A", 1)
                    if hc in (10, 12):
                        rs = 0 if hc == 10 else 1
                        b.wait(a, "A", mb2_toks[(gp, rs)])
                        b.wait(a, "A", store_done.get((gp - 2, 1)))
                        b.inc(a.activation(y2[:, gp % 2, rs, :], y2pre[:, rs, :], AF.Identity,
                                           bias=l2mb[:, rs, :], scale=l2rstd[:, rs, :]),
                              "A", 1)
            pending_ln2[0] = g

        if pending_vln2[0] is not None:
            pending_vln2[0](0)
            pending_vln2[0](1)
            pending_vln2[0] = None

        @blk.scalar
        def _(a, prev_ln2=pending_ln2[0]):
            gp = prev_ln2
            for rs in range(2):
                b.wait(a, "A", resid2_done[(gp, rs)])
                b.wait(a, "A", b.tok("A"))
                b.inc(a.activation(y2sq[:, :], y2pre[:, rs, :], AF.Identity,
                                   accum_out=s2s[:, rs, :]), "A", 1)
                b.wait(a, "A", b.tok("A"))
                b.inc(a.activation(y2sq[:, :], y2pre[:, rs, :], AF.Square,
                                   accum_out=s2q[:, rs, :]), "A", 1)
                b.wait(a, "A", b.tok("A"))
                b.inc(a.activation(l2m[:, rs, :], s2s[:, rs, :], AF.Identity,
                                   scale=1.0 / D), "A", 1)
                b.wait(a, "A", b.tok("A"))
                b.inc(a.activation(l2msq[:, rs, :], l2m[:, rs, :], AF.Square), "A", 1)
                b.inc(a.activation(l2vpe[:, rs, :], s2q[:, rs, :], AF.Identity,
                                   scale=1.0 / D), "A", 1)
                b.wait(a, "A", b.tok("A"))
                b.inc(a.activation(l2vpe[:, rs, :], l2msq[:, rs, :], AF.Identity,
                                   scale=-1.0, bias=l2vpe[:, rs, :]), "A", 1)
                b.wait(a, "A", b.tok("A"))
                b.inc(a.activation(l2ln[:, rs, :], l2vpe[:, rs, :], AF.Sqrt,
                                   bias=lneps[:, :]), "A", 1)
                b.wait(a, "A", mb2_toks[(gp, rs)])
                b.wait(a, "A", store_done.get((gp - 2, 1)))
                b.inc(a.activation(y2[:, gp % 2, rs, :], y2pre[:, rs, :], AF.Identity,
                                   bias=l2mb[:, rs, :], scale=l2rstd[:, rs, :]), "A", 1)

        @blk.gpsimd
        def _(g_):
            g_.wait_ge(b.sems["st0"], b.cnt["st0"])
            g_.wait_ge(b.sems["st1"], b.cnt["st1"])

    es_b.close()
    es_outer.close()
    return nc


_IDENT = np.eye(128, dtype=ml_dtypes.bfloat16)


def kernel(L0_attn_x, L0_attn_W_qkv, L0_attn_b_qkv, L0_attn_W_out, L0_attn_b_out,
           L0_ffn_w_gate, L0_ffn_w_up, L0_ffn_w_down):
    x = np.ascontiguousarray(np.asarray(L0_attn_x, dtype=np.float32))
    wqkv = np.ascontiguousarray(np.asarray(L0_attn_W_qkv, dtype=np.float32))
    bqkv = np.ascontiguousarray(np.asarray(L0_attn_b_qkv, dtype=np.float32)).reshape(1, 3 * D)
    wout = np.ascontiguousarray(np.asarray(L0_attn_W_out, dtype=np.float32))
    bout = np.ascontiguousarray(np.asarray(L0_attn_b_out, dtype=np.float32)).reshape(1, D)
    wg = np.ascontiguousarray(np.asarray(L0_ffn_w_gate, dtype=np.int32))
    wu = np.ascontiguousarray(np.asarray(L0_ffn_w_up, dtype=np.int32))
    wd = np.ascontiguousarray(np.asarray(L0_ffn_w_down, dtype=np.int32))

    nc = build_program()
    in_maps = []
    for i in range(NCORES):
        in_maps.append({
            "x": x[SEQS * i:SEQS * (i + 1)],
            "wqkv": wqkv, "bqkv": bqkv, "wout": wout, "bout": bout,
            "wg": wg, "wu": wu, "wd": wd, "ident": _IDENT,
        })
    res = run_bass_kernel_spmd(nc, in_maps, core_ids=list(range(NCORES)))
    return np.concatenate([r["out"] for r in res.results], axis=0)


# version 30
# speedup vs baseline: 1.2875x; 1.2875x over previous
import contextlib

import numpy as np
import ml_dtypes
import concourse.bass as bass
import concourse.mybir as mybir
from concourse.bass_utils import run_bass_kernel_spmd

F32 = mybir.dt.float32
BF16 = mybir.dt.bfloat16
I32 = mybir.dt.int32
AF = mybir.ActivationFunctionType
ALU = mybir.AluOpType
AX = mybir.AxisListType

NCORES = 8
SEQS = 2
L = 4096
D = 1024
HID = 3072
TB = 17
NW = 16
RTS = 32
INV127 = float(np.float32(1.0 / 127.0))
EPS = 1e-5


class Bld:
    def __init__(self, nc, es):
        self.nc = nc
        self.sems = {}
        self.cnt = {}
        self.waited = {}
        self.es = es

    def sem(self, name):
        self.sems[name] = self.es.enter_context(self.nc.semaphore(name))
        self.cnt[name] = 0

    def inc(self, instr, name, by=1):
        instr.then_inc(self.sems[name], by)
        self.cnt[name] += by
        return (name, self.cnt[name])

    def tok(self, name):
        return (name, self.cnt[name])

    def wait(self, eng, engkey, tok):
        if tok is None:
            return
        name, val = tok
        if val <= 0:
            return
        key = (engkey, name)
        if self.waited.get(key, 0) >= val:
            return
        self.waited[key] = val
        eng.wait_ge(self.sems[name], val)


def build_program(debug=False, seqs=SEQS, l=L):
    TB = l // 256 + 1
    NW = l // 256
    RTS = l // 128
    SEQSv, Lv = seqs, l

    def lastwin(gb):
        if gb is None or gb < 0:
            return None
        s, t = gb // TB, gb % TB
        return s * NW + min(t, NW - 1)

    nc = bass.Bass("TRN2", target_bir_lowering=False)

    x_h = nc.dram_tensor("x", [SEQSv, Lv, D], F32, kind="ExternalInput")
    wqkv_h = nc.dram_tensor("wqkv", [D, 3 * D], F32, kind="ExternalInput")
    bqkv_h = nc.dram_tensor("bqkv", [1, 3 * D], F32, kind="ExternalInput")
    wout_h = nc.dram_tensor("wout", [D, D], F32, kind="ExternalInput")
    bout_h = nc.dram_tensor("bout", [1, D], F32, kind="ExternalInput")
    wg_h = nc.dram_tensor("wg", [HID, D], I32, kind="ExternalInput")
    wu_h = nc.dram_tensor("wu", [HID, D], I32, kind="ExternalInput")
    wd_h = nc.dram_tensor("wd", [D, HID], I32, kind="ExternalInput")
    ident_h = nc.dram_tensor("ident", [128, 128], BF16, kind="ExternalInput")
    out_h = nc.dram_tensor("out", [SEQSv, Lv, D], F32, kind="ExternalOutput")
    ykind = "ExternalOutput" if debug else "Internal"
    y1f_h = nc.dram_tensor("y1dbg", [SEQSv, Lv, D], F32, kind=ykind)
    y1fm_h = nc.dram_tensor("y1fm_scr", [SEQSv, Lv // 256, 128, 8, 256], BF16,
                            kind="Internal")
    wgT_h = nc.dram_tensor("wgT_scr", [24, 128, D], BF16, kind="Internal")
    wuT_h = nc.dram_tensor("wuT_scr", [24, 128, D], BF16, kind="Internal")
    wdT_h = nc.dram_tensor("wdT_scr", [24, 128, D], BF16, kind="Internal")

    es_outer = contextlib.ExitStack()
    b = Bld(nc, es_outer)
    for s in ["T", "V", "A", "G",
              "xq0", "xq1", "xq2", "xq3", "xr0", "xr1",
              "wl0", "wl1", "wl2", "wl3", "wl4", "wg0",
              "tx0", "tx1", "tv0", "tv1", "ty0", "ty1", "tw0", "tw1",
              "wc0", "wc1", "wc2", "wc3", "wi0", "pk0", "pk1", "pf0", "pf1", "pw0", "pw1",
              "yf0", "yf1", "yr0", "yr1", "st0", "st1"]:
        b.sem(s)

    SO = lambda n, sh, dt, **kw: es_outer.enter_context(nc.sbuf_tensor(n, sh, dt, **kw))
    lneps = SO("lneps", [128, 1], F32)
    PS = lambda n, sh, dt: es_outer.enter_context(nc.psum_tensor(n, sh, dt))
    psum_qkv = PS("psum_qkv", [128, D], F32)
    psum_S = PS("psum_S", [128, 512], F32)
    psum_TR = PS("psum_TR", [128, 1024], BF16)
    psum_O = PS("psum_O", [128, 2048], F32)

    es_a = contextlib.ExitStack()
    SA = lambda n, sh, dt, **kw: es_a.enter_context(nc.sbuf_tensor(n, sh, dt, **kw))

    es_qkv = contextlib.ExitStack()
    wqkv_bf = es_qkv.enter_context(
        nc.sbuf_tensor("wqkv_bf", [128, 8, 3 * D], BF16, side="right"))
    wgT_holder = [None]
    wout_bf = SA("wout_bf", [128, 8, D], BF16)
    bq_rep = SA("bq_rep", [128, 3 * D], BF16)
    bo_rep = SA("bo_rep", [128, D], BF16)
    ident_sb = SA("ident_sb", [128, 128], BF16)
    ones128 = SA("ones128", [128, 128], BF16)
    ones1 = SA("ones1", [1, 128], BF16)

    xt_bf = SA("xt_bf", [128, 3, D], BF16)
    x_fm = SA("x_fm", [128, 2, 8, 128], BF16)
    q_tm = SA("q_tm", [128, 3, D], BF16)
    k_tm = SA("k_tm", [128, 3, D], BF16)
    v_tmp = SA("v_tmp", [128, 3, D], BF16)
    v_fm = SA("v_fm", [128, 3, 8, 128], BF16)
    attn_out = SA("attn_out", [128, 8, 512], BF16)
    negmax = SA("negmax", [128, 8], F32)
    ssub = SA("ssub", [128, 512], F32)
    aexp = SA("aexp", [128, 512], F32)
    asum = SA("asum", [128, 8], F32)
    arecip = SA("arecip", [128, 8], F32)
    a_bf = SA("a_bf", [128, 512], BF16)
    a_bd = SA("a_bd", [128, 2, 8, 128], BF16)
    at_bf = SA("at_bf", [128, 1024], BF16)
    xr32 = SA("xr32", [128, 2, D], F32)
    y1pre = SA("y1pre", [128, 2, D], F32)
    y1sq = SA("y1sq", [128, D], F32)
    sums = SA("sums", [128, 2, 1], F32)
    sumsq = SA("sumsq", [128, 2, 1], F32)
    lnm = SA("lnm", [128, 2, 1], F32)
    lnmsq = SA("lnmsq", [128, 2, 1], F32)
    lnvpe = SA("lnvpe", [128, 2, 1], F32)
    lnrstd = SA("lnrstd", [128, 2, 1], F32)
    lnln = SA("lnln", [128, 2, 1], F32)
    lnmb = SA("lnmb", [128, 2, 1], F32)
    y1_f32 = SA("y1_f32", [128, 2, D], F32)
    y1_bf = SA("y1_bf", [128, 2, D], BF16)
    y1fm_stage = SA("y1fm_stage", [128, 4, 8, 256], BF16)

    es_w = contextlib.ExitStack()
    SW = lambda n, sh, dt, **kw: es_w.enter_context(nc.sbuf_tensor(n, sh, dt, **kw))
    w_in_bf = SW("w_in_bf", [128, 3, HID], BF16)
    wT_stage = SW("wT_stage", [128, 8, 128], BF16)
    bq_bf = SW("bq_bf", [1, 3 * D], BF16)
    bo_bf = SW("bo_bf", [1, D], BF16)

    with nc.Block() as blk:
        @blk.gpsimd
        def _(g):
            b.inc(g.memset(ones128[:, :], 1.0), "G", 1)
            b.inc(g.memset(ones1[:, :], 1.0), "G", 1)
            b.inc(g.memset(a_bd[:, :, :, :], 0.0), "G", 1)
            b.inc(g.memset(lneps[:, :], float(EPS)), "G", 1)
        memset_tok = b.tok("G")

        @blk.sync
        def _(q):
            b.inc(q.dma_start(ident_sb[:, :], ident_h.ap()), "wl0", 16)

        @blk.gpsimd
        def _(g):
            for c in range(8):
                b.inc(g.dma_start(wqkv_bf[:, c, :], wqkv_h.ap()[128 * c:128 * (c + 1), :]),
                      "wi0", 16)
                b.inc(g.dma_start(wout_bf[:, c, :], wout_h.ap()[128 * c:128 * (c + 1), :]),
                      "wi0", 16)
            b.inc(g.dma_start(bq_bf[:, :], bqkv_h.ap()), "wi0", 16)
            b.inc(g.dma_start(bo_bf[:, :], bout_h.ap()), "wi0", 16)
        wcast_tok = b.tok("wi0")
        prev_cast = wcast_tok

        @blk.vector
        def _(v):
            b.wait(v, "V", wcast_tok)
            b.inc(v.tensor_scalar_mul(bq_bf[:, :], bq_bf[:, :], 1.0 / 128.0), "V", 1)
            b.inc(v.tensor_scalar_mul(bo_bf[:, :], bo_bf[:, :], 1.0 / 128.0), "V", 1)
        bias_bf_tok = b.tok("V")

        brep_tok = None
        for g3 in range(4):
            @blk.tensor
            def _(t_, g3=g3, brep_tok=brep_tok):
                b.wait(t_, "T", bias_bf_tok)
                b.wait(t_, "T", memset_tok)
                b.wait(t_, "T", brep_tok)
                if g3 < 3:
                    r0 = bq_bf[0:1, 1024 * g3:1024 * g3 + 512]
                    r1 = bq_bf[0:1, 1024 * g3 + 512:1024 * (g3 + 1)]
                else:
                    r0, r1 = bo_bf[0:1, 0:512], bo_bf[0:1, 512:1024]
                b.inc(t_.matmul(psum_qkv[:, 0:512], ones1[:, :], r0, start=True, stop=True), "T", 1)
                b.inc(t_.matmul(psum_qkv[:, 512:1024], ones1[:, :], r1, start=True, stop=True), "T", 1)
            mm_tok = b.tok("T")

            @blk.vector
            def _(v, g3=g3, mm_tok=mm_tok):
                b.wait(v, "V", mm_tok)
                dst = bq_rep[:, 1024 * g3:1024 * (g3 + 1)] if g3 < 3 else bo_rep[:, :]
                b.inc(v.tensor_copy(dst, psum_qkv[:, :]), "V", 1)
            brep_tok = b.tok("V")

        @blk.sync
        def _(q):
            b.wait(q, "Q", prev_cast)
            b.wait(q, "Q", brep_tok)

    cast_done, tx_done, qkv_mm_done, evac_done, tv_done = {}, {}, {}, {}, {}
    evac_half, qkv_half = {}, {}
    xfm_done, tvmm_done, tymm_done = {}, {}, {}
    tr_banks = [[0, None], [1, None], [2, None]]
    tr_idx = [0]
    qkv_free = [None]
    tr_noq = [False]

    def claim_tr():
        bk = tr_banks[tr_idx[0]]
        tr_idx[0] = (tr_idx[0] + 1) % 3
        if tr_noq[0] and bk[0] == 2:
            bk = tr_banks[tr_idx[0]]
            tr_idx[0] = (tr_idx[0] + 1) % 3
        extra = qkv_free[0] if bk[0] == 2 else None
        return bk, extra

    def tr_slice(bi, c0, c1):
        if bi == 0:
            return psum_TR[:, c0:c1]
        if bi == 1:
            return psum_S[:, c0 // 2:c1 // 2].bitcast(BF16)
        return psum_qkv[:, c0 // 2:c1 // 2].bitcast(BF16)
    s_mm_done, ssub_done, exp_done, scale_done, bd_done = {}, {}, {}, {}, {}
    at_mm_done, atevac_done, o_mm_done, oevac_done = {}, {}, {}, {}
    proj_done, resid_done, sq_done, apply_done = {}, {}, {}, {}
    ty_done, pk_done, pf_done, xq_done, xr_done = {}, {}, {}, {}, {}
    last_resid = None

    a_join_T = [wcast_tok, memset_tok, brep_tok, ("wl0", 16)]

    with nc.Block() as blk:
        @blk.tensor
        def _(t_):
            for tk in a_join_T:
                b.wait(t_, "T", tk)

        def emit_xload(gt2):
            if gt2 >= SEQSv * TB:
                return
            s2, t2 = gt2 // TB, gt2 % TB
            xsl = gt2 % 3

            @blk.gpsimd
            def _(g, s2=s2, t2=t2, gt2=gt2, xsl=xsl):
                b.wait(g, "G", tx_done.get(gt2 - 3))
                b.inc(g.dma_start(xt_bf[:, xsl, :], x_h.ap()[s2, 128 * t2:128 * (t2 + 1), :]),
                      "xq%d" % xsl, 16)
            cast_done[gt2] = b.tok("xq%d" % xsl)
            xq_done[gt2] = cast_done[gt2]

        def emit_xtr(gt2):
            if gt2 >= SEQSv * TB:
                return
            sl2 = gt2 % 2
            xsl = gt2 % 3
            bk, qtok = claim_tr()
            bi, freetok = bk[0], bk[1]

            @blk.tensor
            def _(t_, gt2=gt2, xsl=xsl, bi=bi, freetok=freetok, qtok=qtok):
                b.wait(t_, "T", cast_done[gt2])
                b.wait(t_, "T", freetok)
                b.wait(t_, "T", qtok)
                for c in range(8):
                    b.inc(t_.transpose(tr_slice(bi, 128 * c, 128 * (c + 1)),
                                       xt_bf[:, xsl, 128 * c:128 * (c + 1)],
                                       ident_sb[:, :]), "T", 1)
            tx_done[gt2] = b.tok("T")

            @blk.vector
            def _(v, gt2=gt2, sl2=sl2, bi=bi):
                b.wait(v, "V", tx_done[gt2])
                b.wait(v, "V", qkv_mm_done.get(gt2 - 2))
                b.inc(v.tensor_copy(x_fm[:, sl2, :, :].rearrange("p a b -> p (a b)"),
                                    tr_slice(bi, 0, 1024)), "V", 1)
            xfm_done[gt2] = b.tok("V")
            bk[1] = xfm_done[gt2]

        pending_typf = [None]

        pf_idx = [0]
        pf_by_idx = {}

        def emit_ty_one(gw, s, n, k2):
            wsl = n % 4
            if True:
                grt = s * RTS + 2 * n + k2
                rsl = grt % 2
                bk, qtok = claim_tr()
                bi, freetok = bk[0], bk[1]

                @blk.tensor
                def _(t_, grt=grt, rsl=rsl, bi=bi, freetok=freetok, qtok=qtok):
                    b.wait(t_, "T", apply_done[grt])
                    b.wait(t_, "T", freetok)
                    b.wait(t_, "T", qtok)
                    for c in range(8):
                        b.inc(t_.transpose(tr_slice(bi, 128 * c, 128 * (c + 1)),
                                           y1_bf[:, rsl, 128 * c:128 * (c + 1)],
                                           ident_sb[:, :]), "T", 1)
                tymm_done[grt] = b.tok("T")

                pp1 = pf_by_idx.get(pf_idx[0] - 1)
                pp2 = pf_by_idx.get(pf_idx[0] - 2)

                @blk.vector
                def _(v, grt=grt, rsl=rsl, k2=k2, wsl=wsl, bi=bi, pp1=pp1, pp2=pp2):
                    b.wait(v, "V", tymm_done[grt])
                    b.wait(v, "V", pp1)
                    b.wait(v, "V", pp2)
                    dst = y1fm_stage[:, wsl, :, 128 * k2:128 * (k2 + 1)]
                    b.inc(v.tensor_copy(dst, tr_slice(bi, 0, 1024)
                                        .rearrange("p (a b) -> p a b", a=8)),
                          "V", 1)
                ty_done[grt] = b.tok("V")
                bk[1] = ty_done[grt]

        def emit_pf(gw, s, n):
            wsl = n % 4
            nw0 = n - wsl
            pi = pf_idx[0]
            pf_idx[0] += 1
            prev2 = pf_by_idx.get(pi - 2)

            @blk.gpsimd
            def _(g, s=s, n=n, gw=gw, wsl=wsl, nw0=nw0, pi=pi, prev2=prev2):
                b.wait(g, "G", prev2)
                b.wait(g, "G", ty_done[s * RTS + 2 * n])
                b.wait(g, "G", ty_done[s * RTS + 2 * n + 1])
                src_ap = y1fm_stage[:, 0:wsl + 1, :, :].rearrange(
                    "p w a b -> p w (a b)")
                b.inc(g.dma_start(
                    y1fm_h.ap()[s, nw0:n + 1].rearrange(
                        "w p a b -> p w (a b)"),
                    src_ap), "pf%d" % (pi % 2), 16)
            pf_done[gw] = b.tok("pf%d" % (pi % 2))
            pf_by_idx[pi] = pf_done[gw]

        def flush_typf(which=2):
            if pending_typf[0] is not None:
                gw0, s0, n0 = pending_typf[0]
                if which >= 1 and pending_typf[0] not in ty_started:
                    ty_started.add(pending_typf[0])
                    emit_ty_one(gw0, s0, n0, 0)
                if which == 2:
                    emit_ty_one(gw0, s0, n0, 1)
                    if (n0 % 4 == 3) or (n0 == NW - 1):
                        emit_pf(gw0, s0, n0)
                    pending_typf[0] = None
        ty_started = set()

        wprep_state = {"stage_free": None}
        deq_prev = {}
        tw_prev = {}

        def prep_load(i):
            load_src, n_el, _batches = wprep_jobs[i]
            sl = i % 3
            sp = i % 2

            @blk.gpsimd
            def _(g, i=i, sl=sl, load_src=load_src, n_el=n_el):
                b.wait(g, "G", tw_prev.get(i - 3))
                b.inc(g.dma_start(w_in_bf[:, sl, 0:n_el], load_src), "wc%d" % sl, 16)
            wc_tok = b.tok("wc%d" % sl)

            deq_prev[i] = wc_tok

        def prep_transpose(i):
            _src, n_el, sub_transposes = wprep_jobs[i]
            sl = i % 3
            sp = i % 2

            @blk.vector
            def _(v, i=i, sl=sl, n_el=n_el):
                b.wait(v, "V", deq_prev[i])
                b.inc(v.tensor_scalar_mul(w_in_bf[:, sl, 0:n_el], w_in_bf[:, sl, 0:n_el],
                                          INV127), "V", 1)
            dq_tok = b.tok("V")
            for batch in sub_transposes:
                bk, qtok = claim_tr()
                bi, tr_free = bk[0], bk[1]

                @blk.tensor
                def _(t_, i=i, sl=sl, batch=batch, bi=bi, tr_free=tr_free,
                      qtok=qtok, dq_tok=dq_tok):
                    b.wait(t_, "T", dq_tok)
                    b.wait(t_, "T", tr_free)
                    b.wait(t_, "T", qtok)
                    for j, (src_sl, _dst, _comb) in enumerate(batch):
                        b.inc(t_.transpose(tr_slice(bi, 128 * j, 128 * (j + 1)),
                                           w_in_bf[:, sl, src_sl[0]:src_sl[1]],
                                           ident_sb[:, :]), "T", 1)
                tw_tok = b.tok("T")
                tw_prev[i] = tw_tok

                @blk.vector
                def _(v, tw_tok=tw_tok, bi=bi, sf=wprep_state["stage_free"]):
                    b.wait(v, "V", tw_tok)
                    b.wait(v, "V", sf)
                    b.inc(v.tensor_copy(wT_stage[:, :, :].rearrange("p a b -> p (a b)"),
                                        tr_slice(bi, 0, 1024)), "V", 1)
                ev_tok = b.tok("V")
                bk[1] = ev_tok

                @blk.gpsimd
                def _(g, batch=batch, ev_tok=ev_tok, sl=sl):
                    b.wait(g, "G", ev_tok)
                    comb, park_tr = batch[0][2], batch[0][1]
                    if park_tr:
                        b.inc(g.dma_start(comb.transpose([1, 0, 2]), wT_stage[:, :, :]),
                              "pw%d" % sp, 16)
                    else:
                        b.inc(g.dma_start(comb,
                                          wT_stage[:, :, :].rearrange("p a b -> p (a b)")),
                              "pw%d" % sp, 16)
                wprep_state["stage_free"] = b.tok("pw%d" % sp)

        wprep_jobs = []
        for src_h, dst_h in [(wg_h, wgT_h), (wu_h, wuT_h)]:
            for hc in range(24):
                comb = dst_h.ap()[hc]
                batch = [((128 * dc, 128 * (dc + 1)), False, comb) for dc in range(8)]
                wprep_jobs.append((src_h.ap()[128 * hc:128 * (hc + 1), :], D, [batch]))
        for dc in range(8):
            batches = []
            for hb in range(3):
                comb = wdT_h.ap()[8 * hb:8 * (hb + 1), :, 128 * dc:128 * (dc + 1)]
                batches.append([((128 * (8 * hb + j), 128 * (8 * hb + j + 1)), True, comb)
                                for j in range(8)])
            wprep_jobs.append((wd_h.ap()[128 * dc:128 * (dc + 1), :], HID, batches))
        wprep_ld = [0]
        wprep_tr = [0]

        def prep_loads(n_load):
            k = 0
            while wprep_ld[0] < len(wprep_jobs) and k < n_load:
                i = wprep_ld[0]
                if i >= 3 and (i - 3) not in tw_prev:
                    break
                prep_load(i)
                wprep_ld[0] += 1
                k += 1

        def prep_trs(n_tr):
            k = 0
            while wprep_tr[0] < wprep_ld[0] and k < n_tr:
                if wprep_tr[0] + 2 > wprep_ld[0]:
                    break
                prep_transpose(wprep_tr[0])
                wprep_tr[0] += 1
                k += 1

        def emit_wg_stream(gt_last):
            es_qkv.close()
            wgT_holder[0] = es_outer.enter_context(
                nc.sbuf_tensor("wgT", [128, 24, D], BF16, side="right"))
            pw_fin = [b.tok("pw0"), b.tok("pw1")]

            @blk.sync
            def _(q, gt_last=gt_last, pw_fin=pw_fin):
                b.wait(q, "Q", qkv_mm_done[(gt_last, 2)])
                for tk in pw_fin:
                    b.wait(q, "Q", tk)
                b.inc(q.dma_start(wgT_holder[0][:, :, :],
                                  wgT_h.ap().transpose([1, 0, 2])), "wg0", 16)

        emit_xload(0)
        emit_xload(1)
        emit_xload(2)
        emit_xtr(0)
        emit_xtr(1)

        for s in range(SEQSv):
            for t in range(TB + 1):
                gt = s * TB + t
                sl = gt % 2
                sl3 = gt % 3
                have_qkv = t < TB
                have_w1 = 1 <= t <= NW
                have_w2 = t >= 2
                abase = [b.cnt["A"]]
                off_exp = 9 if have_w2 else 1

                prep_loads(2)
                if have_qkv:
                    emit_xload(gt + 3)

                def emit_qkv_pass(p, gt=gt, sl=sl):
                    prev = (gt, p - 1) if p > 0 else (gt - 1, 2)
                    qsfree = tr_banks[2][1] if p == 0 else None

                    @blk.tensor
                    def _(t_, gt=gt, sl=sl, p=p, prev=prev, qsfree=qsfree):
                        b.wait(t_, "T", xfm_done[gt])
                        b.wait(t_, "T", evac_half.get((prev, 0)))
                        b.wait(t_, "T", qsfree)
                        for c in range(8):
                            lhsT = x_fm[:, sl, c, :]
                            b.inc(t_.matmul(psum_qkv[:, 0:512],
                                            lhsT, wqkv_bf[:, c, 1024 * p:1024 * p + 512],
                                            start=(c == 0), stop=False), "T", 1)
                            if c == 0:
                                b.wait(t_, "T", evac_half.get((prev, 1)))
                            b.inc(t_.matmul(psum_qkv[:, 512:1024],
                                            lhsT, wqkv_bf[:, c, 1024 * p + 512:1024 * (p + 1)],
                                            start=(c == 0), stop=False), "T", 1)
                        b.inc(t_.matmul(psum_qkv[:, 0:512], ones128[:, :],
                                        bq_rep[:, 1024 * p:1024 * p + 512],
                                        start=False, stop=True), "T", 1)
                        qkv_half[(gt, p, 0)] = b.tok("T")
                        b.inc(t_.matmul(psum_qkv[:, 512:1024], ones128[:, :],
                                        bq_rep[:, 1024 * p + 512:1024 * (p + 1)],
                                        start=False, stop=True), "T", 1)
                    qkv_mm_done[(gt, p)] = b.tok("T")
                    qkv_half[(gt, p, 1)] = qkv_mm_done[(gt, p)]
                    if p == 2:
                        qkv_mm_done[gt] = b.tok("T")

                    @blk.vector
                    def _(v, gt=gt, sl3=sl3, p=p):
                        b.wait(v, "V", qkv_half[(gt, p, 0)])
                        lw = lastwin(gt - 3)
                        if p == 0:
                            b.wait(v, "V", s_mm_done.get(lw))
                            dst = q_tm
                        elif p == 1:
                            b.wait(v, "V", s_mm_done.get(lw))
                            dst = k_tm
                        else:
                            b.wait(v, "V", tv_done.get(gt - 3))
                            dst = v_tmp
                        b.inc(v.tensor_copy(dst[:, sl3, 0:512], psum_qkv[:, 0:512]), "V", 1)
                        evac_half[((gt, p), 0)] = b.tok("V")
                        b.wait(v, "V", qkv_half[(gt, p, 1)])
                        b.inc(v.tensor_copy(dst[:, sl3, 512:1024], psum_qkv[:, 512:1024]),
                              "V", 1)
                    evac_half[((gt, p), 1)] = b.tok("V")
                    evac_done[(gt, p)] = b.tok("V")

                if have_qkv:
                    emit_qkv_pass(0)
                    emit_qkv_pass(1)

                if have_w1:
                    n1 = t - 1
                    gw1 = s * NW + n1
                    s_sl0 = (s * TB + n1) % 3
                    s_sl1 = (s * TB + n1 + 1) % 3
                    gt_n1 = s * TB + n1 + 1
                    sfree = tr_banks[1][1]
                    exp_done[gw1] = ("A", abase[0] + off_exp)

                    @blk.tensor
                    def _(t_, gw1=gw1, s_sl0=s_sl0, s_sl1=s_sl1, gt_n1=gt_n1, sfree=sfree):
                        b.wait(t_, "T", evac_half[((gt_n1, 1), 0)])
                        b.wait(t_, "T", ssub_done.get(gw1 - 1))
                        b.wait(t_, "T", sfree)
                        for hp in range(8):
                            if hp == 4:
                                b.wait(t_, "T", evac_half[((gt_n1, 1), 1)])
                            for par in range(2):
                                h = 2 * hp + par
                                for j, slj in enumerate((s_sl0, s_sl1)):
                                    b.inc(t_.matmul(
                                        psum_S[64 * par:64 * par + 64, 64 * hp:64 * hp + 64],
                                        q_tm[:, slj, 64 * h:64 * h + 64],
                                        k_tm[:, slj, 64 * h:64 * h + 64],
                                        start=(j == 0), stop=(j == 1),
                                        skip_group_check=True), "T", 1)
                    s_mm_done[gw1] = b.tok("T")

                    @blk.vector
                    def _(v, gw1=gw1):
                        b.wait(v, "V", s_mm_done[gw1])
                        b.wait(v, "V", b.tok("V"))
                        b.inc(v.reduce_max(negmax[:, :],
                                           psum_S[:, :].rearrange("p (s x) -> p s x", s=8),
                                           axis=AX.X, negate=True), "V", 1)
                        b.wait(v, "V", b.tok("V"))
                        b.wait(v, "V", exp_done.get(gw1 - 1))
                        nm = negmax[:, :].unsqueeze(2).broadcast_to([128, 8, 64])
                        b.inc(v.tensor_tensor(ssub[:, :].rearrange("p (s x) -> p s x", s=8),
                                              psum_S[:, :].rearrange("p (s x) -> p s x", s=8),
                                              nm, ALU.add), "V", 1)
                    ssub_done[gw1] = b.tok("V")
                    tr_banks[1][1] = ssub_done[gw1]

                if have_w2:
                    n = t - 2
                    gw = s * NW + n
                    sl0 = (s * TB + n) % 3
                    sl1 = (s * TB + n + 1) % 3
                    wpar = gw % 2

                    @blk.tensor
                    def _(t_, gw=gw, wpar=wpar):
                        b.wait(t_, "T", bd_done[gw])
                        b.wait(t_, "T", oevac_done.get(gw - 1))
                        b.wait(t_, "T", resid_done.get(2 * (gw - 1)))
                        at_dst = psum_O[:, 0:512].bitcast(BF16)
                        for hp in range(8):
                            b.inc(t_.transpose(at_dst[0:64, 128 * hp:128 * (hp + 1)],
                                               a_bd[:, wpar, hp, 0:64], ident_sb[:, :]),
                                  "T", 1)
                            b.inc(t_.transpose(at_dst[64:128, 128 * hp:128 * (hp + 1)],
                                               a_bd[:, wpar, hp, 64:128], ident_sb[:, :]),
                                  "T", 1)
                    at_mm_done[gw] = b.tok("T")

                    @blk.vector
                    def _(v, gw=gw):
                        b.wait(v, "V", at_mm_done[gw])
                        b.wait(v, "V", b.tok("V"))
                        b.wait(v, "V", o_mm_done.get(gw - 1))
                        b.inc(v.tensor_copy(at_bf[:, :], psum_O[:, 0:512].bitcast(BF16)),
                              "V", 1)
                    atevac_done[gw] = b.tok("V")

                if have_qkv:
                    emit_qkv_pass(2)
                    qkv_free[0] = evac_half[((gt, 2), 1)]
                    emit_xtr(gt + 2)

                if s == SEQSv - 1 and t == TB - 1 and wprep_tr[0] >= 24:
                    emit_wg_stream(gt)

                if have_qkv:
                    vbk, vqtok = claim_tr()
                    vbi, vfreetok = vbk[0], vbk[1]

                    @blk.tensor
                    def _(t_, gt=gt, sl3=sl3, vbi=vbi, vfreetok=vfreetok, vqtok=vqtok):
                        b.wait(t_, "T", evac_half[((gt, 2), 0)])
                        b.wait(t_, "T", vfreetok)
                        b.wait(t_, "T", vqtok)
                        for c in range(8):
                            if c == 4:
                                b.wait(t_, "T", evac_half[((gt, 2), 1)])
                            b.inc(t_.transpose(tr_slice(vbi, 128 * c, 128 * (c + 1)),
                                               v_tmp[:, sl3, 128 * c:128 * (c + 1)],
                                               ident_sb[:, :]), "T", 1)
                    tvmm_done[gt] = b.tok("T")

                    @blk.vector
                    def _(v, gt=gt, sl3=sl3, vbi=vbi):
                        b.wait(v, "V", tvmm_done[gt])
                        b.wait(v, "V", o_mm_done.get(lastwin(gt - 3)))
                        b.inc(v.tensor_copy(v_fm[:, sl3, :, :].rearrange("p a b -> p (a b)"),
                                            tr_slice(vbi, 0, 1024)), "V", 1)
                    tv_done[gt] = b.tok("V")
                    vbk[1] = tv_done[gt]

                if have_w2:
                    flush_typf(which=1)

                    @blk.tensor
                    def _(t_, gw=gw, s=s, n=n, sl0=sl0, sl1=sl1):
                        b.wait(t_, "T", atevac_done[gw])
                        b.wait(t_, "T", tv_done[s * TB + n])
                        b.wait(t_, "T", tv_done[s * TB + n + 1])
                        b.wait(t_, "T", oevac_done.get(gw - 1))
                        b.wait(t_, "T", resid_done.get(2 * (gw - 1) + 1))
                        for hp in range(8):
                            for c4 in range(4):
                                slj = sl0 if c4 < 2 else sl1
                                off = 64 * (c4 % 2)
                                lhsT = v_fm[:, slj, hp, off:off + 64]
                                dst0 = 512 * (hp // 2) + 256 * (hp % 2) + 64 * c4
                                for par in range(2):
                                    b.inc(t_.matmul(
                                        psum_O[64 * par:64 * par + 64, dst0:dst0 + 64],
                                        lhsT,
                                        at_bf[:, 128 * hp + 64 * par:128 * hp + 64 * par + 64],
                                        start=True, stop=True,
                                        skip_group_check=True), "T", 1)
                    o_mm_done[gw] = b.tok("T")

                    flush_typf(which=2)
                    prep_trs(2)

                    oevac_done[gw] = ("A", abase[0] + 8)
                    lnstart = abase[0] + (10 if have_w1 else 9)

                    for k2 in range(2):
                        rt = 2 * n + k2
                        grt = s * RTS + rt
                        rsl = grt % 2

                        @blk.sync
                        def _(q, s=s, rt=rt, grt=grt, rsl=rsl):
                            b.wait(q, "Q", resid_done.get(grt - 2))
                            b.inc(q.dma_start(xr32[:, rsl, :],
                                              x_h.ap()[s, 128 * rt:128 * (rt + 1), :]),
                                  "xr%d" % rsl, 16)
                        xr_done[grt] = b.tok("xr%d" % rsl)

                        @blk.tensor
                        def _(t_, gw=gw, n=n, k2=k2, grt=grt, s=s):
                            pj = psum_O[:, 1024 * k2:1024 * (k2 + 1)]
                            rb = 256 * (n % 2) + 128 * k2
                            for c in range(8):
                                lhsT = attn_out[:, c, rb:rb + 128]
                                b.wait(t_, "T", ("A", abase[0] + 1 + max(c, 1 + 4 * k2)))
                                b.inc(t_.matmul(pj[:, 0:512], lhsT,
                                                wout_bf[:, c, 0:512],
                                                start=(c == 0), stop=False,
                                                skip_group_check=True), "T", 1)
                                b.wait(t_, "T", ("A", abase[0] + 1 + max(c, 3 + 4 * k2)))
                                b.inc(t_.matmul(pj[:, 512:1024], lhsT,
                                                wout_bf[:, c, 512:1024],
                                                start=(c == 0), stop=False,
                                                skip_group_check=True), "T", 1)
                            b.inc(t_.matmul(pj[:, 0:512], ones128[:, :],
                                            bo_rep[:, 0:512], start=False, stop=True,
                                            skip_group_check=True), "T", 1)
                            b.inc(t_.matmul(pj[:, 512:1024], ones128[:, :],
                                            bo_rep[:, 512:1024], start=False, stop=True,
                                            skip_group_check=True), "T", 1)
                        proj_done[2 * gw + k2] = b.tok("T")

                        @blk.vector
                        def _(v, grt=grt, rsl=rsl, gw=gw, k2=k2):
                            b.wait(v, "V", proj_done[2 * gw + k2])
                            b.wait(v, "V", xr_done[grt])
                            b.wait(v, "V", apply_done.get(grt - 2))
                            b.wait(v, "V", b.tok("V"))
                            b.inc(v.tensor_tensor(y1pre[:, rsl, :],
                                                  psum_O[:, 1024 * k2:1024 * (k2 + 1)],
                                                  xr32[:, rsl, :], ALU.add), "V", 1)
                        resid_done[grt] = b.tok("V")
                        last_resid = resid_done[grt]

                        apply_done[grt] = ("A", lnstart + 11 + 12 * k2)

                        @blk.gpsimd
                        def _(g, s=s, rt=rt, grt=grt, rsl=rsl, lnstart=lnstart):
                            b.wait(g, "G", ("A", lnstart + 10 + 12 * k2))
                            b.inc(g.dma_start(y1f_h.ap()[s, 128 * rt:128 * (rt + 1), :],
                                              y1_f32[:, rsl, :]), "pk%d" % rsl, 16)
                        pk_done[grt] = b.tok("pk%d" % rsl)

                    pending_typf[0] = (gw, s, n)

                if have_w1 or have_w2:
                    w2ctx = (gw, s, n) if have_w2 else None
                    w1gw = gw1 if have_w1 else None

                    @blk.scalar
                    def _(a, w2ctx=w2ctx, w1gw=w1gw, abase=abase):
                        assert b.cnt["A"] == abase[0], (b.cnt["A"], abase[0])
                        if w2ctx is not None:
                            gw, s_, n = w2ctx
                            b.wait(a, "A", o_mm_done[gw])
                            b.wait(a, "A", proj_done.get(2 * (gw - 2) + 1))
                            rb = 256 * (n % 2)
                            for hp in range(8):
                                base = 512 * (hp // 2) + 256 * (hp % 2)
                                b.inc(a.activation(
                                    attn_out[:, hp, rb:rb + 256]
                                    .rearrange("p (d c) -> p d c", c=4),
                                    psum_O[:, base:base + 256]
                                    .rearrange("p (c d) -> p d c", c=4),
                                    AF.Copy), "A", 1)
                        if w1gw is not None:
                            b.wait(a, "A", ssub_done[w1gw])
                            b.wait(a, "A", scale_done.get(w1gw - 1))
                            b.inc(a.activation(aexp[:, :], ssub[:, :], AF.Exp, scale=0.125),
                                  "A", 1)
                        if w2ctx is not None:
                            gw, s_, n = w2ctx
                            for k2 in range(2):
                                grt = s_ * RTS + 2 * n + k2
                                rsl = grt % 2
                                b.wait(a, "A", resid_done[grt])
                                b.wait(a, "A", b.tok("A"))
                                b.inc(a.activation(y1sq[:, :], y1pre[:, rsl, :], AF.Identity,
                                                   accum_out=sums[:, rsl, :]), "A", 1)
                                b.wait(a, "A", b.tok("A"))
                                b.inc(a.activation(y1sq[:, :], y1pre[:, rsl, :], AF.Square,
                                                   accum_out=sumsq[:, rsl, :]), "A", 1)
                                b.wait(a, "A", b.tok("A"))
                                b.inc(a.activation(lnm[:, rsl, :], sums[:, rsl, :], AF.Identity,
                                                   scale=1.0 / D), "A", 1)
                                b.wait(a, "A", b.tok("A"))
                                b.inc(a.activation(lnmsq[:, rsl, :], lnm[:, rsl, :], AF.Square),
                                      "A", 1)
                                b.inc(a.activation(lnvpe[:, rsl, :], sumsq[:, rsl, :],
                                                   AF.Identity, scale=1.0 / D), "A", 1)
                                b.wait(a, "A", b.tok("A"))
                                b.inc(a.activation(lnvpe[:, rsl, :], lnmsq[:, rsl, :],
                                                   AF.Identity,
                                                   scale=-1.0, bias=lnvpe[:, rsl, :]), "A", 1)
                                b.wait(a, "A", b.tok("A"))
                                b.inc(a.activation(lnln[:, rsl, :], lnvpe[:, rsl, :], AF.Ln,
                                                   bias=lneps[:, :]), "A", 1)
                                b.wait(a, "A", b.tok("A"))
                                b.inc(a.activation(lnrstd[:, rsl, :], lnln[:, rsl, :], AF.Exp,
                                                   scale=-0.5), "A", 1)
                                b.wait(a, "A", b.tok("A"))
                                b.inc(a.activation(lnmb[:, rsl, :], lnm[:, rsl, :], AF.Identity,
                                                   scale=lnrstd[:, rsl, :]), "A", 1)
                                b.wait(a, "A", b.tok("A"))
                                b.inc(a.activation(lnmb[:, rsl, :], lnmb[:, rsl, :], AF.Identity,
                                                   scale=-1.0), "A", 1)
                                b.wait(a, "A", b.tok("A"))
                                b.wait(a, "A", pk_done.get(grt - 2))
                                b.wait(a, "A", tymm_done.get(grt - 2))
                                b.inc(a.activation(y1_f32[:, rsl, :], y1pre[:, rsl, :],
                                                   AF.Identity, bias=lnmb[:, rsl, :],
                                                   scale=lnrstd[:, rsl, :]), "A", 1)
                                b.inc(a.activation(y1_bf[:, rsl, :], y1pre[:, rsl, :],
                                                   AF.Identity, bias=lnmb[:, rsl, :],
                                                   scale=lnrstd[:, rsl, :]), "A", 1)

                if have_w1:
                    @blk.vector
                    def _(v, gw1=gw1):
                        b.wait(v, "V", exp_done[gw1])
                        b.wait(v, "V", b.tok("V"))
                        b.inc(v.reduce_sum(asum[:, :],
                                           aexp[:, :].rearrange("p (s x) -> p s x", s=8),
                                           axis=AX.X), "V", 1)
                        b.wait(v, "V", b.tok("V"))
                        b.inc(v.reciprocal(arecip[:, :], asum[:, :]), "V", 1)
                        b.wait(v, "V", b.tok("V"))
                        ar = arecip[:, :].unsqueeze(2).broadcast_to([128, 8, 64])
                        b.inc(v.tensor_tensor(a_bf[:, :].rearrange("p (s x) -> p s x", s=8),
                                              aexp[:, :].rearrange("p (s x) -> p s x", s=8),
                                              ar, ALU.mult), "V", 1)
                    scale_done[gw1] = b.tok("V")

                    @blk.vector
                    def _(v, gw1=gw1):
                        b.wait(v, "V", scale_done[gw1])
                        b.wait(v, "V", b.tok("V"))
                        b.wait(v, "V", at_mm_done.get(gw1 - 2))
                        b.wait(v, "V", memset_tok)
                        wp1 = gw1 % 2
                        b.inc(v.tensor_copy(
                            a_bd[0:64, wp1, :, 0:64],
                            a_bf[0:64, :].rearrange("p (h x) -> p h x", h=8)), "V", 1)
                        b.inc(v.tensor_copy(
                            a_bd[64:128, wp1, :, 64:128],
                            a_bf[64:128, :].rearrange("p (h x) -> p h x", h=8)), "V", 1)
                    bd_done[gw1] = b.tok("V")

        while wprep_tr[0] < len(wprep_jobs):
            while wprep_ld[0] < len(wprep_jobs) and wprep_ld[0] < wprep_tr[0] + 3:
                prep_load(wprep_ld[0])
                wprep_ld[0] += 1
            prep_transpose(wprep_tr[0])
            wprep_tr[0] += 1
        flush_typf()
        if wgT_holder[0] is None:
            emit_wg_stream(SEQSv * TB - 1)
        wg_tok = b.tok("wg0")

        @blk.sync
        def _(q):
            b.wait(q, "Q", b.tok("pk0"))
            b.wait(q, "Q", b.tok("pk1"))
            b.wait(q, "Q", b.tok("pf0"))
            b.wait(q, "Q", b.tok("pf1"))

    a_end = [b.tok(k) for k in ("T", "V", "A", "G")]
    a_end += [b.tok(k) for k in ("pk0", "pk1", "pf0", "pf1",
                                 "xq0", "xq1", "xq2", "xq3", "xr0", "xr1",
                                 "pw0", "pw1", "wc0", "wc1")]

    es_w.close()
    es_a.close()

    es_b = contextlib.ExitStack()
    SBB = lambda n, sh, dt, **kw: es_b.enter_context(nc.sbuf_tensor(n, sh, dt, **kw))
    wgT = wgT_holder[0]
    wuT = SBB("wuT", [128, 24, D], BF16)
    wdT = SBB("wdT", [128, 24, D], BF16)
    y1fm_sb = SBB("y1fm_sb", [128, 2, 8, 256], BF16)
    y1r32 = SBB("y1r32", [128, 2, 2, D], F32)
    sg_bf = SBB("sg_bf", [128, 2, 256], BF16)
    h_bf = SBB("h_bf", [128, 2, 256], BF16)
    y2pre = SBB("y2pre", [128, 2, D], F32)
    y2sq = SBB("y2sq", [128, D], F32)
    y2 = SBB("y2", [128, 2, 2, D], F32)
    s2s = SBB("s2s", [128, 2, 1], F32)
    s2q = SBB("s2q", [128, 2, 1], F32)
    l2m = SBB("l2m", [128, 2, 1], F32)
    l2msq = SBB("l2msq", [128, 2, 1], F32)
    l2vpe = SBB("l2vpe", [128, 2, 1], F32)
    l2rstd = SBB("l2rstd", [128, 2, 1], F32)
    l2ln = SBB("l2ln", [128, 2, 1], F32)
    l2mb = SBB("l2mb", [128, 2, 1], F32)

    gate_ps = [psum_qkv[:, 0:256], psum_qkv[:, 512:768]]
    up_ps = [psum_S[:, 0:256], psum_TR[:, 0:512].bitcast(F32)[:, 0:256]]
    down_ps = psum_O

    gate_done, silu_done, up_done, mul_done, down_done = {}, {}, {}, {}, {}
    apply2_done = {}
    mb2_toks = {}
    pending_ln2 = [None]
    pending_vln2 = [None]
    dload_done, rload_done, resid2_done, sq2_done, y2_done, store_done = {}, {}, {}, {}, {}, {}
    NGg = SEQSv * (Lv // 256)

    with nc.Block() as blk:
        @blk.tensor
        def _(t_):
            for tk in a_end:
                b.wait(t_, "T", tk)

        @blk.vector
        def _(v):
            for tk in a_end:
                b.wait(v, "V", tk)

        @blk.scalar
        def _(a):
            for tk in a_end:
                b.wait(a, "A", tk)

        @blk.gpsimd
        def _(g):
            for tk in a_end:
                b.wait(g, "G", tk)

        @blk.sync
        def _(q):
            for tk in a_end:
                b.wait(q, "Q", tk)

        wu_tok, wd_tok = {}, {}

        def emit_wu_chunk(hc):
            @blk.sync
            def _(q, hc=hc):
                b.wait(q, "Q", up_done.get((0, hc - 2)))
                b.inc(q.dma_start(wuT[:, hc, :], wuT_h.ap()[hc]),
                      "wl%d" % (hc % 2), 16)
            wu_tok[hc] = b.tok("wl%d" % (hc % 2))

        def emit_wd_chunk(hc):
            @blk.sync
            def _(q, hc=hc):
                b.wait(q, "Q", down_done.get((0, hc - 3)))
                b.inc(q.dma_start(wdT[:, hc, :], wdT_h.ap()[hc]),
                      "wl%d" % (2 + hc % 3), 16)
            wd_tok[hc] = b.tok("wl%d" % (2 + hc % 3))

        def emit_gloads(g):
            s = g // (Lv // 256)
            r0 = 256 * (g % (Lv // 256))
            gsl = g % 2

            @blk.sync
            def _(q, g=g, s=s, r0=r0, gsl=gsl):
                b.wait(q, "Q", up_done.get((g - 2, 23)))
                b.inc(q.dma_start(y1fm_sb[:, gsl, :, :],
                                  y1fm_h.ap()[s, g % (Lv // 256)]), "yf%d" % gsl, 16)
            dload_done[g] = b.tok("yf%d" % gsl)

            @blk.sync
            def _(q, g=g, s=s, r0=r0, gsl=gsl):
                b.wait(q, "Q", resid2_done.get((g - 2, 1)))
                for rs in range(2):
                    b.inc(q.dma_start(y1r32[:, gsl, rs, :],
                                      y1f_h.ap()[s, r0 + 128 * rs:r0 + 128 * (rs + 1), :]),
                          "yr%d" % gsl, 16)
            rload_done[g] = b.tok("yr%d" % gsl)

        emit_gloads(0)
        if NGg > 1:
            emit_gloads(1)
        for c0 in (0, 1):
            emit_wu_chunk(c0)
            emit_wd_chunk(c0)
        emit_wd_chunk(2)

        b2bases = {}

        def nb2(g):
            return b2bases[g] + (24 if g == 0 else 40)

        for g in range(NGg):
            s = g // (Lv // 256)
            r0 = 256 * (g % (Lv // 256))
            gsl = g % 2
            b2base = [b.cnt["A"] if g == 0 else nb2(g - 1)]
            b2bases[g] = b2base[0]

            def emit_down(hc, g=g):
                hsl = hc % 2

                @blk.tensor
                def _(t_, g=g, hc=hc, hsl=hsl):
                    b.wait(t_, "T", wd_tok[hc])
                    b.wait(t_, "T", mul_done[(g, hc)])
                    if hc == 0:
                        b.wait(t_, "T", resid2_done.get((g - 1, 1)))
                    for rs in range(2):
                        lhsT = h_bf[:, hsl, 128 * rs:128 * (rs + 1)]
                        b.inc(t_.matmul(down_ps[:, 1024 * rs:1024 * rs + 512],
                                        lhsT, wdT[:, hc, 0:512],
                                        start=(hc == 0), stop=(hc == 23),
                                        skip_group_check=True), "T", 1)
                        b.inc(t_.matmul(down_ps[:, 1024 * rs + 512:1024 * (rs + 1)],
                                        lhsT, wdT[:, hc, 512:1024],
                                        start=(hc == 0), stop=(hc == 23),
                                        skip_group_check=True), "T", 1)
                down_done[(g, hc)] = b.tok("T")

            for hc in range(24):
                hsl = hc % 2

                @blk.tensor
                def _(t_, g=g, gsl=gsl, hc=hc, hsl=hsl):
                    b.wait(t_, "T", wg_tok)
                    b.wait(t_, "T", dload_done[g])
                    b.wait(t_, "T", silu_done.get((g, hc - 2) if hc >= 2 else (g - 1, 22 + hc)))
                    for dc in range(8):
                        b.inc(t_.matmul(gate_ps[hsl], wgT[:, hc, 128 * dc:128 * (dc + 1)],
                                        y1fm_sb[:, gsl, dc, :],
                                        start=(dc == 0), stop=(dc == 7)), "T", 1)
                gate_done[(g, hc)] = b.tok("T")

                @blk.tensor
                def _(t_, g=g, gsl=gsl, hc=hc, hsl=hsl):
                    b.wait(t_, "T", wu_tok[hc])
                    b.wait(t_, "T", mul_done.get((g, hc - 2) if hc >= 2 else (g - 1, 22 + hc)))
                    for dc in range(8):
                        b.inc(t_.matmul(up_ps[hsl], wuT[:, hc, 128 * dc:128 * (dc + 1)],
                                        y1fm_sb[:, gsl, dc, :],
                                        start=(dc == 0), stop=(dc == 7)), "T", 1)
                up_done[(g, hc)] = b.tok("T")

                silu_done[(g, hc)] = ("A", b2base[0] + (
                    1 + hc if hc < 5 or g == 0 else
                    3 + hc if hc < 6 else
                    7 + hc if hc < 7 else
                    9 + hc if hc < 8 else
                    13 + hc if hc < 9 else
                    15 + hc if hc < 11 else
                    16 + hc if hc < 13 else
                    17 + hc))

                @blk.vector
                def _(v, g=g, hc=hc, hsl=hsl):
                    b.wait(v, "V", up_done[(g, hc)])
                    b.wait(v, "V", silu_done[(g, hc)])
                    b.wait(v, "V", down_done.get((g, hc - 2) if hc >= 2 else (g - 1, 22 + hc)))
                    b.wait(v, "V", b.tok("V"))
                    b.inc(v.tensor_tensor(h_bf[:, hsl, :], up_ps[hsl], sg_bf[:, hsl, :],
                                          ALU.mult), "V", 1)
                mul_done[(g, hc)] = b.tok("V")

                if hc == 9 and pending_vln2[0] is not None:
                    pending_vln2[0](0)
                if hc == 10 and pending_vln2[0] is not None:
                    pending_vln2[0](1)
                    pending_vln2[0] = None
                if hc >= 1:
                    emit_down(hc - 1)
                if g == 0 and hc + 2 <= 23:
                    emit_wu_chunk(hc + 2)
                    if hc >= 1:
                        emit_wd_chunk(hc + 2)
            emit_down(23)

            for rs in range(2):
                @blk.vector
                def _(v, g=g, gsl=gsl, rs=rs):
                    b.wait(v, "V", down_done[(g, 23)])
                    b.wait(v, "V", rload_done[g])
                    b.wait(v, "V", apply2_done.get((g - 1, rs)))
                    b.wait(v, "V", b.tok("V"))
                    b.inc(v.tensor_tensor(y2pre[:, rs, :], down_ps[:, 1024 * rs:1024 * (rs + 1)],
                                          y1r32[:, gsl, rs, :], ALU.add), "V", 1)
                resid2_done[(g, rs)] = b.tok("V")

            for rs in range(2):
                apply2_done[(g, rs)] = ("A", nb2(g) + (8 + 8 * rs if g == NGg - 1 else 26 + 3 * rs))

            if g + 2 < NGg:
                emit_gloads(g + 2)

            def emit_vln2(rs, g=g, s=s, r0=r0):
                rstd2_done = ("A", nb2(g) + (7 + 8 * rs if g == NGg - 1 else 22 + rs))

                @blk.vector
                def _(v, g=g, rs=rs, rstd2_done=rstd2_done):
                    b.wait(v, "V", rstd2_done)
                    b.wait(v, "V", b.tok("V"))
                    b.inc(v.reciprocal(l2rstd[:, rs, :], l2ln[:, rs, :]), "V", 1)
                    b.wait(v, "V", b.tok("V"))
                    b.inc(v.scalar_tensor_tensor(l2mb[:, rs, :], l2m[:, rs, :],
                                                 -1.0, l2rstd[:, rs, :],
                                                 ALU.mult, ALU.mult), "V", 1)
                mb2_toks[(g, rs)] = b.tok("V")

                @blk.gpsimd
                def _(g_, g=g, rs=rs, s=s, r0=r0):
                    b.wait(g_, "G", apply2_done[(g, rs)])
                    b.inc(g_.dma_start(out_h.ap()[s, r0 + 128 * rs:r0 + 128 * (rs + 1), :],
                                       y2[:, g % 2, rs, :]), "st%d" % (g % 2), 16)
                store_done[(g, rs)] = b.tok("st%d" % (g % 2))

            pending_vln2[0] = emit_vln2

            @blk.scalar
            def _(a, g=g, b2base=b2base, prev_ln2=pending_ln2[0]):
                assert b.cnt["A"] == b2base[0], (b.cnt["A"], b2base[0])
                for hc in range(24):
                    hsl = hc % 2
                    b.wait(a, "A", gate_done[(g, hc)])
                    b.wait(a, "A", mul_done.get((g, hc - 2) if hc >= 2 else (g - 1, 22 + hc)))
                    b.inc(a.activation(sg_bf[:, hsl, :], gate_ps[hsl], AF.Silu), "A", 1)
                    if prev_ln2 is None:
                        continue
                    gp = prev_ln2
                    if hc in (4, 6):
                        rs = 0 if hc == 4 else 1
                        b.wait(a, "A", resid2_done[(gp, rs)])
                        b.wait(a, "A", b.tok("A"))
                        b.inc(a.activation(y2sq[:, :], y2pre[:, rs, :], AF.Identity,
                                           accum_out=s2s[:, rs, :]), "A", 1)
                        b.wait(a, "A", b.tok("A"))
                        b.inc(a.activation(y2sq[:, :], y2pre[:, rs, :], AF.Square,
                                           accum_out=s2q[:, rs, :]), "A", 1)
                    if hc in (5, 7):
                        rs = 0 if hc == 5 else 1
                        b.wait(a, "A", b.tok("A"))
                        b.inc(a.activation(l2m[:, rs, :], s2s[:, rs, :], AF.Identity,
                                           scale=1.0 / D), "A", 1)
                        b.wait(a, "A", b.tok("A"))
                        b.inc(a.activation(l2msq[:, rs, :], l2m[:, rs, :], AF.Square), "A", 1)
                        b.inc(a.activation(l2vpe[:, rs, :], s2q[:, rs, :], AF.Identity,
                                           scale=1.0 / D), "A", 1)
                        b.wait(a, "A", b.tok("A"))
                        b.inc(a.activation(l2vpe[:, rs, :], l2msq[:, rs, :], AF.Identity,
                                           scale=-1.0, bias=l2vpe[:, rs, :]), "A", 1)
                    if hc == 8:
                        for rs in range(2):
                            b.wait(a, "A", b.tok("A"))
                            b.inc(a.activation(l2ln[:, rs, :], l2vpe[:, rs, :], AF.Sqrt,
                                               bias=lneps[:, :]), "A", 1)
                    if hc in (10, 12):
                        rs = 0 if hc == 10 else 1
                        b.wait(a, "A", mb2_toks[(gp, rs)])
                        b.wait(a, "A", store_done.get((gp - 2, 1)))
                        b.inc(a.activation(y2[:, gp % 2, rs, :], y2pre[:, rs, :], AF.Identity,
                                           bias=l2mb[:, rs, :], scale=l2rstd[:, rs, :]),
                              "A", 1)
            pending_ln2[0] = g

        if pending_vln2[0] is not None:
            pending_vln2[0](0)
            pending_vln2[0](1)
            pending_vln2[0] = None

        @blk.scalar
        def _(a, prev_ln2=pending_ln2[0]):
            gp = prev_ln2
            for rs in range(2):
                b.wait(a, "A", resid2_done[(gp, rs)])
                b.wait(a, "A", b.tok("A"))
                b.inc(a.activation(y2sq[:, :], y2pre[:, rs, :], AF.Identity,
                                   accum_out=s2s[:, rs, :]), "A", 1)
                b.wait(a, "A", b.tok("A"))
                b.inc(a.activation(y2sq[:, :], y2pre[:, rs, :], AF.Square,
                                   accum_out=s2q[:, rs, :]), "A", 1)
                b.wait(a, "A", b.tok("A"))
                b.inc(a.activation(l2m[:, rs, :], s2s[:, rs, :], AF.Identity,
                                   scale=1.0 / D), "A", 1)
                b.wait(a, "A", b.tok("A"))
                b.inc(a.activation(l2msq[:, rs, :], l2m[:, rs, :], AF.Square), "A", 1)
                b.inc(a.activation(l2vpe[:, rs, :], s2q[:, rs, :], AF.Identity,
                                   scale=1.0 / D), "A", 1)
                b.wait(a, "A", b.tok("A"))
                b.inc(a.activation(l2vpe[:, rs, :], l2msq[:, rs, :], AF.Identity,
                                   scale=-1.0, bias=l2vpe[:, rs, :]), "A", 1)
                b.wait(a, "A", b.tok("A"))
                b.inc(a.activation(l2ln[:, rs, :], l2vpe[:, rs, :], AF.Sqrt,
                                   bias=lneps[:, :]), "A", 1)
                b.wait(a, "A", mb2_toks[(gp, rs)])
                b.wait(a, "A", store_done.get((gp - 2, 1)))
                b.inc(a.activation(y2[:, gp % 2, rs, :], y2pre[:, rs, :], AF.Identity,
                                   bias=l2mb[:, rs, :], scale=l2rstd[:, rs, :]), "A", 1)

        @blk.gpsimd
        def _(g_):
            g_.wait_ge(b.sems["st0"], b.cnt["st0"])
            g_.wait_ge(b.sems["st1"], b.cnt["st1"])

    es_b.close()
    es_outer.close()
    return nc


_IDENT = np.eye(128, dtype=ml_dtypes.bfloat16)


def kernel(L0_attn_x, L0_attn_W_qkv, L0_attn_b_qkv, L0_attn_W_out, L0_attn_b_out,
           L0_ffn_w_gate, L0_ffn_w_up, L0_ffn_w_down):
    x = np.ascontiguousarray(np.asarray(L0_attn_x, dtype=np.float32))
    wqkv = np.ascontiguousarray(np.asarray(L0_attn_W_qkv, dtype=np.float32))
    bqkv = np.ascontiguousarray(np.asarray(L0_attn_b_qkv, dtype=np.float32)).reshape(1, 3 * D)
    wout = np.ascontiguousarray(np.asarray(L0_attn_W_out, dtype=np.float32))
    bout = np.ascontiguousarray(np.asarray(L0_attn_b_out, dtype=np.float32)).reshape(1, D)
    wg = np.ascontiguousarray(np.asarray(L0_ffn_w_gate, dtype=np.int32))
    wu = np.ascontiguousarray(np.asarray(L0_ffn_w_up, dtype=np.int32))
    wd = np.ascontiguousarray(np.asarray(L0_ffn_w_down, dtype=np.int32))

    nc = build_program()
    in_maps = []
    for i in range(NCORES):
        in_maps.append({
            "x": x[SEQS * i:SEQS * (i + 1)],
            "wqkv": wqkv, "bqkv": bqkv, "wout": wout, "bout": bout,
            "wg": wg, "wu": wu, "wd": wd, "ident": _IDENT,
        })
    res = run_bass_kernel_spmd(nc, in_maps, core_ids=list(range(NCORES)))
    return np.concatenate([r["out"] for r in res.results], axis=0)


# version 31
# speedup vs baseline: 1.4172x; 1.1007x over previous
import contextlib

import numpy as np
import ml_dtypes
import concourse.bass as bass
import concourse.mybir as mybir
from concourse.bass_utils import run_bass_kernel_spmd

F32 = mybir.dt.float32
BF16 = mybir.dt.bfloat16
I32 = mybir.dt.int32
AF = mybir.ActivationFunctionType
ALU = mybir.AluOpType
AX = mybir.AxisListType

NCORES = 8
SEQS = 2
L = 4096
D = 1024
HID = 3072
TB = 17
NW = 16
RTS = 32
INV127 = float(np.float32(1.0 / 127.0))
EPS = 1e-5


class Bld:
    def __init__(self, nc, es):
        self.nc = nc
        self.sems = {}
        self.cnt = {}
        self.waited = {}
        self.es = es

    def sem(self, name):
        self.sems[name] = self.es.enter_context(self.nc.semaphore(name))
        self.cnt[name] = 0

    def inc(self, instr, name, by=1):
        instr.then_inc(self.sems[name], by)
        self.cnt[name] += by
        return (name, self.cnt[name])

    def tok(self, name):
        return (name, self.cnt[name])

    def wait(self, eng, engkey, tok):
        if tok is None:
            return
        name, val = tok
        if val <= 0:
            return
        key = (engkey, name)
        if self.waited.get(key, 0) >= val:
            return
        self.waited[key] = val
        eng.wait_ge(self.sems[name], val)


def build_program(debug=False, seqs=SEQS, l=L):
    TB = l // 256 + 1
    NW = l // 256
    RTS = l // 128
    SEQSv, Lv = seqs, l

    def lastwin(gb):
        if gb is None or gb < 0:
            return None
        s, t = gb // TB, gb % TB
        return s * NW + min(t, NW - 1)

    nc = bass.Bass("TRN2", target_bir_lowering=False)

    x_h = nc.dram_tensor("x", [SEQSv, Lv, D], F32, kind="ExternalInput")
    wqkv_h = nc.dram_tensor("wqkv", [D, 3 * D], F32, kind="ExternalInput")
    bqkv_h = nc.dram_tensor("bqkv", [1, 3 * D], F32, kind="ExternalInput")
    wout_h = nc.dram_tensor("wout", [D, D], F32, kind="ExternalInput")
    bout_h = nc.dram_tensor("bout", [1, D], F32, kind="ExternalInput")
    wg_h = nc.dram_tensor("wg", [HID, D], I32, kind="ExternalInput")
    wu_h = nc.dram_tensor("wu", [HID, D], I32, kind="ExternalInput")
    wd_h = nc.dram_tensor("wd", [D, HID], I32, kind="ExternalInput")
    ident_h = nc.dram_tensor("ident", [128, 128], BF16, kind="ExternalInput")
    out_h = nc.dram_tensor("out", [SEQSv, Lv, D], F32, kind="ExternalOutput")
    ykind = "ExternalOutput" if debug else "Internal"
    y1f_h = nc.dram_tensor("y1dbg", [SEQSv, Lv, D], F32, kind=ykind)
    y1fm_h = nc.dram_tensor("y1fm_scr", [SEQSv, Lv // 256, 128, 8, 256], BF16,
                            kind="Internal")
    wgT_h = nc.dram_tensor("wgT_scr", [24, 128, D], BF16, kind="Internal")
    wuT_h = nc.dram_tensor("wuT_scr", [24, 128, D], BF16, kind="Internal")
    wdT_h = nc.dram_tensor("wdT_scr", [24, 128, D], BF16, kind="Internal")

    es_outer = contextlib.ExitStack()
    b = Bld(nc, es_outer)
    for s in ["T", "V", "A", "G",
              "xq0", "xq1", "xq2", "xq3", "xr0", "xr1",
              "wl0", "wl1", "wl2", "wl3", "wl4", "wg0",
              "tx0", "tx1", "tv0", "tv1", "ty0", "ty1", "tw0", "tw1",
              "wc0", "wc1", "wc2", "wc3", "wi0", "pk0", "pk1", "pf0", "pf1", "pw0", "pw1",
              "yf0", "yf1", "yr0", "yr1", "st0", "st1"]:
        b.sem(s)

    SO = lambda n, sh, dt, **kw: es_outer.enter_context(nc.sbuf_tensor(n, sh, dt, **kw))
    lneps = SO("lneps", [128, 1], F32)
    PS = lambda n, sh, dt: es_outer.enter_context(nc.psum_tensor(n, sh, dt))
    psum_qkv = PS("psum_qkv", [128, D], F32)
    psum_S = PS("psum_S", [128, 512], F32)
    psum_TR = PS("psum_TR", [128, 1024], BF16)
    psum_O = PS("psum_O", [128, 2048], F32)

    es_a = contextlib.ExitStack()
    SA = lambda n, sh, dt, **kw: es_a.enter_context(nc.sbuf_tensor(n, sh, dt, **kw))

    es_qkv = contextlib.ExitStack()
    wqkv_bf = es_qkv.enter_context(
        nc.sbuf_tensor("wqkv_bf", [128, 8, 3 * D], BF16, side="right"))
    wgT_holder = [None]
    wout_bf = SA("wout_bf", [128, 8, D], BF16)
    bq_rep = SA("bq_rep", [128, 3 * D], BF16)
    bo_rep = SA("bo_rep", [128, D], BF16)
    ident_sb = SA("ident_sb", [128, 128], BF16)
    ones128 = SA("ones128", [128, 128], BF16)
    ones1 = SA("ones1", [1, 128], BF16)

    xt_bf = SA("xt_bf", [128, 3, D], BF16)
    x_fm = SA("x_fm", [128, 2, 8, 128], BF16)
    q_tm = SA("q_tm", [128, 3, D], BF16)
    k_tm = SA("k_tm", [128, 3, D], BF16)
    v_tmp = SA("v_tmp", [128, 3, D], BF16)
    v_fm = SA("v_fm", [128, 3, 8, 128], BF16)
    attn_out = SA("attn_out", [128, 8, 512], BF16)
    negmax = SA("negmax", [128, 8], F32)
    ssub = SA("ssub", [128, 512], F32)
    aexp = SA("aexp", [128, 512], F32)
    asum = SA("asum", [128, 8], F32)
    arecip = SA("arecip", [128, 8], F32)
    a_bf = SA("a_bf", [128, 512], BF16)
    a_bd = SA("a_bd", [128, 2, 8, 128], BF16)
    at_bf = SA("at_bf", [128, 1024], BF16)
    xr32 = SA("xr32", [128, 2, D], F32)
    y1pre = SA("y1pre", [128, 2, D], F32)
    y1sq = SA("y1sq", [128, D], F32)
    sums = SA("sums", [128, 2, 1], F32)
    sumsq = SA("sumsq", [128, 2, 1], F32)
    lnm = SA("lnm", [128, 2, 1], F32)
    lnmsq = SA("lnmsq", [128, 2, 1], F32)
    lnvpe = SA("lnvpe", [128, 2, 1], F32)
    lnrstd = SA("lnrstd", [128, 2, 1], F32)
    lnln = SA("lnln", [128, 2, 1], F32)
    lnmb = SA("lnmb", [128, 2, 1], F32)
    y1_f32 = SA("y1_f32", [128, 2, D], F32)
    y1_bf = SA("y1_bf", [128, 2, D], BF16)
    y1fm_stage = SA("y1fm_stage", [128, 4, 8, 256], BF16)

    es_w = contextlib.ExitStack()
    SW = lambda n, sh, dt, **kw: es_w.enter_context(nc.sbuf_tensor(n, sh, dt, **kw))
    w_in_bf = SW("w_in_bf", [128, 3, HID], BF16)
    wT_stage = SW("wT_stage", [128, 8, 128], BF16)
    bq_bf = SW("bq_bf", [1, 3 * D], BF16)
    bo_bf = SW("bo_bf", [1, D], BF16)

    with nc.Block() as blk:
        @blk.gpsimd
        def _(g):
            b.inc(g.memset(ones128[:, :], 1.0), "G", 1)
            b.inc(g.memset(ones1[:, :], 1.0), "G", 1)
            b.inc(g.memset(a_bd[:, :, :, :], 0.0), "G", 1)
            b.inc(g.memset(lneps[:, :], float(EPS)), "G", 1)
        memset_tok = b.tok("G")

        @blk.sync
        def _(q):
            b.inc(q.dma_start(ident_sb[:, :], ident_h.ap()), "wl0", 16)

        @blk.gpsimd
        def _(g):
            for c in range(8):
                b.inc(g.dma_start(wqkv_bf[:, c, :], wqkv_h.ap()[128 * c:128 * (c + 1), :]),
                      "wi0", 16)
                b.inc(g.dma_start(wout_bf[:, c, :], wout_h.ap()[128 * c:128 * (c + 1), :]),
                      "wi0", 16)
            b.inc(g.dma_start(bq_bf[:, :], bqkv_h.ap()), "wi0", 16)
            b.inc(g.dma_start(bo_bf[:, :], bout_h.ap()), "wi0", 16)
        wcast_tok = b.tok("wi0")
        prev_cast = wcast_tok

        @blk.vector
        def _(v):
            b.wait(v, "V", wcast_tok)
            b.inc(v.tensor_scalar_mul(bq_bf[:, :], bq_bf[:, :], 1.0 / 128.0), "V", 1)
            b.inc(v.tensor_scalar_mul(bo_bf[:, :], bo_bf[:, :], 1.0 / 128.0), "V", 1)
        bias_bf_tok = b.tok("V")

        brep_tok = None
        for g3 in range(4):
            @blk.tensor
            def _(t_, g3=g3, brep_tok=brep_tok):
                b.wait(t_, "T", bias_bf_tok)
                b.wait(t_, "T", memset_tok)
                b.wait(t_, "T", brep_tok)
                if g3 < 3:
                    r0 = bq_bf[0:1, 1024 * g3:1024 * g3 + 512]
                    r1 = bq_bf[0:1, 1024 * g3 + 512:1024 * (g3 + 1)]
                else:
                    r0, r1 = bo_bf[0:1, 0:512], bo_bf[0:1, 512:1024]
                b.inc(t_.matmul(psum_qkv[:, 0:512], ones1[:, :], r0, start=True, stop=True), "T", 1)
                b.inc(t_.matmul(psum_qkv[:, 512:1024], ones1[:, :], r1, start=True, stop=True), "T", 1)
            mm_tok = b.tok("T")

            @blk.vector
            def _(v, g3=g3, mm_tok=mm_tok):
                b.wait(v, "V", mm_tok)
                dst = bq_rep[:, 1024 * g3:1024 * (g3 + 1)] if g3 < 3 else bo_rep[:, :]
                b.inc(v.tensor_copy(dst, psum_qkv[:, :]), "V", 1)
            brep_tok = b.tok("V")

        @blk.sync
        def _(q):
            b.wait(q, "Q", prev_cast)
            b.wait(q, "Q", brep_tok)

    cast_done, tx_done, qkv_mm_done, evac_done, tv_done = {}, {}, {}, {}, {}
    evac_half, qkv_half = {}, {}
    xfm_done, tvmm_done, tymm_done = {}, {}, {}
    tr_banks = [[0, None], [1, None], [2, None]]
    tr_idx = [0]
    qkv_free = [None]
    tr_noq = [False]

    def claim_tr():
        bk = tr_banks[tr_idx[0]]
        tr_idx[0] = (tr_idx[0] + 1) % 3
        if tr_noq[0] and bk[0] == 2:
            bk = tr_banks[tr_idx[0]]
            tr_idx[0] = (tr_idx[0] + 1) % 3
        extra = qkv_free[0] if bk[0] == 2 else None
        return bk, extra

    def tr_slice(bi, c0, c1):
        if bi == 0:
            return psum_TR[:, c0:c1]
        if bi == 1:
            return psum_S[:, c0 // 2:c1 // 2].bitcast(BF16)
        return psum_qkv[:, c0 // 2:c1 // 2].bitcast(BF16)
    s_mm_done, ssub_done, exp_done, scale_done, bd_done = {}, {}, {}, {}, {}
    at_mm_done, atevac_done, o_mm_done, oevac_done = {}, {}, {}, {}
    proj_done, resid_done, sq_done, apply_done = {}, {}, {}, {}
    ty_done, pk_done, pf_done, xq_done, xr_done = {}, {}, {}, {}, {}
    last_resid = None

    a_join_T = [wcast_tok, memset_tok, brep_tok, ("wl0", 16)]

    with nc.Block() as blk:
        @blk.tensor
        def _(t_):
            for tk in a_join_T:
                b.wait(t_, "T", tk)

        def emit_xload(gt2):
            if gt2 >= SEQSv * TB:
                return
            s2, t2 = gt2 // TB, gt2 % TB
            xsl = gt2 % 3

            @blk.gpsimd
            def _(g, s2=s2, t2=t2, gt2=gt2, xsl=xsl):
                b.wait(g, "G", tx_done.get(gt2 - 3))
                b.inc(g.dma_start(xt_bf[:, xsl, :], x_h.ap()[s2, 128 * t2:128 * (t2 + 1), :]),
                      "xq%d" % xsl, 16)
            cast_done[gt2] = b.tok("xq%d" % xsl)
            xq_done[gt2] = cast_done[gt2]

        def emit_xtr(gt2):
            if gt2 >= SEQSv * TB:
                return
            sl2 = gt2 % 2
            xsl = gt2 % 3
            bk, qtok = claim_tr()
            bi, freetok = bk[0], bk[1]

            @blk.tensor
            def _(t_, gt2=gt2, xsl=xsl, bi=bi, freetok=freetok, qtok=qtok):
                b.wait(t_, "T", cast_done[gt2])
                b.wait(t_, "T", freetok)
                b.wait(t_, "T", qtok)
                for c in range(8):
                    b.inc(t_.transpose(tr_slice(bi, 128 * c, 128 * (c + 1)),
                                       xt_bf[:, xsl, 128 * c:128 * (c + 1)],
                                       ident_sb[:, :]), "T", 1)
            tx_done[gt2] = b.tok("T")

            @blk.vector
            def _(v, gt2=gt2, sl2=sl2, bi=bi):
                b.wait(v, "V", tx_done[gt2])
                b.wait(v, "V", qkv_mm_done.get(gt2 - 2))
                b.inc(v.tensor_copy(x_fm[:, sl2, :, :].rearrange("p a b -> p (a b)"),
                                    tr_slice(bi, 0, 1024)), "V", 1)
            xfm_done[gt2] = b.tok("V")
            bk[1] = xfm_done[gt2]

        pending_typf = [None]

        pf_idx = [0]
        pf_by_idx = {}

        def emit_ty_one(gw, s, n, k2):
            wsl = n % 4
            if True:
                grt = s * RTS + 2 * n + k2
                rsl = grt % 2
                bk, qtok = claim_tr()
                bi, freetok = bk[0], bk[1]

                @blk.tensor
                def _(t_, grt=grt, rsl=rsl, bi=bi, freetok=freetok, qtok=qtok):
                    b.wait(t_, "T", apply_done[grt])
                    b.wait(t_, "T", freetok)
                    b.wait(t_, "T", qtok)
                    for c in range(8):
                        b.inc(t_.transpose(tr_slice(bi, 128 * c, 128 * (c + 1)),
                                           y1_bf[:, rsl, 128 * c:128 * (c + 1)],
                                           ident_sb[:, :]), "T", 1)
                tymm_done[grt] = b.tok("T")

                pp1 = pf_by_idx.get(pf_idx[0] - 1)
                pp2 = pf_by_idx.get(pf_idx[0] - 2)

                @blk.vector
                def _(v, grt=grt, rsl=rsl, k2=k2, wsl=wsl, bi=bi, pp1=pp1, pp2=pp2):
                    b.wait(v, "V", tymm_done[grt])
                    b.wait(v, "V", pp1)
                    b.wait(v, "V", pp2)
                    dst = y1fm_stage[:, wsl, :, 128 * k2:128 * (k2 + 1)]
                    b.inc(v.tensor_copy(dst, tr_slice(bi, 0, 1024)
                                        .rearrange("p (a b) -> p a b", a=8)),
                          "V", 1)
                ty_done[grt] = b.tok("V")
                bk[1] = ty_done[grt]

        def emit_pf(gw, s, n):
            wsl = n % 4
            nw0 = n - wsl
            pi = pf_idx[0]
            pf_idx[0] += 1
            prev2 = pf_by_idx.get(pi - 2)

            @blk.gpsimd
            def _(g, s=s, n=n, gw=gw, wsl=wsl, nw0=nw0, pi=pi, prev2=prev2):
                b.wait(g, "G", prev2)
                b.wait(g, "G", ty_done[s * RTS + 2 * n])
                b.wait(g, "G", ty_done[s * RTS + 2 * n + 1])
                src_ap = y1fm_stage[:, 0:wsl + 1, :, :].rearrange(
                    "p w a b -> p w (a b)")
                b.inc(g.dma_start(
                    y1fm_h.ap()[s, nw0:n + 1].rearrange(
                        "w p a b -> p w (a b)"),
                    src_ap), "pf%d" % (pi % 2), 16)
            pf_done[gw] = b.tok("pf%d" % (pi % 2))
            pf_by_idx[pi] = pf_done[gw]

        def flush_typf(which=2):
            if pending_typf[0] is not None:
                gw0, s0, n0 = pending_typf[0]
                if which >= 1 and pending_typf[0] not in ty_started:
                    ty_started.add(pending_typf[0])
                    emit_ty_one(gw0, s0, n0, 0)
                if which == 2:
                    emit_ty_one(gw0, s0, n0, 1)
                    if (n0 % 4 == 3) or (n0 == NW - 1):
                        emit_pf(gw0, s0, n0)
                    pending_typf[0] = None
        ty_started = set()

        wprep_state = {"stage_free": None}
        deq_prev = {}
        tw_prev = {}

        def prep_load(i):
            load_src, n_el, _batches = wprep_jobs[i]
            sl = i % 3
            sp = i % 2

            @blk.gpsimd
            def _(g, i=i, sl=sl, load_src=load_src, n_el=n_el):
                b.wait(g, "G", tw_prev.get(i - 3))
                b.inc(g.dma_start(w_in_bf[:, sl, 0:n_el], load_src), "wc%d" % sl, 16)
            wc_tok = b.tok("wc%d" % sl)

            deq_prev[i] = wc_tok

        def prep_transpose(i):
            _src, n_el, sub_transposes = wprep_jobs[i]
            sl = i % 3
            sp = i % 2

            @blk.vector
            def _(v, i=i, sl=sl, n_el=n_el):
                b.wait(v, "V", deq_prev[i])
                b.inc(v.tensor_scalar_mul(w_in_bf[:, sl, 0:n_el], w_in_bf[:, sl, 0:n_el],
                                          INV127), "V", 1)
            dq_tok = b.tok("V")
            for batch in sub_transposes:
                bk, qtok = claim_tr()
                bi, tr_free = bk[0], bk[1]

                @blk.tensor
                def _(t_, i=i, sl=sl, batch=batch, bi=bi, tr_free=tr_free,
                      qtok=qtok, dq_tok=dq_tok):
                    b.wait(t_, "T", dq_tok)
                    b.wait(t_, "T", tr_free)
                    b.wait(t_, "T", qtok)
                    for j, (src_sl, _dst, _comb) in enumerate(batch):
                        b.inc(t_.transpose(tr_slice(bi, 128 * j, 128 * (j + 1)),
                                           w_in_bf[:, sl, src_sl[0]:src_sl[1]],
                                           ident_sb[:, :]), "T", 1)
                tw_tok = b.tok("T")
                tw_prev[i] = tw_tok

                @blk.vector
                def _(v, tw_tok=tw_tok, bi=bi, sf=wprep_state["stage_free"]):
                    b.wait(v, "V", tw_tok)
                    b.wait(v, "V", sf)
                    b.inc(v.tensor_copy(wT_stage[:, :, :].rearrange("p a b -> p (a b)"),
                                        tr_slice(bi, 0, 1024)), "V", 1)
                ev_tok = b.tok("V")
                bk[1] = ev_tok

                @blk.gpsimd
                def _(g, batch=batch, ev_tok=ev_tok, sl=sl):
                    b.wait(g, "G", ev_tok)
                    comb, park_tr = batch[0][2], batch[0][1]
                    if park_tr:
                        b.inc(g.dma_start(comb.transpose([1, 0, 2]), wT_stage[:, :, :]),
                              "pw%d" % sp, 16)
                    else:
                        b.inc(g.dma_start(comb,
                                          wT_stage[:, :, :].rearrange("p a b -> p (a b)")),
                              "pw%d" % sp, 16)
                wprep_state["stage_free"] = b.tok("pw%d" % sp)

        wprep_jobs = []
        for src_h, dst_h in [(wg_h, wgT_h), (wu_h, wuT_h)]:
            for hc in range(24):
                comb = dst_h.ap()[hc]
                batch = [((128 * dc, 128 * (dc + 1)), False, comb) for dc in range(8)]
                wprep_jobs.append((src_h.ap()[128 * hc:128 * (hc + 1), :], D, [batch]))
        for dc in range(8):
            batches = []
            for hb in range(3):
                comb = wdT_h.ap()[8 * hb:8 * (hb + 1), :, 128 * dc:128 * (dc + 1)]
                batches.append([((128 * (8 * hb + j), 128 * (8 * hb + j + 1)), True, comb)
                                for j in range(8)])
            wprep_jobs.append((wd_h.ap()[128 * dc:128 * (dc + 1), :], HID, batches))
        wprep_ld = [0]
        wprep_tr = [0]

        def prep_loads(n_load):
            k = 0
            while wprep_ld[0] < len(wprep_jobs) and k < n_load:
                i = wprep_ld[0]
                if i >= 3 and (i - 3) not in tw_prev:
                    break
                prep_load(i)
                wprep_ld[0] += 1
                k += 1

        def prep_trs(n_tr):
            k = 0
            while wprep_tr[0] < wprep_ld[0] and k < n_tr:
                if wprep_tr[0] + 2 > wprep_ld[0]:
                    break
                prep_transpose(wprep_tr[0])
                wprep_tr[0] += 1
                k += 1

        def emit_wg_stream(gt_last):
            es_qkv.close()
            wgT_holder[0] = es_outer.enter_context(
                nc.sbuf_tensor("wgT", [128, 24, D], BF16, side="right"))
            pw_fin = [b.tok("pw0"), b.tok("pw1")]

            @blk.sync
            def _(q, gt_last=gt_last, pw_fin=pw_fin):
                b.wait(q, "Q", qkv_mm_done[(gt_last, 2)])
                for tk in pw_fin:
                    b.wait(q, "Q", tk)
                b.inc(q.dma_start(wgT_holder[0][:, :, :],
                                  wgT_h.ap().transpose([1, 0, 2])), "wg0", 16)

        emit_xload(0)
        emit_xload(1)
        emit_xload(2)
        emit_xtr(0)
        emit_xtr(1)

        for s in range(SEQSv):
            for t in range(TB + 1):
                gt = s * TB + t
                sl = gt % 2
                sl3 = gt % 3
                have_qkv = t < TB
                have_w1 = 1 <= t <= NW
                have_w2 = t >= 2
                abase = [b.cnt["A"]]
                off_exp = 9 if have_w2 else 1

                prep_loads(2)
                if have_qkv:
                    emit_xload(gt + 3)

                def emit_qkv_pass(p, gt=gt, sl=sl):
                    prev = (gt, p - 1) if p > 0 else (gt - 1, 2)
                    qsfree = tr_banks[2][1] if p == 0 else None

                    @blk.tensor
                    def _(t_, gt=gt, sl=sl, p=p, prev=prev, qsfree=qsfree):
                        b.wait(t_, "T", xfm_done[gt])
                        b.wait(t_, "T", evac_half.get((prev, 0)))
                        b.wait(t_, "T", qsfree)
                        for c in range(8):
                            lhsT = x_fm[:, sl, c, :]
                            b.inc(t_.matmul(psum_qkv[:, 0:512],
                                            lhsT, wqkv_bf[:, c, 1024 * p:1024 * p + 512],
                                            start=(c == 0), stop=False), "T", 1)
                            if c == 0:
                                b.wait(t_, "T", evac_half.get((prev, 1)))
                            b.inc(t_.matmul(psum_qkv[:, 512:1024],
                                            lhsT, wqkv_bf[:, c, 1024 * p + 512:1024 * (p + 1)],
                                            start=(c == 0), stop=False), "T", 1)
                        b.inc(t_.matmul(psum_qkv[:, 0:512], ones128[:, :],
                                        bq_rep[:, 1024 * p:1024 * p + 512],
                                        start=False, stop=True), "T", 1)
                        qkv_half[(gt, p, 0)] = b.tok("T")
                        b.inc(t_.matmul(psum_qkv[:, 512:1024], ones128[:, :],
                                        bq_rep[:, 1024 * p + 512:1024 * (p + 1)],
                                        start=False, stop=True), "T", 1)
                    qkv_mm_done[(gt, p)] = b.tok("T")
                    qkv_half[(gt, p, 1)] = qkv_mm_done[(gt, p)]
                    if p == 2:
                        qkv_mm_done[gt] = b.tok("T")

                    @blk.vector
                    def _(v, gt=gt, sl3=sl3, p=p):
                        b.wait(v, "V", qkv_half[(gt, p, 0)])
                        lw = lastwin(gt - 3)
                        if p == 0:
                            b.wait(v, "V", s_mm_done.get(lw))
                            dst = q_tm
                        elif p == 1:
                            b.wait(v, "V", s_mm_done.get(lw))
                            dst = k_tm
                        else:
                            b.wait(v, "V", tv_done.get(gt - 3))
                            dst = v_tmp
                        b.inc(v.tensor_copy(dst[:, sl3, 0:512], psum_qkv[:, 0:512]), "V", 1)
                        evac_half[((gt, p), 0)] = b.tok("V")
                        b.wait(v, "V", qkv_half[(gt, p, 1)])
                        b.inc(v.tensor_copy(dst[:, sl3, 512:1024], psum_qkv[:, 512:1024]),
                              "V", 1)
                    evac_half[((gt, p), 1)] = b.tok("V")
                    evac_done[(gt, p)] = b.tok("V")

                if have_qkv:
                    emit_qkv_pass(0)
                    emit_qkv_pass(1)

                if have_w1:
                    n1 = t - 1
                    gw1 = s * NW + n1
                    s_sl0 = (s * TB + n1) % 3
                    s_sl1 = (s * TB + n1 + 1) % 3
                    gt_n1 = s * TB + n1 + 1
                    sfree = tr_banks[1][1]
                    exp_done[gw1] = ("A", abase[0] + off_exp)

                    @blk.tensor
                    def _(t_, gw1=gw1, s_sl0=s_sl0, s_sl1=s_sl1, gt_n1=gt_n1, sfree=sfree):
                        b.wait(t_, "T", evac_half[((gt_n1, 1), 0)])
                        b.wait(t_, "T", ssub_done.get(gw1 - 1))
                        b.wait(t_, "T", sfree)
                        for hp in range(8):
                            if hp == 4:
                                b.wait(t_, "T", evac_half[((gt_n1, 1), 1)])
                            for par in range(2):
                                h = 2 * hp + par
                                for j, slj in enumerate((s_sl0, s_sl1)):
                                    b.inc(t_.matmul(
                                        psum_S[64 * par:64 * par + 64, 64 * hp:64 * hp + 64],
                                        q_tm[:, slj, 64 * h:64 * h + 64],
                                        k_tm[:, slj, 64 * h:64 * h + 64],
                                        start=(j == 0), stop=(j == 1),
                                        skip_group_check=True), "T", 1)
                    s_mm_done[gw1] = b.tok("T")

                    @blk.vector
                    def _(v, gw1=gw1):
                        b.wait(v, "V", s_mm_done[gw1])
                        b.wait(v, "V", b.tok("V"))
                        b.inc(v.reduce_max(negmax[:, :],
                                           psum_S[:, :].rearrange("p (s x) -> p s x", s=8),
                                           axis=AX.X, negate=True), "V", 1)
                        b.wait(v, "V", b.tok("V"))
                        b.wait(v, "V", exp_done.get(gw1 - 1))
                        nm = negmax[:, :].unsqueeze(2).broadcast_to([128, 8, 64])
                        b.inc(v.tensor_tensor(ssub[:, :].rearrange("p (s x) -> p s x", s=8),
                                              psum_S[:, :].rearrange("p (s x) -> p s x", s=8),
                                              nm, ALU.add), "V", 1)
                    ssub_done[gw1] = b.tok("V")
                    tr_banks[1][1] = ssub_done[gw1]

                if have_qkv:
                    emit_qkv_pass(2)
                    qkv_free[0] = evac_half[((gt, 2), 1)]
                    emit_xtr(gt + 2)

                if s == SEQSv - 1 and t == TB - 1 and wprep_tr[0] >= 24:
                    emit_wg_stream(gt)

                if have_qkv:
                    vbk, vqtok = claim_tr()
                    vbi, vfreetok = vbk[0], vbk[1]

                    @blk.tensor
                    def _(t_, gt=gt, sl3=sl3, vbi=vbi, vfreetok=vfreetok, vqtok=vqtok):
                        b.wait(t_, "T", evac_half[((gt, 2), 0)])
                        b.wait(t_, "T", vfreetok)
                        b.wait(t_, "T", vqtok)
                        for c in range(8):
                            if c == 4:
                                b.wait(t_, "T", evac_half[((gt, 2), 1)])
                            b.inc(t_.transpose(tr_slice(vbi, 128 * c, 128 * (c + 1)),
                                               v_tmp[:, sl3, 128 * c:128 * (c + 1)],
                                               ident_sb[:, :]), "T", 1)
                    tvmm_done[gt] = b.tok("T")

                    @blk.vector
                    def _(v, gt=gt, sl3=sl3, vbi=vbi):
                        b.wait(v, "V", tvmm_done[gt])
                        b.wait(v, "V", o_mm_done.get(lastwin(gt - 3)))
                        b.inc(v.tensor_copy(v_fm[:, sl3, :, :].rearrange("p a b -> p (a b)"),
                                            tr_slice(vbi, 0, 1024)), "V", 1)
                    tv_done[gt] = b.tok("V")
                    vbk[1] = tv_done[gt]

                if have_w2:
                    n = t - 2
                    gw = s * NW + n
                    sl0 = (s * TB + n) % 3
                    sl1 = (s * TB + n + 1) % 3
                    wpar = gw % 2

                    flush_typf(which=1)

                    @blk.tensor
                    def _(t_, gw=gw, wpar=wpar):
                        b.wait(t_, "T", bd_done[gw])
                        b.wait(t_, "T", oevac_done.get(gw - 1))
                        b.wait(t_, "T", resid_done.get(2 * (gw - 1)))
                        at_dst = psum_O[:, 0:512].bitcast(BF16)
                        for hp in range(8):
                            b.inc(t_.transpose(at_dst[0:64, 128 * hp:128 * (hp + 1)],
                                               a_bd[:, wpar, hp, 0:64], ident_sb[:, :]),
                                  "T", 1)
                            b.inc(t_.transpose(at_dst[64:128, 128 * hp:128 * (hp + 1)],
                                               a_bd[:, wpar, hp, 64:128], ident_sb[:, :]),
                                  "T", 1)
                    at_mm_done[gw] = b.tok("T")

                    @blk.vector
                    def _(v, gw=gw):
                        b.wait(v, "V", at_mm_done[gw])
                        b.wait(v, "V", b.tok("V"))
                        b.wait(v, "V", o_mm_done.get(gw - 1))
                        b.inc(v.tensor_copy(at_bf[:, :], psum_O[:, 0:512].bitcast(BF16)),
                              "V", 1)
                    atevac_done[gw] = b.tok("V")

                    @blk.tensor
                    def _(t_, gw=gw, s=s, n=n, sl0=sl0, sl1=sl1):
                        b.wait(t_, "T", atevac_done[gw])
                        b.wait(t_, "T", tv_done[s * TB + n])
                        b.wait(t_, "T", tv_done[s * TB + n + 1])
                        b.wait(t_, "T", oevac_done.get(gw - 1))
                        b.wait(t_, "T", resid_done.get(2 * (gw - 1) + 1))
                        for hp in range(8):
                            for c4 in range(4):
                                slj = sl0 if c4 < 2 else sl1
                                off = 64 * (c4 % 2)
                                lhsT = v_fm[:, slj, hp, off:off + 64]
                                dst0 = 512 * (hp // 2) + 256 * (hp % 2) + 64 * c4
                                for par in range(2):
                                    b.inc(t_.matmul(
                                        psum_O[64 * par:64 * par + 64, dst0:dst0 + 64],
                                        lhsT,
                                        at_bf[:, 128 * hp + 64 * par:128 * hp + 64 * par + 64],
                                        start=True, stop=True,
                                        skip_group_check=True), "T", 1)
                    o_mm_done[gw] = b.tok("T")

                    flush_typf(which=2)
                    prep_trs(2)

                    oevac_done[gw] = ("A", abase[0] + 8)
                    lnstart = abase[0] + (10 if have_w1 else 9)

                    for k2 in range(2):
                        rt = 2 * n + k2
                        grt = s * RTS + rt
                        rsl = grt % 2

                        @blk.sync
                        def _(q, s=s, rt=rt, grt=grt, rsl=rsl):
                            b.wait(q, "Q", resid_done.get(grt - 2))
                            b.inc(q.dma_start(xr32[:, rsl, :],
                                              x_h.ap()[s, 128 * rt:128 * (rt + 1), :]),
                                  "xr%d" % rsl, 16)
                        xr_done[grt] = b.tok("xr%d" % rsl)

                        @blk.tensor
                        def _(t_, gw=gw, n=n, k2=k2, grt=grt, s=s):
                            pj = psum_O[:, 1024 * k2:1024 * (k2 + 1)]
                            rb = 256 * (n % 2) + 128 * k2
                            for c in range(8):
                                lhsT = attn_out[:, c, rb:rb + 128]
                                b.wait(t_, "T", ("A", abase[0] + 1 + max(c, 1 + 4 * k2)))
                                b.inc(t_.matmul(pj[:, 0:512], lhsT,
                                                wout_bf[:, c, 0:512],
                                                start=(c == 0), stop=False,
                                                skip_group_check=True), "T", 1)
                                b.wait(t_, "T", ("A", abase[0] + 1 + max(c, 3 + 4 * k2)))
                                b.inc(t_.matmul(pj[:, 512:1024], lhsT,
                                                wout_bf[:, c, 512:1024],
                                                start=(c == 0), stop=False,
                                                skip_group_check=True), "T", 1)
                            b.inc(t_.matmul(pj[:, 0:512], ones128[:, :],
                                            bo_rep[:, 0:512], start=False, stop=True,
                                            skip_group_check=True), "T", 1)
                            b.inc(t_.matmul(pj[:, 512:1024], ones128[:, :],
                                            bo_rep[:, 512:1024], start=False, stop=True,
                                            skip_group_check=True), "T", 1)
                        proj_done[2 * gw + k2] = b.tok("T")

                        @blk.vector
                        def _(v, grt=grt, rsl=rsl, gw=gw, k2=k2):
                            b.wait(v, "V", proj_done[2 * gw + k2])
                            b.wait(v, "V", xr_done[grt])
                            b.wait(v, "V", apply_done.get(grt - 2))
                            b.wait(v, "V", b.tok("V"))
                            b.inc(v.tensor_tensor(y1pre[:, rsl, :],
                                                  psum_O[:, 1024 * k2:1024 * (k2 + 1)],
                                                  xr32[:, rsl, :], ALU.add), "V", 1)
                        resid_done[grt] = b.tok("V")
                        last_resid = resid_done[grt]

                        apply_done[grt] = ("A", lnstart + 11 + 12 * k2)

                        @blk.gpsimd
                        def _(g, s=s, rt=rt, grt=grt, rsl=rsl, lnstart=lnstart):
                            b.wait(g, "G", ("A", lnstart + 10 + 12 * k2))
                            b.inc(g.dma_start(y1f_h.ap()[s, 128 * rt:128 * (rt + 1), :],
                                              y1_f32[:, rsl, :]), "pk%d" % rsl, 16)
                        pk_done[grt] = b.tok("pk%d" % rsl)

                    pending_typf[0] = (gw, s, n)

                if have_w1 or have_w2:
                    w2ctx = (gw, s, n) if have_w2 else None
                    w1gw = gw1 if have_w1 else None

                    @blk.scalar
                    def _(a, w2ctx=w2ctx, w1gw=w1gw, abase=abase):
                        assert b.cnt["A"] == abase[0], (b.cnt["A"], abase[0])
                        if w2ctx is not None:
                            gw, s_, n = w2ctx
                            b.wait(a, "A", o_mm_done[gw])
                            b.wait(a, "A", proj_done.get(2 * (gw - 2) + 1))
                            rb = 256 * (n % 2)
                            for hp in range(8):
                                base = 512 * (hp // 2) + 256 * (hp % 2)
                                b.inc(a.activation(
                                    attn_out[:, hp, rb:rb + 256]
                                    .rearrange("p (d c) -> p d c", c=4),
                                    psum_O[:, base:base + 256]
                                    .rearrange("p (c d) -> p d c", c=4),
                                    AF.Copy), "A", 1)
                        if w1gw is not None:
                            b.wait(a, "A", ssub_done[w1gw])
                            b.wait(a, "A", scale_done.get(w1gw - 1))
                            b.inc(a.activation(aexp[:, :], ssub[:, :], AF.Exp, scale=0.125),
                                  "A", 1)
                        if w2ctx is not None:
                            gw, s_, n = w2ctx
                            for k2 in range(2):
                                grt = s_ * RTS + 2 * n + k2
                                rsl = grt % 2
                                b.wait(a, "A", resid_done[grt])
                                b.wait(a, "A", b.tok("A"))
                                b.inc(a.activation(y1sq[:, :], y1pre[:, rsl, :], AF.Identity,
                                                   accum_out=sums[:, rsl, :]), "A", 1)
                                b.wait(a, "A", b.tok("A"))
                                b.inc(a.activation(y1sq[:, :], y1pre[:, rsl, :], AF.Square,
                                                   accum_out=sumsq[:, rsl, :]), "A", 1)
                                b.wait(a, "A", b.tok("A"))
                                b.inc(a.activation(lnm[:, rsl, :], sums[:, rsl, :], AF.Identity,
                                                   scale=1.0 / D), "A", 1)
                                b.wait(a, "A", b.tok("A"))
                                b.inc(a.activation(lnmsq[:, rsl, :], lnm[:, rsl, :], AF.Square),
                                      "A", 1)
                                b.inc(a.activation(lnvpe[:, rsl, :], sumsq[:, rsl, :],
                                                   AF.Identity, scale=1.0 / D), "A", 1)
                                b.wait(a, "A", b.tok("A"))
                                b.inc(a.activation(lnvpe[:, rsl, :], lnmsq[:, rsl, :],
                                                   AF.Identity,
                                                   scale=-1.0, bias=lnvpe[:, rsl, :]), "A", 1)
                                b.wait(a, "A", b.tok("A"))
                                b.inc(a.activation(lnln[:, rsl, :], lnvpe[:, rsl, :], AF.Ln,
                                                   bias=lneps[:, :]), "A", 1)
                                b.wait(a, "A", b.tok("A"))
                                b.inc(a.activation(lnrstd[:, rsl, :], lnln[:, rsl, :], AF.Exp,
                                                   scale=-0.5), "A", 1)
                                b.wait(a, "A", b.tok("A"))
                                b.inc(a.activation(lnmb[:, rsl, :], lnm[:, rsl, :], AF.Identity,
                                                   scale=lnrstd[:, rsl, :]), "A", 1)
                                b.wait(a, "A", b.tok("A"))
                                b.inc(a.activation(lnmb[:, rsl, :], lnmb[:, rsl, :], AF.Identity,
                                                   scale=-1.0), "A", 1)
                                b.wait(a, "A", b.tok("A"))
                                b.wait(a, "A", pk_done.get(grt - 2))
                                b.wait(a, "A", tymm_done.get(grt - 2))
                                b.inc(a.activation(y1_f32[:, rsl, :], y1pre[:, rsl, :],
                                                   AF.Identity, bias=lnmb[:, rsl, :],
                                                   scale=lnrstd[:, rsl, :]), "A", 1)
                                b.inc(a.activation(y1_bf[:, rsl, :], y1pre[:, rsl, :],
                                                   AF.Identity, bias=lnmb[:, rsl, :],
                                                   scale=lnrstd[:, rsl, :]), "A", 1)

                if have_w1:
                    @blk.vector
                    def _(v, gw1=gw1):
                        b.wait(v, "V", exp_done[gw1])
                        b.wait(v, "V", b.tok("V"))
                        b.inc(v.reduce_sum(asum[:, :],
                                           aexp[:, :].rearrange("p (s x) -> p s x", s=8),
                                           axis=AX.X), "V", 1)
                        b.wait(v, "V", b.tok("V"))
                        b.inc(v.reciprocal(arecip[:, :], asum[:, :]), "V", 1)
                        b.wait(v, "V", b.tok("V"))
                        ar = arecip[:, :].unsqueeze(2).broadcast_to([128, 8, 64])
                        b.inc(v.tensor_tensor(a_bf[:, :].rearrange("p (s x) -> p s x", s=8),
                                              aexp[:, :].rearrange("p (s x) -> p s x", s=8),
                                              ar, ALU.mult), "V", 1)
                    scale_done[gw1] = b.tok("V")

                    @blk.vector
                    def _(v, gw1=gw1):
                        b.wait(v, "V", scale_done[gw1])
                        b.wait(v, "V", b.tok("V"))
                        b.wait(v, "V", at_mm_done.get(gw1 - 2))
                        b.wait(v, "V", memset_tok)
                        wp1 = gw1 % 2
                        b.inc(v.tensor_copy(
                            a_bd[0:64, wp1, :, 0:64],
                            a_bf[0:64, :].rearrange("p (h x) -> p h x", h=8)), "V", 1)
                        b.inc(v.tensor_copy(
                            a_bd[64:128, wp1, :, 64:128],
                            a_bf[64:128, :].rearrange("p (h x) -> p h x", h=8)), "V", 1)
                    bd_done[gw1] = b.tok("V")

        while wprep_tr[0] < len(wprep_jobs):
            while wprep_ld[0] < len(wprep_jobs) and wprep_ld[0] < wprep_tr[0] + 3:
                prep_load(wprep_ld[0])
                wprep_ld[0] += 1
            prep_transpose(wprep_tr[0])
            wprep_tr[0] += 1
        flush_typf()
        if wgT_holder[0] is None:
            emit_wg_stream(SEQSv * TB - 1)
        wg_tok = b.tok("wg0")

        @blk.sync
        def _(q):
            b.wait(q, "Q", b.tok("pk0"))
            b.wait(q, "Q", b.tok("pk1"))
            b.wait(q, "Q", b.tok("pf0"))
            b.wait(q, "Q", b.tok("pf1"))

    a_end = [b.tok(k) for k in ("T", "V", "A", "G")]
    a_end += [b.tok(k) for k in ("pk0", "pk1", "pf0", "pf1",
                                 "xq0", "xq1", "xq2", "xq3", "xr0", "xr1",
                                 "pw0", "pw1", "wc0", "wc1")]

    es_w.close()
    es_a.close()

    es_b = contextlib.ExitStack()
    SBB = lambda n, sh, dt, **kw: es_b.enter_context(nc.sbuf_tensor(n, sh, dt, **kw))
    wgT = wgT_holder[0]
    wuT = SBB("wuT", [128, 24, D], BF16)
    wdT = SBB("wdT", [128, 24, D], BF16)
    y1fm_sb = SBB("y1fm_sb", [128, 2, 8, 256], BF16)
    y1r32 = SBB("y1r32", [128, 2, 2, D], F32)
    sg_bf = SBB("sg_bf", [128, 2, 256], BF16)
    h_bf = SBB("h_bf", [128, 2, 256], BF16)
    y2pre = SBB("y2pre", [128, 2, D], F32)
    y2sq = SBB("y2sq", [128, D], F32)
    y2 = SBB("y2", [128, 2, 2, D], F32)
    s2s = SBB("s2s", [128, 2, 1], F32)
    s2q = SBB("s2q", [128, 2, 1], F32)
    l2m = SBB("l2m", [128, 2, 1], F32)
    l2msq = SBB("l2msq", [128, 2, 1], F32)
    l2vpe = SBB("l2vpe", [128, 2, 1], F32)
    l2rstd = SBB("l2rstd", [128, 2, 1], F32)
    l2ln = SBB("l2ln", [128, 2, 1], F32)
    l2mb = SBB("l2mb", [128, 2, 1], F32)

    gate_ps = [psum_qkv[:, 0:256], psum_qkv[:, 512:768]]
    up_ps = [psum_S[:, 0:256], psum_TR[:, 0:512].bitcast(F32)[:, 0:256]]
    down_ps = psum_O

    gate_done, silu_done, up_done, mul_done, down_done = {}, {}, {}, {}, {}
    apply2_done = {}
    mb2_toks = {}
    pending_ln2 = [None]
    pending_vln2 = [None]
    dload_done, rload_done, resid2_done, sq2_done, y2_done, store_done = {}, {}, {}, {}, {}, {}
    NGg = SEQSv * (Lv // 256)

    with nc.Block() as blk:
        @blk.tensor
        def _(t_):
            for tk in a_end:
                b.wait(t_, "T", tk)

        @blk.vector
        def _(v):
            for tk in a_end:
                b.wait(v, "V", tk)

        @blk.scalar
        def _(a):
            for tk in a_end:
                b.wait(a, "A", tk)

        @blk.gpsimd
        def _(g):
            for tk in a_end:
                b.wait(g, "G", tk)

        @blk.sync
        def _(q):
            for tk in a_end:
                b.wait(q, "Q", tk)

        wu_tok, wd_tok = {}, {}

        def emit_wu_chunk(hc):
            @blk.sync
            def _(q, hc=hc):
                b.wait(q, "Q", up_done.get((0, hc - 2)))
                b.inc(q.dma_start(wuT[:, hc, :], wuT_h.ap()[hc]),
                      "wl%d" % (hc % 2), 16)
            wu_tok[hc] = b.tok("wl%d" % (hc % 2))

        def emit_wd_chunk(hc):
            @blk.sync
            def _(q, hc=hc):
                b.wait(q, "Q", down_done.get((0, hc - 3)))
                b.inc(q.dma_start(wdT[:, hc, :], wdT_h.ap()[hc]),
                      "wl%d" % (2 + hc % 3), 16)
            wd_tok[hc] = b.tok("wl%d" % (2 + hc % 3))

        def emit_gloads(g):
            s = g // (Lv // 256)
            r0 = 256 * (g % (Lv // 256))
            gsl = g % 2

            @blk.sync
            def _(q, g=g, s=s, r0=r0, gsl=gsl):
                b.wait(q, "Q", up_done.get((g - 2, 23)))
                b.inc(q.dma_start(y1fm_sb[:, gsl, :, :],
                                  y1fm_h.ap()[s, g % (Lv // 256)]), "yf%d" % gsl, 16)
            dload_done[g] = b.tok("yf%d" % gsl)

            @blk.sync
            def _(q, g=g, s=s, r0=r0, gsl=gsl):
                b.wait(q, "Q", resid2_done.get((g - 2, 1)))
                for rs in range(2):
                    b.inc(q.dma_start(y1r32[:, gsl, rs, :],
                                      y1f_h.ap()[s, r0 + 128 * rs:r0 + 128 * (rs + 1), :]),
                          "yr%d" % gsl, 16)
            rload_done[g] = b.tok("yr%d" % gsl)

        emit_gloads(0)
        if NGg > 1:
            emit_gloads(1)
        for c0 in (0, 1):
            emit_wu_chunk(c0)
            emit_wd_chunk(c0)
        emit_wd_chunk(2)

        b2bases = {}

        def nb2(g):
            return b2bases[g] + (24 if g == 0 else 40)

        for g in range(NGg):
            s = g // (Lv // 256)
            r0 = 256 * (g % (Lv // 256))
            gsl = g % 2
            b2base = [b.cnt["A"] if g == 0 else nb2(g - 1)]
            b2bases[g] = b2base[0]

            def emit_down(hc, g=g):
                hsl = hc % 2

                @blk.tensor
                def _(t_, g=g, hc=hc, hsl=hsl):
                    b.wait(t_, "T", wd_tok[hc])
                    b.wait(t_, "T", mul_done[(g, hc)])
                    if hc == 0:
                        b.wait(t_, "T", resid2_done.get((g - 1, 1)))
                    for rs in range(2):
                        lhsT = h_bf[:, hsl, 128 * rs:128 * (rs + 1)]
                        b.inc(t_.matmul(down_ps[:, 1024 * rs:1024 * rs + 512],
                                        lhsT, wdT[:, hc, 0:512],
                                        start=(hc == 0), stop=(hc == 23),
                                        skip_group_check=True), "T", 1)
                        b.inc(t_.matmul(down_ps[:, 1024 * rs + 512:1024 * (rs + 1)],
                                        lhsT, wdT[:, hc, 512:1024],
                                        start=(hc == 0), stop=(hc == 23),
                                        skip_group_check=True), "T", 1)
                down_done[(g, hc)] = b.tok("T")

            for hc in range(24):
                hsl = hc % 2

                @blk.tensor
                def _(t_, g=g, gsl=gsl, hc=hc, hsl=hsl):
                    b.wait(t_, "T", wg_tok)
                    b.wait(t_, "T", dload_done[g])
                    b.wait(t_, "T", silu_done.get((g, hc - 2) if hc >= 2 else (g - 1, 22 + hc)))
                    for dc in range(8):
                        b.inc(t_.matmul(gate_ps[hsl], wgT[:, hc, 128 * dc:128 * (dc + 1)],
                                        y1fm_sb[:, gsl, dc, :],
                                        start=(dc == 0), stop=(dc == 7)), "T", 1)
                gate_done[(g, hc)] = b.tok("T")

                @blk.tensor
                def _(t_, g=g, gsl=gsl, hc=hc, hsl=hsl):
                    b.wait(t_, "T", wu_tok[hc])
                    b.wait(t_, "T", mul_done.get((g, hc - 2) if hc >= 2 else (g - 1, 22 + hc)))
                    for dc in range(8):
                        b.inc(t_.matmul(up_ps[hsl], wuT[:, hc, 128 * dc:128 * (dc + 1)],
                                        y1fm_sb[:, gsl, dc, :],
                                        start=(dc == 0), stop=(dc == 7)), "T", 1)
                up_done[(g, hc)] = b.tok("T")

                silu_done[(g, hc)] = ("A", b2base[0] + (
                    1 + hc if hc < 5 or g == 0 else
                    3 + hc if hc < 6 else
                    7 + hc if hc < 7 else
                    9 + hc if hc < 8 else
                    13 + hc if hc < 9 else
                    15 + hc if hc < 11 else
                    16 + hc if hc < 13 else
                    17 + hc))

                @blk.vector
                def _(v, g=g, hc=hc, hsl=hsl):
                    b.wait(v, "V", up_done[(g, hc)])
                    b.wait(v, "V", silu_done[(g, hc)])
                    b.wait(v, "V", down_done.get((g, hc - 2) if hc >= 2 else (g - 1, 22 + hc)))
                    b.wait(v, "V", b.tok("V"))
                    b.inc(v.tensor_tensor(h_bf[:, hsl, :], up_ps[hsl], sg_bf[:, hsl, :],
                                          ALU.mult), "V", 1)
                mul_done[(g, hc)] = b.tok("V")

                if hc == 9 and pending_vln2[0] is not None:
                    pending_vln2[0](0)
                if hc == 10 and pending_vln2[0] is not None:
                    pending_vln2[0](1)
                    pending_vln2[0] = None
                if hc >= 1:
                    emit_down(hc - 1)
                if g == 0 and hc + 2 <= 23:
                    emit_wu_chunk(hc + 2)
                    if hc >= 1:
                        emit_wd_chunk(hc + 2)
            emit_down(23)

            for rs in range(2):
                @blk.vector
                def _(v, g=g, gsl=gsl, rs=rs):
                    b.wait(v, "V", down_done[(g, 23)])
                    b.wait(v, "V", rload_done[g])
                    b.wait(v, "V", apply2_done.get((g - 1, rs)))
                    b.wait(v, "V", b.tok("V"))
                    b.inc(v.tensor_tensor(y2pre[:, rs, :], down_ps[:, 1024 * rs:1024 * (rs + 1)],
                                          y1r32[:, gsl, rs, :], ALU.add), "V", 1)
                resid2_done[(g, rs)] = b.tok("V")

            for rs in range(2):
                apply2_done[(g, rs)] = ("A", nb2(g) + (8 + 8 * rs if g == NGg - 1 else 26 + 3 * rs))

            if g + 2 < NGg:
                emit_gloads(g + 2)

            def emit_vln2(rs, g=g, s=s, r0=r0):
                rstd2_done = ("A", nb2(g) + (7 + 8 * rs if g == NGg - 1 else 22 + rs))

                @blk.vector
                def _(v, g=g, rs=rs, rstd2_done=rstd2_done):
                    b.wait(v, "V", rstd2_done)
                    b.wait(v, "V", b.tok("V"))
                    b.inc(v.reciprocal(l2rstd[:, rs, :], l2ln[:, rs, :]), "V", 1)
                    b.wait(v, "V", b.tok("V"))
                    b.inc(v.scalar_tensor_tensor(l2mb[:, rs, :], l2m[:, rs, :],
                                                 -1.0, l2rstd[:, rs, :],
                                                 ALU.mult, ALU.mult), "V", 1)
                mb2_toks[(g, rs)] = b.tok("V")

                @blk.gpsimd
                def _(g_, g=g, rs=rs, s=s, r0=r0):
                    b.wait(g_, "G", apply2_done[(g, rs)])
                    b.inc(g_.dma_start(out_h.ap()[s, r0 + 128 * rs:r0 + 128 * (rs + 1), :],
                                       y2[:, g % 2, rs, :]), "st%d" % (g % 2), 16)
                store_done[(g, rs)] = b.tok("st%d" % (g % 2))

            pending_vln2[0] = emit_vln2

            @blk.scalar
            def _(a, g=g, b2base=b2base, prev_ln2=pending_ln2[0]):
                assert b.cnt["A"] == b2base[0], (b.cnt["A"], b2base[0])
                for hc in range(24):
                    hsl = hc % 2
                    b.wait(a, "A", gate_done[(g, hc)])
                    b.wait(a, "A", mul_done.get((g, hc - 2) if hc >= 2 else (g - 1, 22 + hc)))
                    b.inc(a.activation(sg_bf[:, hsl, :], gate_ps[hsl], AF.Silu), "A", 1)
                    if prev_ln2 is None:
                        continue
                    gp = prev_ln2
                    if hc in (4, 6):
                        rs = 0 if hc == 4 else 1
                        b.wait(a, "A", resid2_done[(gp, rs)])
                        b.wait(a, "A", b.tok("A"))
                        b.inc(a.activation(y2sq[:, :], y2pre[:, rs, :], AF.Identity,
                                           accum_out=s2s[:, rs, :]), "A", 1)
                        b.wait(a, "A", b.tok("A"))
                        b.inc(a.activation(y2sq[:, :], y2pre[:, rs, :], AF.Square,
                                           accum_out=s2q[:, rs, :]), "A", 1)
                    if hc in (5, 7):
                        rs = 0 if hc == 5 else 1
                        b.wait(a, "A", b.tok("A"))
                        b.inc(a.activation(l2m[:, rs, :], s2s[:, rs, :], AF.Identity,
                                           scale=1.0 / D), "A", 1)
                        b.wait(a, "A", b.tok("A"))
                        b.inc(a.activation(l2msq[:, rs, :], l2m[:, rs, :], AF.Square), "A", 1)
                        b.inc(a.activation(l2vpe[:, rs, :], s2q[:, rs, :], AF.Identity,
                                           scale=1.0 / D), "A", 1)
                        b.wait(a, "A", b.tok("A"))
                        b.inc(a.activation(l2vpe[:, rs, :], l2msq[:, rs, :], AF.Identity,
                                           scale=-1.0, bias=l2vpe[:, rs, :]), "A", 1)
                    if hc == 8:
                        for rs in range(2):
                            b.wait(a, "A", b.tok("A"))
                            b.inc(a.activation(l2ln[:, rs, :], l2vpe[:, rs, :], AF.Sqrt,
                                               bias=lneps[:, :]), "A", 1)
                    if hc in (10, 12):
                        rs = 0 if hc == 10 else 1
                        b.wait(a, "A", mb2_toks[(gp, rs)])
                        b.wait(a, "A", store_done.get((gp - 2, 1)))
                        b.inc(a.activation(y2[:, gp % 2, rs, :], y2pre[:, rs, :], AF.Identity,
                                           bias=l2mb[:, rs, :], scale=l2rstd[:, rs, :]),
                              "A", 1)
            pending_ln2[0] = g

        if pending_vln2[0] is not None:
            pending_vln2[0](0)
            pending_vln2[0](1)
            pending_vln2[0] = None

        @blk.scalar
        def _(a, prev_ln2=pending_ln2[0]):
            gp = prev_ln2
            for rs in range(2):
                b.wait(a, "A", resid2_done[(gp, rs)])
                b.wait(a, "A", b.tok("A"))
                b.inc(a.activation(y2sq[:, :], y2pre[:, rs, :], AF.Identity,
                                   accum_out=s2s[:, rs, :]), "A", 1)
                b.wait(a, "A", b.tok("A"))
                b.inc(a.activation(y2sq[:, :], y2pre[:, rs, :], AF.Square,
                                   accum_out=s2q[:, rs, :]), "A", 1)
                b.wait(a, "A", b.tok("A"))
                b.inc(a.activation(l2m[:, rs, :], s2s[:, rs, :], AF.Identity,
                                   scale=1.0 / D), "A", 1)
                b.wait(a, "A", b.tok("A"))
                b.inc(a.activation(l2msq[:, rs, :], l2m[:, rs, :], AF.Square), "A", 1)
                b.inc(a.activation(l2vpe[:, rs, :], s2q[:, rs, :], AF.Identity,
                                   scale=1.0 / D), "A", 1)
                b.wait(a, "A", b.tok("A"))
                b.inc(a.activation(l2vpe[:, rs, :], l2msq[:, rs, :], AF.Identity,
                                   scale=-1.0, bias=l2vpe[:, rs, :]), "A", 1)
                b.wait(a, "A", b.tok("A"))
                b.inc(a.activation(l2ln[:, rs, :], l2vpe[:, rs, :], AF.Sqrt,
                                   bias=lneps[:, :]), "A", 1)
                b.wait(a, "A", mb2_toks[(gp, rs)])
                b.wait(a, "A", store_done.get((gp - 2, 1)))
                b.inc(a.activation(y2[:, gp % 2, rs, :], y2pre[:, rs, :], AF.Identity,
                                   bias=l2mb[:, rs, :], scale=l2rstd[:, rs, :]), "A", 1)

        @blk.gpsimd
        def _(g_):
            g_.wait_ge(b.sems["st0"], b.cnt["st0"])
            g_.wait_ge(b.sems["st1"], b.cnt["st1"])

    es_b.close()
    es_outer.close()
    return nc


_IDENT = np.eye(128, dtype=ml_dtypes.bfloat16)


def kernel(L0_attn_x, L0_attn_W_qkv, L0_attn_b_qkv, L0_attn_W_out, L0_attn_b_out,
           L0_ffn_w_gate, L0_ffn_w_up, L0_ffn_w_down):
    x = np.ascontiguousarray(np.asarray(L0_attn_x, dtype=np.float32))
    wqkv = np.ascontiguousarray(np.asarray(L0_attn_W_qkv, dtype=np.float32))
    bqkv = np.ascontiguousarray(np.asarray(L0_attn_b_qkv, dtype=np.float32)).reshape(1, 3 * D)
    wout = np.ascontiguousarray(np.asarray(L0_attn_W_out, dtype=np.float32))
    bout = np.ascontiguousarray(np.asarray(L0_attn_b_out, dtype=np.float32)).reshape(1, D)
    wg = np.ascontiguousarray(np.asarray(L0_ffn_w_gate, dtype=np.int32))
    wu = np.ascontiguousarray(np.asarray(L0_ffn_w_up, dtype=np.int32))
    wd = np.ascontiguousarray(np.asarray(L0_ffn_w_down, dtype=np.int32))

    nc = build_program()
    in_maps = []
    for i in range(NCORES):
        in_maps.append({
            "x": x[SEQS * i:SEQS * (i + 1)],
            "wqkv": wqkv, "bqkv": bqkv, "wout": wout, "bout": bout,
            "wg": wg, "wu": wu, "wd": wd, "ident": _IDENT,
        })
    res = run_bass_kernel_spmd(nc, in_maps, core_ids=list(range(NCORES)))
    return np.concatenate([r["out"] for r in res.results], axis=0)


# version 32
# speedup vs baseline: 1.4175x; 1.0002x over previous
import contextlib

import numpy as np
import ml_dtypes
import concourse.bass as bass
import concourse.mybir as mybir
from concourse.bass_utils import run_bass_kernel_spmd

F32 = mybir.dt.float32
BF16 = mybir.dt.bfloat16
I32 = mybir.dt.int32
AF = mybir.ActivationFunctionType
ALU = mybir.AluOpType
AX = mybir.AxisListType

NCORES = 8
SEQS = 2
L = 4096
D = 1024
HID = 3072
TB = 17
NW = 16
RTS = 32
INV127 = float(np.float32(1.0 / 127.0))
EPS = 1e-5


class Bld:
    def __init__(self, nc, es):
        self.nc = nc
        self.sems = {}
        self.cnt = {}
        self.waited = {}
        self.es = es

    def sem(self, name):
        self.sems[name] = self.es.enter_context(self.nc.semaphore(name))
        self.cnt[name] = 0

    def inc(self, instr, name, by=1):
        instr.then_inc(self.sems[name], by)
        self.cnt[name] += by
        return (name, self.cnt[name])

    def tok(self, name):
        return (name, self.cnt[name])

    def wait(self, eng, engkey, tok):
        if tok is None:
            return
        name, val = tok
        if val <= 0:
            return
        key = (engkey, name)
        if self.waited.get(key, 0) >= val:
            return
        self.waited[key] = val
        eng.wait_ge(self.sems[name], val)


def build_program(debug=False, seqs=SEQS, l=L):
    TB = l // 256 + 1
    NW = l // 256
    RTS = l // 128
    SEQSv, Lv = seqs, l

    def lastwin(gb):
        if gb is None or gb < 0:
            return None
        s, t = gb // TB, gb % TB
        return s * NW + min(t, NW - 1)

    nc = bass.Bass("TRN2", target_bir_lowering=False)

    x_h = nc.dram_tensor("x", [SEQSv, Lv, D], F32, kind="ExternalInput")
    wqkv_h = nc.dram_tensor("wqkv", [D, 3 * D], F32, kind="ExternalInput")
    bqkv_h = nc.dram_tensor("bqkv", [1, 3 * D], F32, kind="ExternalInput")
    wout_h = nc.dram_tensor("wout", [D, D], F32, kind="ExternalInput")
    bout_h = nc.dram_tensor("bout", [1, D], F32, kind="ExternalInput")
    wg_h = nc.dram_tensor("wg", [HID, D], I32, kind="ExternalInput")
    wu_h = nc.dram_tensor("wu", [HID, D], I32, kind="ExternalInput")
    wd_h = nc.dram_tensor("wd", [D, HID], I32, kind="ExternalInput")
    ident_h = nc.dram_tensor("ident", [128, 128], BF16, kind="ExternalInput")
    out_h = nc.dram_tensor("out", [SEQSv, Lv, D], F32, kind="ExternalOutput")
    ykind = "ExternalOutput" if debug else "Internal"
    y1f_h = nc.dram_tensor("y1dbg", [SEQSv, Lv, D], F32, kind=ykind)
    y1fm_h = nc.dram_tensor("y1fm_scr", [SEQSv, Lv // 256, 128, 8, 256], BF16,
                            kind="Internal")
    wgT_h = nc.dram_tensor("wgT_scr", [24, 128, D], BF16, kind="Internal")
    wuT_h = nc.dram_tensor("wuT_scr", [24, 128, D], BF16, kind="Internal")
    wdT_h = nc.dram_tensor("wdT_scr", [24, 128, D], BF16, kind="Internal")

    es_outer = contextlib.ExitStack()
    b = Bld(nc, es_outer)
    for s in ["T", "V", "A", "G",
              "xq0", "xq1", "xq2", "xq3", "xr0", "xr1",
              "wl0", "wl1", "wl2", "wl3", "wl4", "wg0",
              "tx0", "tx1", "tv0", "tv1", "ty0", "ty1", "tw0", "tw1",
              "wc0", "wc1", "wc2", "wc3", "wi0", "pk0", "pk1", "pf0", "pf1", "pw0", "pw1",
              "yf0", "yf1", "yr0", "yr1", "st0", "st1"]:
        b.sem(s)

    SO = lambda n, sh, dt, **kw: es_outer.enter_context(nc.sbuf_tensor(n, sh, dt, **kw))
    lneps = SO("lneps", [128, 1], F32)
    PS = lambda n, sh, dt: es_outer.enter_context(nc.psum_tensor(n, sh, dt))
    psum_qkv = PS("psum_qkv", [128, D], F32)
    psum_S = PS("psum_S", [128, 512], F32)
    psum_TR = PS("psum_TR", [128, 1024], BF16)
    psum_O = PS("psum_O", [128, 2048], F32)

    es_a = contextlib.ExitStack()
    SA = lambda n, sh, dt, **kw: es_a.enter_context(nc.sbuf_tensor(n, sh, dt, **kw))

    es_qkv = contextlib.ExitStack()
    wqkv_bf = es_qkv.enter_context(
        nc.sbuf_tensor("wqkv_bf", [128, 8, 3 * D], BF16, side="right"))
    wgT_holder = [None]
    wout_bf = SA("wout_bf", [128, 8, D], BF16)
    bq_rep = SA("bq_rep", [128, 3 * D], BF16)
    bo_rep = SA("bo_rep", [128, D], BF16)
    ident_sb = SA("ident_sb", [128, 128], BF16)
    ones128 = SA("ones128", [128, 128], BF16)
    ones1 = SA("ones1", [1, 128], BF16)

    xt_bf = SA("xt_bf", [128, 3, D], BF16)
    x_fm = SA("x_fm", [128, 2, 8, 128], BF16)
    q_tm = SA("q_tm", [128, 3, D], BF16)
    k_tm = SA("k_tm", [128, 3, D], BF16)
    v_tmp = SA("v_tmp", [128, 3, D], BF16)
    v_fm = SA("v_fm", [128, 3, 8, 128], BF16)
    attn_out = SA("attn_out", [128, 8, 512], BF16)
    negmax = SA("negmax", [128, 8], F32)
    ssub = SA("ssub", [128, 512], F32)
    aexp = SA("aexp", [128, 512], F32)
    asum = SA("asum", [128, 8], F32)
    arecip = SA("arecip", [128, 8], F32)
    a_bf = SA("a_bf", [128, 512], BF16)
    a_bd = SA("a_bd", [128, 2, 8, 128], BF16)
    at_bf = SA("at_bf", [128, 1024], BF16)
    xr32 = SA("xr32", [128, 2, D], F32)
    y1pre = SA("y1pre", [128, 2, D], F32)
    y1sq = SA("y1sq", [128, D], F32)
    sums = SA("sums", [128, 2, 1], F32)
    sumsq = SA("sumsq", [128, 2, 1], F32)
    lnm = SA("lnm", [128, 2, 1], F32)
    lnmsq = SA("lnmsq", [128, 2, 1], F32)
    lnvpe = SA("lnvpe", [128, 2, 1], F32)
    lnrstd = SA("lnrstd", [128, 2, 1], F32)
    lnln = SA("lnln", [128, 2, 1], F32)
    lnmb = SA("lnmb", [128, 2, 1], F32)
    y1_f32 = SA("y1_f32", [128, 2, D], F32)
    y1_bf = SA("y1_bf", [128, 2, D], BF16)
    y1fm_stage = SA("y1fm_stage", [128, 4, 8, 256], BF16)

    es_w = contextlib.ExitStack()
    SW = lambda n, sh, dt, **kw: es_w.enter_context(nc.sbuf_tensor(n, sh, dt, **kw))
    w_in_bf = SW("w_in_bf", [128, 3, HID], BF16)
    wT_stage = SW("wT_stage", [128, 8, 128], BF16)
    bq_bf = SW("bq_bf", [1, 3 * D], BF16)
    bo_bf = SW("bo_bf", [1, D], BF16)

    with nc.Block() as blk:
        @blk.gpsimd
        def _(g):
            b.inc(g.memset(ones128[:, :], 1.0), "G", 1)
            b.inc(g.memset(ones1[:, :], 1.0), "G", 1)
            b.inc(g.memset(a_bd[:, :, :, :], 0.0), "G", 1)
            b.inc(g.memset(lneps[:, :], float(EPS)), "G", 1)
        memset_tok = b.tok("G")

        @blk.sync
        def _(q):
            b.inc(q.dma_start(ident_sb[:, :], ident_h.ap()), "wl0", 16)

        @blk.gpsimd
        def _(g):
            for c in range(8):
                b.inc(g.dma_start(wqkv_bf[:, c, :], wqkv_h.ap()[128 * c:128 * (c + 1), :]),
                      "wi0", 16)
                b.inc(g.dma_start(wout_bf[:, c, :], wout_h.ap()[128 * c:128 * (c + 1), :]),
                      "wi0", 16)
            b.inc(g.dma_start(bq_bf[:, :], bqkv_h.ap()), "wi0", 16)
            b.inc(g.dma_start(bo_bf[:, :], bout_h.ap()), "wi0", 16)
        wcast_tok = b.tok("wi0")
        prev_cast = wcast_tok

        @blk.vector
        def _(v):
            b.wait(v, "V", wcast_tok)
            b.inc(v.tensor_scalar_mul(bq_bf[:, :], bq_bf[:, :], 1.0 / 128.0), "V", 1)
            b.inc(v.tensor_scalar_mul(bo_bf[:, :], bo_bf[:, :], 1.0 / 128.0), "V", 1)
        bias_bf_tok = b.tok("V")

        brep_tok = None
        for g3 in range(4):
            @blk.tensor
            def _(t_, g3=g3, brep_tok=brep_tok):
                b.wait(t_, "T", bias_bf_tok)
                b.wait(t_, "T", memset_tok)
                b.wait(t_, "T", brep_tok)
                if g3 < 3:
                    r0 = bq_bf[0:1, 1024 * g3:1024 * g3 + 512]
                    r1 = bq_bf[0:1, 1024 * g3 + 512:1024 * (g3 + 1)]
                else:
                    r0, r1 = bo_bf[0:1, 0:512], bo_bf[0:1, 512:1024]
                b.inc(t_.matmul(psum_qkv[:, 0:512], ones1[:, :], r0, start=True, stop=True), "T", 1)
                b.inc(t_.matmul(psum_qkv[:, 512:1024], ones1[:, :], r1, start=True, stop=True), "T", 1)
            mm_tok = b.tok("T")

            @blk.vector
            def _(v, g3=g3, mm_tok=mm_tok):
                b.wait(v, "V", mm_tok)
                dst = bq_rep[:, 1024 * g3:1024 * (g3 + 1)] if g3 < 3 else bo_rep[:, :]
                b.inc(v.tensor_copy(dst, psum_qkv[:, :]), "V", 1)
            brep_tok = b.tok("V")

        @blk.sync
        def _(q):
            b.wait(q, "Q", prev_cast)
            b.wait(q, "Q", brep_tok)

    cast_done, tx_done, qkv_mm_done, evac_done, tv_done = {}, {}, {}, {}, {}
    evac_half, qkv_half = {}, {}
    xfm_done, tvmm_done, tymm_done = {}, {}, {}
    tr_banks = [[0, None], [1, None], [2, None]]
    tr_idx = [0]
    qkv_free = [None]
    tr_noq = [False]

    def claim_tr():
        bk = tr_banks[tr_idx[0]]
        tr_idx[0] = (tr_idx[0] + 1) % 3
        if tr_noq[0] and bk[0] == 2:
            bk = tr_banks[tr_idx[0]]
            tr_idx[0] = (tr_idx[0] + 1) % 3
        extra = qkv_free[0] if bk[0] == 2 else None
        return bk, extra

    def tr_slice(bi, c0, c1):
        if bi == 0:
            return psum_TR[:, c0:c1]
        if bi == 1:
            return psum_S[:, c0 // 2:c1 // 2].bitcast(BF16)
        return psum_qkv[:, c0 // 2:c1 // 2].bitcast(BF16)
    s_mm_done, ssub_done, exp_done, scale_done, bd_done = {}, {}, {}, {}, {}
    at_mm_done, atevac_done, o_mm_done, oevac_done = {}, {}, {}, {}
    proj_done, resid_done, sq_done, apply_done = {}, {}, {}, {}
    ty_done, pk_done, pf_done, xq_done, xr_done = {}, {}, {}, {}, {}
    last_resid = None

    a_join_T = [wcast_tok, memset_tok, brep_tok, ("wl0", 16)]

    with nc.Block() as blk:
        @blk.tensor
        def _(t_):
            for tk in a_join_T:
                b.wait(t_, "T", tk)

        def emit_xload(gt2):
            if gt2 >= SEQSv * TB:
                return
            s2, t2 = gt2 // TB, gt2 % TB
            xsl = gt2 % 3

            @blk.gpsimd
            def _(g, s2=s2, t2=t2, gt2=gt2, xsl=xsl):
                b.wait(g, "G", tx_done.get(gt2 - 3))
                b.inc(g.dma_start(xt_bf[:, xsl, :], x_h.ap()[s2, 128 * t2:128 * (t2 + 1), :]),
                      "xq%d" % xsl, 16)
            cast_done[gt2] = b.tok("xq%d" % xsl)
            xq_done[gt2] = cast_done[gt2]

        def emit_xtr(gt2):
            if gt2 >= SEQSv * TB:
                return
            sl2 = gt2 % 2
            xsl = gt2 % 3
            bk, qtok = claim_tr()
            bi, freetok = bk[0], bk[1]

            @blk.tensor
            def _(t_, gt2=gt2, xsl=xsl, bi=bi, freetok=freetok, qtok=qtok):
                b.wait(t_, "T", cast_done[gt2])
                b.wait(t_, "T", freetok)
                b.wait(t_, "T", qtok)
                for c in range(8):
                    b.inc(t_.transpose(tr_slice(bi, 128 * c, 128 * (c + 1)),
                                       xt_bf[:, xsl, 128 * c:128 * (c + 1)],
                                       ident_sb[:, :]), "T", 1)
            tx_done[gt2] = b.tok("T")

            @blk.vector
            def _(v, gt2=gt2, sl2=sl2, bi=bi):
                b.wait(v, "V", tx_done[gt2])
                b.wait(v, "V", qkv_mm_done.get(gt2 - 2))
                b.inc(v.tensor_copy(x_fm[:, sl2, :, :].rearrange("p a b -> p (a b)"),
                                    tr_slice(bi, 0, 1024)), "V", 1)
            xfm_done[gt2] = b.tok("V")
            bk[1] = xfm_done[gt2]

        pending_typf = [None]

        pf_idx = [0]
        pf_by_idx = {}

        def emit_ty_one(gw, s, n, k2):
            wsl = n % 4
            if True:
                grt = s * RTS + 2 * n + k2
                rsl = grt % 2
                bk, qtok = claim_tr()
                bi, freetok = bk[0], bk[1]

                @blk.tensor
                def _(t_, grt=grt, rsl=rsl, bi=bi, freetok=freetok, qtok=qtok):
                    b.wait(t_, "T", apply_done[grt])
                    b.wait(t_, "T", freetok)
                    b.wait(t_, "T", qtok)
                    for c in range(8):
                        b.inc(t_.transpose(tr_slice(bi, 128 * c, 128 * (c + 1)),
                                           y1_bf[:, rsl, 128 * c:128 * (c + 1)],
                                           ident_sb[:, :]), "T", 1)
                tymm_done[grt] = b.tok("T")

                pp1 = pf_by_idx.get(pf_idx[0] - 1)
                pp2 = pf_by_idx.get(pf_idx[0] - 2)

                @blk.vector
                def _(v, grt=grt, rsl=rsl, k2=k2, wsl=wsl, bi=bi, pp1=pp1, pp2=pp2):
                    b.wait(v, "V", tymm_done[grt])
                    b.wait(v, "V", pp1)
                    b.wait(v, "V", pp2)
                    dst = y1fm_stage[:, wsl, :, 128 * k2:128 * (k2 + 1)]
                    b.inc(v.tensor_copy(dst, tr_slice(bi, 0, 1024)
                                        .rearrange("p (a b) -> p a b", a=8)),
                          "V", 1)
                ty_done[grt] = b.tok("V")
                bk[1] = ty_done[grt]

        def emit_pf(gw, s, n):
            wsl = n % 4
            nw0 = n - wsl
            pi = pf_idx[0]
            pf_idx[0] += 1
            prev2 = pf_by_idx.get(pi - 2)

            @blk.gpsimd
            def _(g, s=s, n=n, gw=gw, wsl=wsl, nw0=nw0, pi=pi, prev2=prev2):
                b.wait(g, "G", prev2)
                b.wait(g, "G", ty_done[s * RTS + 2 * n])
                b.wait(g, "G", ty_done[s * RTS + 2 * n + 1])
                src_ap = y1fm_stage[:, 0:wsl + 1, :, :].rearrange(
                    "p w a b -> p w (a b)")
                b.inc(g.dma_start(
                    y1fm_h.ap()[s, nw0:n + 1].rearrange(
                        "w p a b -> p w (a b)"),
                    src_ap), "pf%d" % (pi % 2), 16)
            pf_done[gw] = b.tok("pf%d" % (pi % 2))
            pf_by_idx[pi] = pf_done[gw]

        def flush_typf(which=2):
            if pending_typf[0] is not None:
                gw0, s0, n0 = pending_typf[0]
                if which >= 1 and pending_typf[0] not in ty_started:
                    ty_started.add(pending_typf[0])
                    emit_ty_one(gw0, s0, n0, 0)
                if which == 2:
                    emit_ty_one(gw0, s0, n0, 1)
                    if (n0 % 4 == 3) or (n0 == NW - 1):
                        emit_pf(gw0, s0, n0)
                    pending_typf[0] = None
        ty_started = set()

        wprep_state = {"stage_free": None}
        deq_prev = {}
        tw_prev = {}

        def prep_load(i):
            load_src, n_el, _batches = wprep_jobs[i]
            sl = i % 3
            sp = i % 2

            @blk.gpsimd
            def _(g, i=i, sl=sl, load_src=load_src, n_el=n_el):
                b.wait(g, "G", tw_prev.get(i - 3))
                b.inc(g.dma_start(w_in_bf[:, sl, 0:n_el], load_src), "wc%d" % sl, 16)
            wc_tok = b.tok("wc%d" % sl)

            deq_prev[i] = wc_tok

        def prep_transpose(i):
            _src, n_el, sub_transposes = wprep_jobs[i]
            sl = i % 3
            sp = i % 2

            @blk.vector
            def _(v, i=i, sl=sl, n_el=n_el):
                b.wait(v, "V", deq_prev[i])
                b.inc(v.tensor_scalar_mul(w_in_bf[:, sl, 0:n_el], w_in_bf[:, sl, 0:n_el],
                                          INV127), "V", 1)
            dq_tok = b.tok("V")
            for batch in sub_transposes:
                bk, qtok = claim_tr()
                bi, tr_free = bk[0], bk[1]

                @blk.tensor
                def _(t_, i=i, sl=sl, batch=batch, bi=bi, tr_free=tr_free,
                      qtok=qtok, dq_tok=dq_tok):
                    b.wait(t_, "T", dq_tok)
                    b.wait(t_, "T", tr_free)
                    b.wait(t_, "T", qtok)
                    for j, (src_sl, _dst, _comb) in enumerate(batch):
                        b.inc(t_.transpose(tr_slice(bi, 128 * j, 128 * (j + 1)),
                                           w_in_bf[:, sl, src_sl[0]:src_sl[1]],
                                           ident_sb[:, :]), "T", 1)
                tw_tok = b.tok("T")
                tw_prev[i] = tw_tok

                @blk.vector
                def _(v, tw_tok=tw_tok, bi=bi, sf=wprep_state["stage_free"]):
                    b.wait(v, "V", tw_tok)
                    b.wait(v, "V", sf)
                    b.inc(v.tensor_copy(wT_stage[:, :, :].rearrange("p a b -> p (a b)"),
                                        tr_slice(bi, 0, 1024)), "V", 1)
                ev_tok = b.tok("V")
                bk[1] = ev_tok

                @blk.gpsimd
                def _(g, batch=batch, ev_tok=ev_tok, sl=sl):
                    b.wait(g, "G", ev_tok)
                    comb, park_tr = batch[0][2], batch[0][1]
                    if park_tr:
                        b.inc(g.dma_start(comb.transpose([1, 0, 2]), wT_stage[:, :, :]),
                              "pw%d" % sp, 16)
                    else:
                        b.inc(g.dma_start(comb,
                                          wT_stage[:, :, :].rearrange("p a b -> p (a b)")),
                              "pw%d" % sp, 16)
                wprep_state["stage_free"] = b.tok("pw%d" % sp)

        wprep_jobs = []
        for src_h, dst_h in [(wg_h, wgT_h), (wu_h, wuT_h)]:
            for hc in range(24):
                comb = dst_h.ap()[hc]
                batch = [((128 * dc, 128 * (dc + 1)), False, comb) for dc in range(8)]
                wprep_jobs.append((src_h.ap()[128 * hc:128 * (hc + 1), :], D, [batch]))
        for dc in range(8):
            batches = []
            for hb in range(3):
                comb = wdT_h.ap()[8 * hb:8 * (hb + 1), :, 128 * dc:128 * (dc + 1)]
                batches.append([((128 * (8 * hb + j), 128 * (8 * hb + j + 1)), True, comb)
                                for j in range(8)])
            wprep_jobs.append((wd_h.ap()[128 * dc:128 * (dc + 1), :], HID, batches))
        wprep_ld = [0]
        wprep_tr = [0]

        def prep_loads(n_load):
            k = 0
            while wprep_ld[0] < len(wprep_jobs) and k < n_load:
                i = wprep_ld[0]
                if i >= 3 and (i - 3) not in tw_prev:
                    break
                prep_load(i)
                wprep_ld[0] += 1
                k += 1

        def prep_trs(n_tr):
            k = 0
            while wprep_tr[0] < wprep_ld[0] and k < n_tr:
                if wprep_tr[0] + 2 > wprep_ld[0]:
                    break
                prep_transpose(wprep_tr[0])
                wprep_tr[0] += 1
                k += 1

        def emit_wg_stream(gt_last):
            es_qkv.close()
            wgT_holder[0] = es_outer.enter_context(
                nc.sbuf_tensor("wgT", [128, 24, D], BF16, side="right"))
            pw_fin = [b.tok("pw0"), b.tok("pw1")]

            @blk.sync
            def _(q, gt_last=gt_last, pw_fin=pw_fin):
                b.wait(q, "Q", qkv_mm_done[(gt_last, 2)])
                for tk in pw_fin:
                    b.wait(q, "Q", tk)
                b.inc(q.dma_start(wgT_holder[0][:, :, :],
                                  wgT_h.ap().transpose([1, 0, 2])), "wg0", 16)

        emit_xload(0)
        emit_xload(1)
        emit_xload(2)
        emit_xtr(0)
        emit_xtr(1)

        for s in range(SEQSv):
            for t in range(TB + 1):
                gt = s * TB + t
                sl = gt % 2
                sl3 = gt % 3
                have_qkv = t < TB
                have_w1 = 1 <= t <= NW
                have_w2 = t >= 2
                abase = [b.cnt["A"]]
                off_exp = 1
                off_oe = 2 if have_w1 else 1

                prep_loads(2)
                if have_qkv:
                    emit_xload(gt + 3)

                def emit_qkv_pass(p, gt=gt, sl=sl):
                    prev = (gt, p - 1) if p > 0 else (gt - 1, 2)
                    qsfree = tr_banks[2][1] if p == 0 else None

                    @blk.tensor
                    def _(t_, gt=gt, sl=sl, p=p, prev=prev, qsfree=qsfree):
                        b.wait(t_, "T", xfm_done[gt])
                        b.wait(t_, "T", evac_half.get((prev, 0)))
                        b.wait(t_, "T", qsfree)
                        for c in range(8):
                            lhsT = x_fm[:, sl, c, :]
                            b.inc(t_.matmul(psum_qkv[:, 0:512],
                                            lhsT, wqkv_bf[:, c, 1024 * p:1024 * p + 512],
                                            start=(c == 0), stop=False), "T", 1)
                            if c == 0:
                                b.wait(t_, "T", evac_half.get((prev, 1)))
                            b.inc(t_.matmul(psum_qkv[:, 512:1024],
                                            lhsT, wqkv_bf[:, c, 1024 * p + 512:1024 * (p + 1)],
                                            start=(c == 0), stop=False), "T", 1)
                        b.inc(t_.matmul(psum_qkv[:, 0:512], ones128[:, :],
                                        bq_rep[:, 1024 * p:1024 * p + 512],
                                        start=False, stop=True), "T", 1)
                        qkv_half[(gt, p, 0)] = b.tok("T")
                        b.inc(t_.matmul(psum_qkv[:, 512:1024], ones128[:, :],
                                        bq_rep[:, 1024 * p + 512:1024 * (p + 1)],
                                        start=False, stop=True), "T", 1)
                    qkv_mm_done[(gt, p)] = b.tok("T")
                    qkv_half[(gt, p, 1)] = qkv_mm_done[(gt, p)]
                    if p == 2:
                        qkv_mm_done[gt] = b.tok("T")

                    @blk.vector
                    def _(v, gt=gt, sl3=sl3, p=p):
                        b.wait(v, "V", qkv_half[(gt, p, 0)])
                        lw = lastwin(gt - 3)
                        if p == 0:
                            b.wait(v, "V", s_mm_done.get(lw))
                            dst = q_tm
                        elif p == 1:
                            b.wait(v, "V", s_mm_done.get(lw))
                            dst = k_tm
                        else:
                            b.wait(v, "V", tv_done.get(gt - 3))
                            dst = v_tmp
                        b.inc(v.tensor_copy(dst[:, sl3, 0:512], psum_qkv[:, 0:512]), "V", 1)
                        evac_half[((gt, p), 0)] = b.tok("V")
                        b.wait(v, "V", qkv_half[(gt, p, 1)])
                        b.inc(v.tensor_copy(dst[:, sl3, 512:1024], psum_qkv[:, 512:1024]),
                              "V", 1)
                    evac_half[((gt, p), 1)] = b.tok("V")
                    evac_done[(gt, p)] = b.tok("V")

                if have_qkv:
                    emit_qkv_pass(0)
                    emit_qkv_pass(1)

                if have_w1:
                    n1 = t - 1
                    gw1 = s * NW + n1
                    s_sl0 = (s * TB + n1) % 3
                    s_sl1 = (s * TB + n1 + 1) % 3
                    gt_n1 = s * TB + n1 + 1
                    sfree = tr_banks[1][1]
                    exp_done[gw1] = ("A", abase[0] + off_exp)

                    @blk.tensor
                    def _(t_, gw1=gw1, s_sl0=s_sl0, s_sl1=s_sl1, gt_n1=gt_n1, sfree=sfree):
                        b.wait(t_, "T", evac_half[((gt_n1, 1), 0)])
                        b.wait(t_, "T", ssub_done.get(gw1 - 1))
                        b.wait(t_, "T", sfree)
                        for hp in range(8):
                            if hp == 4:
                                b.wait(t_, "T", evac_half[((gt_n1, 1), 1)])
                            for par in range(2):
                                h = 2 * hp + par
                                for j, slj in enumerate((s_sl0, s_sl1)):
                                    b.inc(t_.matmul(
                                        psum_S[64 * par:64 * par + 64, 64 * hp:64 * hp + 64],
                                        q_tm[:, slj, 64 * h:64 * h + 64],
                                        k_tm[:, slj, 64 * h:64 * h + 64],
                                        start=(j == 0), stop=(j == 1),
                                        skip_group_check=True), "T", 1)
                    s_mm_done[gw1] = b.tok("T")

                    @blk.vector
                    def _(v, gw1=gw1):
                        b.wait(v, "V", s_mm_done[gw1])
                        b.wait(v, "V", b.tok("V"))
                        b.inc(v.reduce_max(negmax[:, :],
                                           psum_S[:, :].rearrange("p (s x) -> p s x", s=8),
                                           axis=AX.X, negate=True), "V", 1)
                        b.wait(v, "V", b.tok("V"))
                        b.wait(v, "V", exp_done.get(gw1 - 1))
                        nm = negmax[:, :].unsqueeze(2).broadcast_to([128, 8, 64])
                        b.inc(v.tensor_tensor(ssub[:, :].rearrange("p (s x) -> p s x", s=8),
                                              psum_S[:, :].rearrange("p (s x) -> p s x", s=8),
                                              nm, ALU.add), "V", 1)
                    ssub_done[gw1] = b.tok("V")
                    tr_banks[1][1] = ssub_done[gw1]

                if have_qkv:
                    emit_qkv_pass(2)
                    qkv_free[0] = evac_half[((gt, 2), 1)]
                    emit_xtr(gt + 2)

                if s == SEQSv - 1 and t == TB - 1 and wprep_tr[0] >= 24:
                    emit_wg_stream(gt)

                if have_qkv:
                    vbk, vqtok = claim_tr()
                    vbi, vfreetok = vbk[0], vbk[1]

                    @blk.tensor
                    def _(t_, gt=gt, sl3=sl3, vbi=vbi, vfreetok=vfreetok, vqtok=vqtok):
                        b.wait(t_, "T", evac_half[((gt, 2), 0)])
                        b.wait(t_, "T", vfreetok)
                        b.wait(t_, "T", vqtok)
                        for c in range(8):
                            if c == 4:
                                b.wait(t_, "T", evac_half[((gt, 2), 1)])
                            b.inc(t_.transpose(tr_slice(vbi, 128 * c, 128 * (c + 1)),
                                               v_tmp[:, sl3, 128 * c:128 * (c + 1)],
                                               ident_sb[:, :]), "T", 1)
                    tvmm_done[gt] = b.tok("T")

                    @blk.vector
                    def _(v, gt=gt, sl3=sl3, vbi=vbi):
                        b.wait(v, "V", tvmm_done[gt])
                        b.wait(v, "V", o_mm_done.get(lastwin(gt - 3)))
                        b.inc(v.tensor_copy(v_fm[:, sl3, :, :].rearrange("p a b -> p (a b)"),
                                            tr_slice(vbi, 0, 1024)), "V", 1)
                    tv_done[gt] = b.tok("V")
                    vbk[1] = tv_done[gt]

                if have_w2:
                    n = t - 2
                    gw = s * NW + n
                    sl0 = (s * TB + n) % 3
                    sl1 = (s * TB + n + 1) % 3
                    wpar = gw % 2

                    flush_typf(which=1)

                    @blk.tensor
                    def _(t_, gw=gw, wpar=wpar):
                        b.wait(t_, "T", bd_done[gw])
                        b.wait(t_, "T", oevac_done.get(gw - 1))
                        b.wait(t_, "T", resid_done.get(2 * (gw - 1)))
                        at_dst = psum_O[:, 0:512].bitcast(BF16)
                        for hp in range(8):
                            b.inc(t_.transpose(at_dst[0:64, 128 * hp:128 * (hp + 1)],
                                               a_bd[:, wpar, hp, 0:64], ident_sb[:, :]),
                                  "T", 1)
                            b.inc(t_.transpose(at_dst[64:128, 128 * hp:128 * (hp + 1)],
                                               a_bd[:, wpar, hp, 64:128], ident_sb[:, :]),
                                  "T", 1)
                    at_mm_done[gw] = b.tok("T")

                    @blk.vector
                    def _(v, gw=gw):
                        b.wait(v, "V", at_mm_done[gw])
                        b.wait(v, "V", b.tok("V"))
                        b.wait(v, "V", o_mm_done.get(gw - 1))
                        b.inc(v.tensor_copy(at_bf[:, :], psum_O[:, 0:512].bitcast(BF16)),
                              "V", 1)
                    atevac_done[gw] = b.tok("V")

                    @blk.tensor
                    def _(t_, gw=gw, s=s, n=n, sl0=sl0, sl1=sl1):
                        b.wait(t_, "T", atevac_done[gw])
                        b.wait(t_, "T", tv_done[s * TB + n])
                        b.wait(t_, "T", tv_done[s * TB + n + 1])
                        b.wait(t_, "T", oevac_done.get(gw - 1))
                        b.wait(t_, "T", resid_done.get(2 * (gw - 1) + 1))
                        for hp in range(8):
                            for c4 in range(4):
                                slj = sl0 if c4 < 2 else sl1
                                off = 64 * (c4 % 2)
                                lhsT = v_fm[:, slj, hp, off:off + 64]
                                dst0 = 512 * (hp // 2) + 256 * (hp % 2) + 64 * c4
                                for par in range(2):
                                    b.inc(t_.matmul(
                                        psum_O[64 * par:64 * par + 64, dst0:dst0 + 64],
                                        lhsT,
                                        at_bf[:, 128 * hp + 64 * par:128 * hp + 64 * par + 64],
                                        start=True, stop=True,
                                        skip_group_check=True), "T", 1)
                    o_mm_done[gw] = b.tok("T")

                    flush_typf(which=2)
                    prep_trs(2)

                    oevac_done[gw] = ("A", abase[0] + off_oe + 7)
                    lnstart = abase[0] + (10 if have_w1 else 9)

                    for k2 in range(2):
                        rt = 2 * n + k2
                        grt = s * RTS + rt
                        rsl = grt % 2

                        @blk.sync
                        def _(q, s=s, rt=rt, grt=grt, rsl=rsl):
                            b.wait(q, "Q", resid_done.get(grt - 2))
                            b.inc(q.dma_start(xr32[:, rsl, :],
                                              x_h.ap()[s, 128 * rt:128 * (rt + 1), :]),
                                  "xr%d" % rsl, 16)
                        xr_done[grt] = b.tok("xr%d" % rsl)

                        @blk.tensor
                        def _(t_, gw=gw, n=n, k2=k2, grt=grt, s=s):
                            pj = psum_O[:, 1024 * k2:1024 * (k2 + 1)]
                            rb = 256 * (n % 2) + 128 * k2
                            for c in range(8):
                                lhsT = attn_out[:, c, rb:rb + 128]
                                b.wait(t_, "T", ("A", abase[0] + off_oe + max(c, 1 + 4 * k2)))
                                b.inc(t_.matmul(pj[:, 0:512], lhsT,
                                                wout_bf[:, c, 0:512],
                                                start=(c == 0), stop=False,
                                                skip_group_check=True), "T", 1)
                                b.wait(t_, "T", ("A", abase[0] + off_oe + max(c, 3 + 4 * k2)))
                                b.inc(t_.matmul(pj[:, 512:1024], lhsT,
                                                wout_bf[:, c, 512:1024],
                                                start=(c == 0), stop=False,
                                                skip_group_check=True), "T", 1)
                            b.inc(t_.matmul(pj[:, 0:512], ones128[:, :],
                                            bo_rep[:, 0:512], start=False, stop=True,
                                            skip_group_check=True), "T", 1)
                            b.inc(t_.matmul(pj[:, 512:1024], ones128[:, :],
                                            bo_rep[:, 512:1024], start=False, stop=True,
                                            skip_group_check=True), "T", 1)
                        proj_done[2 * gw + k2] = b.tok("T")

                        @blk.vector
                        def _(v, grt=grt, rsl=rsl, gw=gw, k2=k2):
                            b.wait(v, "V", proj_done[2 * gw + k2])
                            b.wait(v, "V", xr_done[grt])
                            b.wait(v, "V", apply_done.get(grt - 2))
                            b.wait(v, "V", b.tok("V"))
                            b.inc(v.tensor_tensor(y1pre[:, rsl, :],
                                                  psum_O[:, 1024 * k2:1024 * (k2 + 1)],
                                                  xr32[:, rsl, :], ALU.add), "V", 1)
                        resid_done[grt] = b.tok("V")
                        last_resid = resid_done[grt]

                        apply_done[grt] = ("A", lnstart + 11 + 12 * k2)

                        @blk.gpsimd
                        def _(g, s=s, rt=rt, grt=grt, rsl=rsl, lnstart=lnstart):
                            b.wait(g, "G", ("A", lnstart + 10 + 12 * k2))
                            b.inc(g.dma_start(y1f_h.ap()[s, 128 * rt:128 * (rt + 1), :],
                                              y1_f32[:, rsl, :]), "pk%d" % rsl, 16)
                        pk_done[grt] = b.tok("pk%d" % rsl)

                    pending_typf[0] = (gw, s, n)

                if have_w1 or have_w2:
                    w2ctx = (gw, s, n) if have_w2 else None
                    w1gw = gw1 if have_w1 else None

                    @blk.scalar
                    def _(a, w2ctx=w2ctx, w1gw=w1gw, abase=abase):
                        assert b.cnt["A"] == abase[0], (b.cnt["A"], abase[0])
                        if w1gw is not None:
                            b.wait(a, "A", ssub_done[w1gw])
                            b.wait(a, "A", scale_done.get(w1gw - 1))
                            b.inc(a.activation(aexp[:, :], ssub[:, :], AF.Exp, scale=0.125),
                                  "A", 1)
                        if w2ctx is not None:
                            gw, s_, n = w2ctx
                            b.wait(a, "A", o_mm_done[gw])
                            b.wait(a, "A", proj_done.get(2 * (gw - 2) + 1))
                            rb = 256 * (n % 2)
                            for hp in range(8):
                                base = 512 * (hp // 2) + 256 * (hp % 2)
                                b.inc(a.activation(
                                    attn_out[:, hp, rb:rb + 256]
                                    .rearrange("p (d c) -> p d c", c=4),
                                    psum_O[:, base:base + 256]
                                    .rearrange("p (c d) -> p d c", c=4),
                                    AF.Copy), "A", 1)
                        if w2ctx is not None:
                            gw, s_, n = w2ctx
                            for k2 in range(2):
                                grt = s_ * RTS + 2 * n + k2
                                rsl = grt % 2
                                b.wait(a, "A", resid_done[grt])
                                b.wait(a, "A", b.tok("A"))
                                b.inc(a.activation(y1sq[:, :], y1pre[:, rsl, :], AF.Identity,
                                                   accum_out=sums[:, rsl, :]), "A", 1)
                                b.wait(a, "A", b.tok("A"))
                                b.inc(a.activation(y1sq[:, :], y1pre[:, rsl, :], AF.Square,
                                                   accum_out=sumsq[:, rsl, :]), "A", 1)
                                b.wait(a, "A", b.tok("A"))
                                b.inc(a.activation(lnm[:, rsl, :], sums[:, rsl, :], AF.Identity,
                                                   scale=1.0 / D), "A", 1)
                                b.wait(a, "A", b.tok("A"))
                                b.inc(a.activation(lnmsq[:, rsl, :], lnm[:, rsl, :], AF.Square),
                                      "A", 1)
                                b.inc(a.activation(lnvpe[:, rsl, :], sumsq[:, rsl, :],
                                                   AF.Identity, scale=1.0 / D), "A", 1)
                                b.wait(a, "A", b.tok("A"))
                                b.inc(a.activation(lnvpe[:, rsl, :], lnmsq[:, rsl, :],
                                                   AF.Identity,
                                                   scale=-1.0, bias=lnvpe[:, rsl, :]), "A", 1)
                                b.wait(a, "A", b.tok("A"))
                                b.inc(a.activation(lnln[:, rsl, :], lnvpe[:, rsl, :], AF.Ln,
                                                   bias=lneps[:, :]), "A", 1)
                                b.wait(a, "A", b.tok("A"))
                                b.inc(a.activation(lnrstd[:, rsl, :], lnln[:, rsl, :], AF.Exp,
                                                   scale=-0.5), "A", 1)
                                b.wait(a, "A", b.tok("A"))
                                b.inc(a.activation(lnmb[:, rsl, :], lnm[:, rsl, :], AF.Identity,
                                                   scale=lnrstd[:, rsl, :]), "A", 1)
                                b.wait(a, "A", b.tok("A"))
                                b.inc(a.activation(lnmb[:, rsl, :], lnmb[:, rsl, :], AF.Identity,
                                                   scale=-1.0), "A", 1)
                                b.wait(a, "A", b.tok("A"))
                                b.wait(a, "A", pk_done.get(grt - 2))
                                b.wait(a, "A", tymm_done.get(grt - 2))
                                b.inc(a.activation(y1_f32[:, rsl, :], y1pre[:, rsl, :],
                                                   AF.Identity, bias=lnmb[:, rsl, :],
                                                   scale=lnrstd[:, rsl, :]), "A", 1)
                                b.inc(a.activation(y1_bf[:, rsl, :], y1pre[:, rsl, :],
                                                   AF.Identity, bias=lnmb[:, rsl, :],
                                                   scale=lnrstd[:, rsl, :]), "A", 1)

                if have_w1:
                    @blk.vector
                    def _(v, gw1=gw1):
                        b.wait(v, "V", exp_done[gw1])
                        b.wait(v, "V", b.tok("V"))
                        b.inc(v.reduce_sum(asum[:, :],
                                           aexp[:, :].rearrange("p (s x) -> p s x", s=8),
                                           axis=AX.X), "V", 1)
                        b.wait(v, "V", b.tok("V"))
                        b.inc(v.reciprocal(arecip[:, :], asum[:, :]), "V", 1)
                        b.wait(v, "V", b.tok("V"))
                        ar = arecip[:, :].unsqueeze(2).broadcast_to([128, 8, 64])
                        b.inc(v.tensor_tensor(a_bf[:, :].rearrange("p (s x) -> p s x", s=8),
                                              aexp[:, :].rearrange("p (s x) -> p s x", s=8),
                                              ar, ALU.mult), "V", 1)
                    scale_done[gw1] = b.tok("V")

                    @blk.vector
                    def _(v, gw1=gw1):
                        b.wait(v, "V", scale_done[gw1])
                        b.wait(v, "V", b.tok("V"))
                        b.wait(v, "V", at_mm_done.get(gw1 - 2))
                        b.wait(v, "V", memset_tok)
                        wp1 = gw1 % 2
                        b.inc(v.tensor_copy(
                            a_bd[0:64, wp1, :, 0:64],
                            a_bf[0:64, :].rearrange("p (h x) -> p h x", h=8)), "V", 1)
                        b.inc(v.tensor_copy(
                            a_bd[64:128, wp1, :, 64:128],
                            a_bf[64:128, :].rearrange("p (h x) -> p h x", h=8)), "V", 1)
                    bd_done[gw1] = b.tok("V")

        while wprep_tr[0] < len(wprep_jobs):
            while wprep_ld[0] < len(wprep_jobs) and wprep_ld[0] < wprep_tr[0] + 3:
                prep_load(wprep_ld[0])
                wprep_ld[0] += 1
            prep_transpose(wprep_tr[0])
            wprep_tr[0] += 1
        flush_typf()
        if wgT_holder[0] is None:
            emit_wg_stream(SEQSv * TB - 1)
        wg_tok = b.tok("wg0")

        @blk.sync
        def _(q):
            b.wait(q, "Q", b.tok("pk0"))
            b.wait(q, "Q", b.tok("pk1"))
            b.wait(q, "Q", b.tok("pf0"))
            b.wait(q, "Q", b.tok("pf1"))

    a_end = [b.tok(k) for k in ("T", "V", "A", "G")]
    a_end += [b.tok(k) for k in ("pk0", "pk1", "pf0", "pf1",
                                 "xq0", "xq1", "xq2", "xq3", "xr0", "xr1",
                                 "pw0", "pw1", "wc0", "wc1")]

    es_w.close()
    es_a.close()

    es_b = contextlib.ExitStack()
    SBB = lambda n, sh, dt, **kw: es_b.enter_context(nc.sbuf_tensor(n, sh, dt, **kw))
    wgT = wgT_holder[0]
    wuT = SBB("wuT", [128, 24, D], BF16)
    wdT = SBB("wdT", [128, 24, D], BF16)
    y1fm_sb = SBB("y1fm_sb", [128, 2, 8, 256], BF16)
    y1r32 = SBB("y1r32", [128, 2, 2, D], F32)
    sg_bf = SBB("sg_bf", [128, 2, 256], BF16)
    h_bf = SBB("h_bf", [128, 2, 256], BF16)
    y2pre = SBB("y2pre", [128, 2, D], F32)
    y2sq = SBB("y2sq", [128, D], F32)
    y2 = SBB("y2", [128, 2, 2, D], F32)
    s2s = SBB("s2s", [128, 2, 1], F32)
    s2q = SBB("s2q", [128, 2, 1], F32)
    l2m = SBB("l2m", [128, 2, 1], F32)
    l2msq = SBB("l2msq", [128, 2, 1], F32)
    l2vpe = SBB("l2vpe", [128, 2, 1], F32)
    l2rstd = SBB("l2rstd", [128, 2, 1], F32)
    l2ln = SBB("l2ln", [128, 2, 1], F32)
    l2mb = SBB("l2mb", [128, 2, 1], F32)

    gate_ps = [psum_qkv[:, 0:256], psum_qkv[:, 512:768]]
    up_ps = [psum_S[:, 0:256], psum_TR[:, 0:512].bitcast(F32)[:, 0:256]]
    down_ps = psum_O

    gate_done, silu_done, up_done, mul_done, down_done = {}, {}, {}, {}, {}
    apply2_done = {}
    mb2_toks = {}
    pending_ln2 = [None]
    pending_vln2 = [None]
    dload_done, rload_done, resid2_done, sq2_done, y2_done, store_done = {}, {}, {}, {}, {}, {}
    NGg = SEQSv * (Lv // 256)

    with nc.Block() as blk:
        @blk.tensor
        def _(t_):
            for tk in a_end:
                b.wait(t_, "T", tk)

        @blk.vector
        def _(v):
            for tk in a_end:
                b.wait(v, "V", tk)

        @blk.scalar
        def _(a):
            for tk in a_end:
                b.wait(a, "A", tk)

        @blk.gpsimd
        def _(g):
            for tk in a_end:
                b.wait(g, "G", tk)

        @blk.sync
        def _(q):
            for tk in a_end:
                b.wait(q, "Q", tk)

        wu_tok, wd_tok = {}, {}

        def emit_wu_chunk(hc):
            @blk.sync
            def _(q, hc=hc):
                b.wait(q, "Q", up_done.get((0, hc - 2)))
                b.inc(q.dma_start(wuT[:, hc, :], wuT_h.ap()[hc]),
                      "wl%d" % (hc % 2), 16)
            wu_tok[hc] = b.tok("wl%d" % (hc % 2))

        def emit_wd_chunk(hc):
            @blk.sync
            def _(q, hc=hc):
                b.wait(q, "Q", down_done.get((0, hc - 3)))
                b.inc(q.dma_start(wdT[:, hc, :], wdT_h.ap()[hc]),
                      "wl%d" % (2 + hc % 3), 16)
            wd_tok[hc] = b.tok("wl%d" % (2 + hc % 3))

        def emit_gloads(g):
            s = g // (Lv // 256)
            r0 = 256 * (g % (Lv // 256))
            gsl = g % 2

            @blk.sync
            def _(q, g=g, s=s, r0=r0, gsl=gsl):
                b.wait(q, "Q", up_done.get((g - 2, 23)))
                b.inc(q.dma_start(y1fm_sb[:, gsl, :, :],
                                  y1fm_h.ap()[s, g % (Lv // 256)]), "yf%d" % gsl, 16)
            dload_done[g] = b.tok("yf%d" % gsl)

            @blk.sync
            def _(q, g=g, s=s, r0=r0, gsl=gsl):
                b.wait(q, "Q", resid2_done.get((g - 2, 1)))
                for rs in range(2):
                    b.inc(q.dma_start(y1r32[:, gsl, rs, :],
                                      y1f_h.ap()[s, r0 + 128 * rs:r0 + 128 * (rs + 1), :]),
                          "yr%d" % gsl, 16)
            rload_done[g] = b.tok("yr%d" % gsl)

        emit_gloads(0)
        if NGg > 1:
            emit_gloads(1)
        for c0 in (0, 1):
            emit_wu_chunk(c0)
            emit_wd_chunk(c0)
        emit_wd_chunk(2)

        b2bases = {}

        def nb2(g):
            return b2bases[g] + (24 if g == 0 else 40)

        for g in range(NGg):
            s = g // (Lv // 256)
            r0 = 256 * (g % (Lv // 256))
            gsl = g % 2
            b2base = [b.cnt["A"] if g == 0 else nb2(g - 1)]
            b2bases[g] = b2base[0]

            def emit_down(hc, g=g):
                hsl = hc % 2

                @blk.tensor
                def _(t_, g=g, hc=hc, hsl=hsl):
                    b.wait(t_, "T", wd_tok[hc])
                    b.wait(t_, "T", mul_done[(g, hc)])
                    if hc == 0:
                        b.wait(t_, "T", resid2_done.get((g - 1, 1)))
                    for rs in range(2):
                        lhsT = h_bf[:, hsl, 128 * rs:128 * (rs + 1)]
                        b.inc(t_.matmul(down_ps[:, 1024 * rs:1024 * rs + 512],
                                        lhsT, wdT[:, hc, 0:512],
                                        start=(hc == 0), stop=(hc == 23),
                                        skip_group_check=True), "T", 1)
                        b.inc(t_.matmul(down_ps[:, 1024 * rs + 512:1024 * (rs + 1)],
                                        lhsT, wdT[:, hc, 512:1024],
                                        start=(hc == 0), stop=(hc == 23),
                                        skip_group_check=True), "T", 1)
                down_done[(g, hc)] = b.tok("T")

            for hc in range(24):
                hsl = hc % 2

                @blk.tensor
                def _(t_, g=g, gsl=gsl, hc=hc, hsl=hsl):
                    b.wait(t_, "T", wg_tok)
                    b.wait(t_, "T", dload_done[g])
                    b.wait(t_, "T", silu_done.get((g, hc - 2) if hc >= 2 else (g - 1, 22 + hc)))
                    for dc in range(8):
                        b.inc(t_.matmul(gate_ps[hsl], wgT[:, hc, 128 * dc:128 * (dc + 1)],
                                        y1fm_sb[:, gsl, dc, :],
                                        start=(dc == 0), stop=(dc == 7)), "T", 1)
                gate_done[(g, hc)] = b.tok("T")

                @blk.tensor
                def _(t_, g=g, gsl=gsl, hc=hc, hsl=hsl):
                    b.wait(t_, "T", wu_tok[hc])
                    b.wait(t_, "T", mul_done.get((g, hc - 2) if hc >= 2 else (g - 1, 22 + hc)))
                    for dc in range(8):
                        b.inc(t_.matmul(up_ps[hsl], wuT[:, hc, 128 * dc:128 * (dc + 1)],
                                        y1fm_sb[:, gsl, dc, :],
                                        start=(dc == 0), stop=(dc == 7)), "T", 1)
                up_done[(g, hc)] = b.tok("T")

                silu_done[(g, hc)] = ("A", b2base[0] + (
                    1 + hc if hc < 5 or g == 0 else
                    3 + hc if hc < 6 else
                    7 + hc if hc < 7 else
                    9 + hc if hc < 8 else
                    13 + hc if hc < 9 else
                    15 + hc if hc < 11 else
                    16 + hc if hc < 13 else
                    17 + hc))

                @blk.vector
                def _(v, g=g, hc=hc, hsl=hsl):
                    b.wait(v, "V", up_done[(g, hc)])
                    b.wait(v, "V", silu_done[(g, hc)])
                    b.wait(v, "V", down_done.get((g, hc - 2) if hc >= 2 else (g - 1, 22 + hc)))
                    b.wait(v, "V", b.tok("V"))
                    b.inc(v.tensor_tensor(h_bf[:, hsl, :], up_ps[hsl], sg_bf[:, hsl, :],
                                          ALU.mult), "V", 1)
                mul_done[(g, hc)] = b.tok("V")

                if hc == 9 and pending_vln2[0] is not None:
                    pending_vln2[0](0)
                if hc == 10 and pending_vln2[0] is not None:
                    pending_vln2[0](1)
                    pending_vln2[0] = None
                if hc >= 1:
                    emit_down(hc - 1)
                if g == 0 and hc + 2 <= 23:
                    emit_wu_chunk(hc + 2)
                    if hc >= 1:
                        emit_wd_chunk(hc + 2)
            emit_down(23)

            for rs in range(2):
                @blk.vector
                def _(v, g=g, gsl=gsl, rs=rs):
                    b.wait(v, "V", down_done[(g, 23)])
                    b.wait(v, "V", rload_done[g])
                    b.wait(v, "V", apply2_done.get((g - 1, rs)))
                    b.wait(v, "V", b.tok("V"))
                    b.inc(v.tensor_tensor(y2pre[:, rs, :], down_ps[:, 1024 * rs:1024 * (rs + 1)],
                                          y1r32[:, gsl, rs, :], ALU.add), "V", 1)
                resid2_done[(g, rs)] = b.tok("V")

            for rs in range(2):
                apply2_done[(g, rs)] = ("A", nb2(g) + (8 + 8 * rs if g == NGg - 1 else 26 + 3 * rs))

            if g + 2 < NGg:
                emit_gloads(g + 2)

            def emit_vln2(rs, g=g, s=s, r0=r0):
                rstd2_done = ("A", nb2(g) + (7 + 8 * rs if g == NGg - 1 else 22 + rs))

                @blk.vector
                def _(v, g=g, rs=rs, rstd2_done=rstd2_done):
                    b.wait(v, "V", rstd2_done)
                    b.wait(v, "V", b.tok("V"))
                    b.inc(v.reciprocal(l2rstd[:, rs, :], l2ln[:, rs, :]), "V", 1)
                    b.wait(v, "V", b.tok("V"))
                    b.inc(v.scalar_tensor_tensor(l2mb[:, rs, :], l2m[:, rs, :],
                                                 -1.0, l2rstd[:, rs, :],
                                                 ALU.mult, ALU.mult), "V", 1)
                mb2_toks[(g, rs)] = b.tok("V")

                @blk.gpsimd
                def _(g_, g=g, rs=rs, s=s, r0=r0):
                    b.wait(g_, "G", apply2_done[(g, rs)])
                    b.inc(g_.dma_start(out_h.ap()[s, r0 + 128 * rs:r0 + 128 * (rs + 1), :],
                                       y2[:, g % 2, rs, :]), "st%d" % (g % 2), 16)
                store_done[(g, rs)] = b.tok("st%d" % (g % 2))

            pending_vln2[0] = emit_vln2

            @blk.scalar
            def _(a, g=g, b2base=b2base, prev_ln2=pending_ln2[0]):
                assert b.cnt["A"] == b2base[0], (b.cnt["A"], b2base[0])
                for hc in range(24):
                    hsl = hc % 2
                    b.wait(a, "A", gate_done[(g, hc)])
                    b.wait(a, "A", mul_done.get((g, hc - 2) if hc >= 2 else (g - 1, 22 + hc)))
                    b.inc(a.activation(sg_bf[:, hsl, :], gate_ps[hsl], AF.Silu), "A", 1)
                    if prev_ln2 is None:
                        continue
                    gp = prev_ln2
                    if hc in (4, 6):
                        rs = 0 if hc == 4 else 1
                        b.wait(a, "A", resid2_done[(gp, rs)])
                        b.wait(a, "A", b.tok("A"))
                        b.inc(a.activation(y2sq[:, :], y2pre[:, rs, :], AF.Identity,
                                           accum_out=s2s[:, rs, :]), "A", 1)
                        b.wait(a, "A", b.tok("A"))
                        b.inc(a.activation(y2sq[:, :], y2pre[:, rs, :], AF.Square,
                                           accum_out=s2q[:, rs, :]), "A", 1)
                    if hc in (5, 7):
                        rs = 0 if hc == 5 else 1
                        b.wait(a, "A", b.tok("A"))
                        b.inc(a.activation(l2m[:, rs, :], s2s[:, rs, :], AF.Identity,
                                           scale=1.0 / D), "A", 1)
                        b.wait(a, "A", b.tok("A"))
                        b.inc(a.activation(l2msq[:, rs, :], l2m[:, rs, :], AF.Square), "A", 1)
                        b.inc(a.activation(l2vpe[:, rs, :], s2q[:, rs, :], AF.Identity,
                                           scale=1.0 / D), "A", 1)
                        b.wait(a, "A", b.tok("A"))
                        b.inc(a.activation(l2vpe[:, rs, :], l2msq[:, rs, :], AF.Identity,
                                           scale=-1.0, bias=l2vpe[:, rs, :]), "A", 1)
                    if hc == 8:
                        for rs in range(2):
                            b.wait(a, "A", b.tok("A"))
                            b.inc(a.activation(l2ln[:, rs, :], l2vpe[:, rs, :], AF.Sqrt,
                                               bias=lneps[:, :]), "A", 1)
                    if hc in (10, 12):
                        rs = 0 if hc == 10 else 1
                        b.wait(a, "A", mb2_toks[(gp, rs)])
                        b.wait(a, "A", store_done.get((gp - 2, 1)))
                        b.inc(a.activation(y2[:, gp % 2, rs, :], y2pre[:, rs, :], AF.Identity,
                                           bias=l2mb[:, rs, :], scale=l2rstd[:, rs, :]),
                              "A", 1)
            pending_ln2[0] = g

        if pending_vln2[0] is not None:
            pending_vln2[0](0)
            pending_vln2[0](1)
            pending_vln2[0] = None

        @blk.scalar
        def _(a, prev_ln2=pending_ln2[0]):
            gp = prev_ln2
            for rs in range(2):
                b.wait(a, "A", resid2_done[(gp, rs)])
                b.wait(a, "A", b.tok("A"))
                b.inc(a.activation(y2sq[:, :], y2pre[:, rs, :], AF.Identity,
                                   accum_out=s2s[:, rs, :]), "A", 1)
                b.wait(a, "A", b.tok("A"))
                b.inc(a.activation(y2sq[:, :], y2pre[:, rs, :], AF.Square,
                                   accum_out=s2q[:, rs, :]), "A", 1)
                b.wait(a, "A", b.tok("A"))
                b.inc(a.activation(l2m[:, rs, :], s2s[:, rs, :], AF.Identity,
                                   scale=1.0 / D), "A", 1)
                b.wait(a, "A", b.tok("A"))
                b.inc(a.activation(l2msq[:, rs, :], l2m[:, rs, :], AF.Square), "A", 1)
                b.inc(a.activation(l2vpe[:, rs, :], s2q[:, rs, :], AF.Identity,
                                   scale=1.0 / D), "A", 1)
                b.wait(a, "A", b.tok("A"))
                b.inc(a.activation(l2vpe[:, rs, :], l2msq[:, rs, :], AF.Identity,
                                   scale=-1.0, bias=l2vpe[:, rs, :]), "A", 1)
                b.wait(a, "A", b.tok("A"))
                b.inc(a.activation(l2ln[:, rs, :], l2vpe[:, rs, :], AF.Sqrt,
                                   bias=lneps[:, :]), "A", 1)
                b.wait(a, "A", mb2_toks[(gp, rs)])
                b.wait(a, "A", store_done.get((gp - 2, 1)))
                b.inc(a.activation(y2[:, gp % 2, rs, :], y2pre[:, rs, :], AF.Identity,
                                   bias=l2mb[:, rs, :], scale=l2rstd[:, rs, :]), "A", 1)

        @blk.gpsimd
        def _(g_):
            g_.wait_ge(b.sems["st0"], b.cnt["st0"])
            g_.wait_ge(b.sems["st1"], b.cnt["st1"])

    es_b.close()
    es_outer.close()
    return nc


_IDENT = np.eye(128, dtype=ml_dtypes.bfloat16)


def kernel(L0_attn_x, L0_attn_W_qkv, L0_attn_b_qkv, L0_attn_W_out, L0_attn_b_out,
           L0_ffn_w_gate, L0_ffn_w_up, L0_ffn_w_down):
    x = np.ascontiguousarray(np.asarray(L0_attn_x, dtype=np.float32))
    wqkv = np.ascontiguousarray(np.asarray(L0_attn_W_qkv, dtype=np.float32))
    bqkv = np.ascontiguousarray(np.asarray(L0_attn_b_qkv, dtype=np.float32)).reshape(1, 3 * D)
    wout = np.ascontiguousarray(np.asarray(L0_attn_W_out, dtype=np.float32))
    bout = np.ascontiguousarray(np.asarray(L0_attn_b_out, dtype=np.float32)).reshape(1, D)
    wg = np.ascontiguousarray(np.asarray(L0_ffn_w_gate, dtype=np.int32))
    wu = np.ascontiguousarray(np.asarray(L0_ffn_w_up, dtype=np.int32))
    wd = np.ascontiguousarray(np.asarray(L0_ffn_w_down, dtype=np.int32))

    nc = build_program()
    in_maps = []
    for i in range(NCORES):
        in_maps.append({
            "x": x[SEQS * i:SEQS * (i + 1)],
            "wqkv": wqkv, "bqkv": bqkv, "wout": wout, "bout": bout,
            "wg": wg, "wu": wu, "wd": wd, "ident": _IDENT,
        })
    res = run_bass_kernel_spmd(nc, in_maps, core_ids=list(range(NCORES)))
    return np.concatenate([r["out"] for r in res.results], axis=0)


# version 33
# speedup vs baseline: 1.4325x; 1.0106x over previous
import contextlib

import numpy as np
import ml_dtypes
import concourse.bass as bass
import concourse.mybir as mybir
from concourse.bass_utils import run_bass_kernel_spmd

F32 = mybir.dt.float32
BF16 = mybir.dt.bfloat16
I32 = mybir.dt.int32
AF = mybir.ActivationFunctionType
ALU = mybir.AluOpType
AX = mybir.AxisListType

NCORES = 8
SEQS = 2
L = 4096
D = 1024
HID = 3072
TB = 17
NW = 16
RTS = 32
INV127 = float(np.float32(1.0 / 127.0))
EPS = 1e-5


class Bld:
    def __init__(self, nc, es):
        self.nc = nc
        self.sems = {}
        self.cnt = {}
        self.waited = {}
        self.es = es

    def sem(self, name):
        self.sems[name] = self.es.enter_context(self.nc.semaphore(name))
        self.cnt[name] = 0

    def inc(self, instr, name, by=1):
        instr.then_inc(self.sems[name], by)
        self.cnt[name] += by
        return (name, self.cnt[name])

    def tok(self, name):
        return (name, self.cnt[name])

    def wait(self, eng, engkey, tok):
        if tok is None:
            return
        name, val = tok
        if val <= 0:
            return
        key = (engkey, name)
        if self.waited.get(key, 0) >= val:
            return
        self.waited[key] = val
        eng.wait_ge(self.sems[name], val)


def build_program(debug=False, seqs=SEQS, l=L):
    TB = l // 256 + 1
    NW = l // 256
    RTS = l // 128
    SEQSv, Lv = seqs, l

    def lastwin(gb):
        if gb is None or gb < 0:
            return None
        s, t = gb // TB, gb % TB
        return s * NW + min(t, NW - 1)

    nc = bass.Bass("TRN2", target_bir_lowering=False)

    x_h = nc.dram_tensor("x", [SEQSv, Lv, D], F32, kind="ExternalInput")
    wqkv_h = nc.dram_tensor("wqkv", [D, 3 * D], F32, kind="ExternalInput")
    bqkv_h = nc.dram_tensor("bqkv", [1, 3 * D], F32, kind="ExternalInput")
    wout_h = nc.dram_tensor("wout", [D, D], F32, kind="ExternalInput")
    bout_h = nc.dram_tensor("bout", [1, D], F32, kind="ExternalInput")
    wg_h = nc.dram_tensor("wg", [HID, D], I32, kind="ExternalInput")
    wu_h = nc.dram_tensor("wu", [HID, D], I32, kind="ExternalInput")
    wd_h = nc.dram_tensor("wd", [D, HID], I32, kind="ExternalInput")
    ident_h = nc.dram_tensor("ident", [128, 128], BF16, kind="ExternalInput")
    out_h = nc.dram_tensor("out", [SEQSv, Lv, D], F32, kind="ExternalOutput")
    ykind = "ExternalOutput" if debug else "Internal"
    y1f_h = nc.dram_tensor("y1dbg", [SEQSv, Lv, D], F32, kind=ykind)
    y1fm_h = nc.dram_tensor("y1fm_scr", [SEQSv, Lv // 256, 128, 8, 256], BF16,
                            kind="Internal")
    wgT_h = nc.dram_tensor("wgT_scr", [24, 128, D], BF16, kind="Internal")
    wuT_h = nc.dram_tensor("wuT_scr", [24, 128, D], BF16, kind="Internal")
    wdT_h = nc.dram_tensor("wdT_scr", [24, 128, D], BF16, kind="Internal")

    es_outer = contextlib.ExitStack()
    b = Bld(nc, es_outer)
    for s in ["T", "V", "A", "G",
              "xq0", "xq1", "xq2", "xq3", "xr0", "xr1",
              "wl0", "wl1", "wl2", "wl3", "wl4", "wg0",
              "tx0", "tx1", "tv0", "tv1", "ty0", "ty1", "tw0", "tw1",
              "wc0", "wc1", "wc2", "wc3", "wi0", "pk0", "pk1", "pf0", "pf1", "pw0", "pw1",
              "yf0", "yf1", "yr0", "yr1", "st0", "st1"]:
        b.sem(s)

    SO = lambda n, sh, dt, **kw: es_outer.enter_context(nc.sbuf_tensor(n, sh, dt, **kw))
    lneps = SO("lneps", [128, 1], F32)
    PS = lambda n, sh, dt: es_outer.enter_context(nc.psum_tensor(n, sh, dt))
    psum_qkv = PS("psum_qkv", [128, D], F32)
    psum_S = PS("psum_S", [128, 512], F32)
    psum_TR = PS("psum_TR", [128, 1024], BF16)
    psum_O = PS("psum_O", [128, 2048], F32)

    es_a = contextlib.ExitStack()
    SA = lambda n, sh, dt, **kw: es_a.enter_context(nc.sbuf_tensor(n, sh, dt, **kw))

    es_qkv = contextlib.ExitStack()
    wqkv_bf = es_qkv.enter_context(
        nc.sbuf_tensor("wqkv_bf", [128, 8, 3 * D], BF16, side="right"))
    wgT_holder = [None]
    wout_bf = SA("wout_bf", [128, 8, D], BF16)
    bq_rep = SA("bq_rep", [128, 3 * D], BF16)
    bo_rep = SA("bo_rep", [128, D], BF16)
    ident_sb = SA("ident_sb", [128, 128], BF16)
    ones128 = SA("ones128", [128, 128], BF16)
    ones1 = SA("ones1", [1, 128], BF16)

    xt_bf = SA("xt_bf", [128, 3, D], BF16)
    x_fm = SA("x_fm", [128, 2, 8, 128], BF16)
    q_tm = SA("q_tm", [128, 3, D], BF16)
    k_tm = SA("k_tm", [128, 3, D], BF16)
    v_tmp = SA("v_tmp", [128, 3, D], BF16)
    v_fm = SA("v_fm", [128, 3, 8, 128], BF16)
    attn_out = SA("attn_out", [128, 8, 512], BF16)
    negmax = SA("negmax", [128, 8], F32)
    ssub = SA("ssub", [128, 512], F32)
    aexp = SA("aexp", [128, 512], F32)
    asum = SA("asum", [128, 8], F32)
    arecip = SA("arecip", [128, 8], F32)
    a_bf = SA("a_bf", [128, 512], BF16)
    a_bd = SA("a_bd", [128, 2, 8, 128], BF16)
    at_bf = SA("at_bf", [128, 1024], BF16)
    xr32 = SA("xr32", [128, 2, D], F32)
    y1pre = SA("y1pre", [128, 2, D], F32)
    y1sq = SA("y1sq", [128, D], F32)
    sums = SA("sums", [128, 2, 1], F32)
    sumsq = SA("sumsq", [128, 2, 1], F32)
    lnm = SA("lnm", [128, 2, 1], F32)
    lnmsq = SA("lnmsq", [128, 2, 1], F32)
    lnvpe = SA("lnvpe", [128, 2, 1], F32)
    lnrstd = SA("lnrstd", [128, 2, 1], F32)
    lnln = SA("lnln", [128, 2, 1], F32)
    lnmb = SA("lnmb", [128, 2, 1], F32)
    y1_f32 = SA("y1_f32", [128, 2, D], F32)
    y1_bf = SA("y1_bf", [128, 2, D], BF16)
    y1fm_stage = SA("y1fm_stage", [128, 4, 8, 256], BF16)

    es_w = contextlib.ExitStack()
    SW = lambda n, sh, dt, **kw: es_w.enter_context(nc.sbuf_tensor(n, sh, dt, **kw))
    w_in_bf = SW("w_in_bf", [128, 3, HID], BF16)
    wT_stage = SW("wT_stage", [128, 8, 128], BF16)
    bq_bf = SW("bq_bf", [1, 3 * D], BF16)
    bo_bf = SW("bo_bf", [1, D], BF16)

    with nc.Block() as blk:
        @blk.gpsimd
        def _(g):
            b.inc(g.memset(ones128[:, :], 1.0), "G", 1)
            b.inc(g.memset(ones1[:, :], 1.0), "G", 1)
            b.inc(g.memset(a_bd[:, :, :, :], 0.0), "G", 1)
            b.inc(g.memset(lneps[:, :], float(EPS)), "G", 1)
        memset_tok = b.tok("G")

        @blk.sync
        def _(q):
            b.inc(q.dma_start(ident_sb[:, :], ident_h.ap()), "wl0", 16)

        @blk.gpsimd
        def _(g):
            for c in range(8):
                b.inc(g.dma_start(wqkv_bf[:, c, :], wqkv_h.ap()[128 * c:128 * (c + 1), :]),
                      "wi0", 16)
                b.inc(g.dma_start(wout_bf[:, c, :], wout_h.ap()[128 * c:128 * (c + 1), :]),
                      "wi0", 16)
            b.inc(g.dma_start(bq_bf[:, :], bqkv_h.ap()), "wi0", 16)
            b.inc(g.dma_start(bo_bf[:, :], bout_h.ap()), "wi0", 16)
        wcast_tok = b.tok("wi0")
        prev_cast = wcast_tok

        @blk.vector
        def _(v):
            b.wait(v, "V", wcast_tok)
            b.inc(v.tensor_scalar_mul(bq_bf[:, :], bq_bf[:, :], 1.0 / 128.0), "V", 1)
            b.inc(v.tensor_scalar_mul(bo_bf[:, :], bo_bf[:, :], 1.0 / 128.0), "V", 1)
        bias_bf_tok = b.tok("V")

        brep_tok = None
        for g3 in range(4):
            @blk.tensor
            def _(t_, g3=g3, brep_tok=brep_tok):
                b.wait(t_, "T", bias_bf_tok)
                b.wait(t_, "T", memset_tok)
                b.wait(t_, "T", brep_tok)
                if g3 < 3:
                    r0 = bq_bf[0:1, 1024 * g3:1024 * g3 + 512]
                    r1 = bq_bf[0:1, 1024 * g3 + 512:1024 * (g3 + 1)]
                else:
                    r0, r1 = bo_bf[0:1, 0:512], bo_bf[0:1, 512:1024]
                b.inc(t_.matmul(psum_qkv[:, 0:512], ones1[:, :], r0, start=True, stop=True), "T", 1)
                b.inc(t_.matmul(psum_qkv[:, 512:1024], ones1[:, :], r1, start=True, stop=True), "T", 1)
            mm_tok = b.tok("T")

            @blk.vector
            def _(v, g3=g3, mm_tok=mm_tok):
                b.wait(v, "V", mm_tok)
                dst = bq_rep[:, 1024 * g3:1024 * (g3 + 1)] if g3 < 3 else bo_rep[:, :]
                b.inc(v.tensor_copy(dst, psum_qkv[:, :]), "V", 1)
            brep_tok = b.tok("V")

        @blk.sync
        def _(q):
            b.wait(q, "Q", prev_cast)
            b.wait(q, "Q", brep_tok)

    cast_done, tx_done, qkv_mm_done, evac_done, tv_done = {}, {}, {}, {}, {}
    evac_half, qkv_half = {}, {}
    xfm_done, tvmm_done, tymm_done = {}, {}, {}
    tr_banks = [[0, None], [1, None], [2, None]]
    tr_idx = [0]
    qkv_free = [None]
    tr_noq = [False]

    def claim_tr():
        bk = tr_banks[tr_idx[0]]
        tr_idx[0] = (tr_idx[0] + 1) % 3
        if tr_noq[0] and bk[0] == 2:
            bk = tr_banks[tr_idx[0]]
            tr_idx[0] = (tr_idx[0] + 1) % 3
        extra = qkv_free[0] if bk[0] == 2 else None
        return bk, extra

    def tr_slice(bi, c0, c1):
        if bi == 0:
            return psum_TR[:, c0:c1]
        if bi == 1:
            return psum_S[:, c0 // 2:c1 // 2].bitcast(BF16)
        return psum_qkv[:, c0 // 2:c1 // 2].bitcast(BF16)
    s_mm_done, ssub_done, exp_done, scale_done, bd_done = {}, {}, {}, {}, {}
    at_mm_done, atevac_done, o_mm_done, oevac_done = {}, {}, {}, {}
    proj_done, resid_done, sq_done, apply_done = {}, {}, {}, {}
    ty_done, pk_done, pf_done, xq_done, xr_done = {}, {}, {}, {}, {}
    last_resid = None

    a_join_T = [wcast_tok, memset_tok, brep_tok, ("wl0", 16)]

    with nc.Block() as blk:
        @blk.tensor
        def _(t_):
            for tk in a_join_T:
                b.wait(t_, "T", tk)

        def emit_xload(gt2):
            if gt2 >= SEQSv * TB:
                return
            s2, t2 = gt2 // TB, gt2 % TB
            xsl = gt2 % 3

            @blk.gpsimd
            def _(g, s2=s2, t2=t2, gt2=gt2, xsl=xsl):
                b.wait(g, "G", tx_done.get(gt2 - 3))
                b.inc(g.dma_start(xt_bf[:, xsl, :], x_h.ap()[s2, 128 * t2:128 * (t2 + 1), :]),
                      "xq%d" % xsl, 16)
            cast_done[gt2] = b.tok("xq%d" % xsl)
            xq_done[gt2] = cast_done[gt2]

        def emit_xtr(gt2):
            if gt2 >= SEQSv * TB:
                return
            sl2 = gt2 % 2
            xsl = gt2 % 3
            bk, qtok = claim_tr()
            bi, freetok = bk[0], bk[1]

            @blk.tensor
            def _(t_, gt2=gt2, xsl=xsl, bi=bi, freetok=freetok, qtok=qtok):
                b.wait(t_, "T", cast_done[gt2])
                b.wait(t_, "T", freetok)
                b.wait(t_, "T", qtok)
                for c in range(8):
                    mm = t_.transpose(tr_slice(bi, 128 * c, 128 * (c + 1)),
                                      xt_bf[:, xsl, 128 * c:128 * (c + 1)],
                                      ident_sb[:, :])
                    if c == 7:
                        b.inc(mm, "T", 1)
            tx_done[gt2] = b.tok("T")

            @blk.vector
            def _(v, gt2=gt2, sl2=sl2, bi=bi):
                b.wait(v, "V", tx_done[gt2])
                b.wait(v, "V", qkv_mm_done.get(gt2 - 2))
                b.inc(v.tensor_copy(x_fm[:, sl2, :, :].rearrange("p a b -> p (a b)"),
                                    tr_slice(bi, 0, 1024)), "V", 1)
            xfm_done[gt2] = b.tok("V")
            bk[1] = xfm_done[gt2]

        pending_typf = [None]

        pf_idx = [0]
        pf_by_idx = {}

        def emit_ty_one(gw, s, n, k2):
            wsl = n % 4
            if True:
                grt = s * RTS + 2 * n + k2
                rsl = grt % 2
                bk, qtok = claim_tr()
                bi, freetok = bk[0], bk[1]

                @blk.tensor
                def _(t_, grt=grt, rsl=rsl, bi=bi, freetok=freetok, qtok=qtok):
                    b.wait(t_, "T", apply_done[grt])
                    b.wait(t_, "T", freetok)
                    b.wait(t_, "T", qtok)
                    for c in range(8):
                        mm = t_.transpose(tr_slice(bi, 128 * c, 128 * (c + 1)),
                                          y1_bf[:, rsl, 128 * c:128 * (c + 1)],
                                          ident_sb[:, :])
                        if c == 7:
                            b.inc(mm, "T", 1)
                tymm_done[grt] = b.tok("T")

                pp1 = pf_by_idx.get(pf_idx[0] - 1)
                pp2 = pf_by_idx.get(pf_idx[0] - 2)

                @blk.vector
                def _(v, grt=grt, rsl=rsl, k2=k2, wsl=wsl, bi=bi, pp1=pp1, pp2=pp2):
                    b.wait(v, "V", tymm_done[grt])
                    b.wait(v, "V", pp1)
                    b.wait(v, "V", pp2)
                    dst = y1fm_stage[:, wsl, :, 128 * k2:128 * (k2 + 1)]
                    b.inc(v.tensor_copy(dst, tr_slice(bi, 0, 1024)
                                        .rearrange("p (a b) -> p a b", a=8)),
                          "V", 1)
                ty_done[grt] = b.tok("V")
                bk[1] = ty_done[grt]

        def emit_pf(gw, s, n):
            wsl = n % 4
            nw0 = n - wsl
            pi = pf_idx[0]
            pf_idx[0] += 1
            prev2 = pf_by_idx.get(pi - 2)

            @blk.gpsimd
            def _(g, s=s, n=n, gw=gw, wsl=wsl, nw0=nw0, pi=pi, prev2=prev2):
                b.wait(g, "G", prev2)
                b.wait(g, "G", ty_done[s * RTS + 2 * n])
                b.wait(g, "G", ty_done[s * RTS + 2 * n + 1])
                src_ap = y1fm_stage[:, 0:wsl + 1, :, :].rearrange(
                    "p w a b -> p w (a b)")
                b.inc(g.dma_start(
                    y1fm_h.ap()[s, nw0:n + 1].rearrange(
                        "w p a b -> p w (a b)"),
                    src_ap), "pf%d" % (pi % 2), 16)
            pf_done[gw] = b.tok("pf%d" % (pi % 2))
            pf_by_idx[pi] = pf_done[gw]

        def flush_typf(which=2):
            if pending_typf[0] is not None:
                gw0, s0, n0 = pending_typf[0]
                if which >= 1 and pending_typf[0] not in ty_started:
                    ty_started.add(pending_typf[0])
                    emit_ty_one(gw0, s0, n0, 0)
                if which == 2:
                    emit_ty_one(gw0, s0, n0, 1)
                    if (n0 % 4 == 3) or (n0 == NW - 1):
                        emit_pf(gw0, s0, n0)
                    pending_typf[0] = None
        ty_started = set()

        wprep_state = {"stage_free": None}
        deq_prev = {}
        tw_prev = {}

        def prep_load(i):
            load_src, n_el, _batches = wprep_jobs[i]
            sl = i % 3
            sp = i % 2

            @blk.gpsimd
            def _(g, i=i, sl=sl, load_src=load_src, n_el=n_el):
                b.wait(g, "G", tw_prev.get(i - 3))
                b.inc(g.dma_start(w_in_bf[:, sl, 0:n_el], load_src), "wc%d" % sl, 16)
            wc_tok = b.tok("wc%d" % sl)

            deq_prev[i] = wc_tok

        def prep_transpose(i):
            _src, n_el, sub_transposes = wprep_jobs[i]
            sl = i % 3
            sp = i % 2

            @blk.vector
            def _(v, i=i, sl=sl, n_el=n_el):
                b.wait(v, "V", deq_prev[i])
                b.inc(v.tensor_scalar_mul(w_in_bf[:, sl, 0:n_el], w_in_bf[:, sl, 0:n_el],
                                          INV127), "V", 1)
            dq_tok = b.tok("V")
            for batch in sub_transposes:
                bk, qtok = claim_tr()
                bi, tr_free = bk[0], bk[1]

                @blk.tensor
                def _(t_, i=i, sl=sl, batch=batch, bi=bi, tr_free=tr_free,
                      qtok=qtok, dq_tok=dq_tok):
                    b.wait(t_, "T", dq_tok)
                    b.wait(t_, "T", tr_free)
                    b.wait(t_, "T", qtok)
                    for j, (src_sl, _dst, _comb) in enumerate(batch):
                        mm = t_.transpose(tr_slice(bi, 128 * j, 128 * (j + 1)),
                                          w_in_bf[:, sl, src_sl[0]:src_sl[1]],
                                          ident_sb[:, :])
                        if j == len(batch) - 1:
                            b.inc(mm, "T", 1)
                tw_tok = b.tok("T")
                tw_prev[i] = tw_tok

                @blk.vector
                def _(v, tw_tok=tw_tok, bi=bi, sf=wprep_state["stage_free"]):
                    b.wait(v, "V", tw_tok)
                    b.wait(v, "V", sf)
                    b.inc(v.tensor_copy(wT_stage[:, :, :].rearrange("p a b -> p (a b)"),
                                        tr_slice(bi, 0, 1024)), "V", 1)
                ev_tok = b.tok("V")
                bk[1] = ev_tok

                @blk.gpsimd
                def _(g, batch=batch, ev_tok=ev_tok, sl=sl):
                    b.wait(g, "G", ev_tok)
                    comb, park_tr = batch[0][2], batch[0][1]
                    if park_tr:
                        b.inc(g.dma_start(comb.transpose([1, 0, 2]), wT_stage[:, :, :]),
                              "pw%d" % sp, 16)
                    else:
                        b.inc(g.dma_start(comb,
                                          wT_stage[:, :, :].rearrange("p a b -> p (a b)")),
                              "pw%d" % sp, 16)
                wprep_state["stage_free"] = b.tok("pw%d" % sp)

        wprep_jobs = []
        for src_h, dst_h in [(wg_h, wgT_h), (wu_h, wuT_h)]:
            for hc in range(24):
                comb = dst_h.ap()[hc]
                batch = [((128 * dc, 128 * (dc + 1)), False, comb) for dc in range(8)]
                wprep_jobs.append((src_h.ap()[128 * hc:128 * (hc + 1), :], D, [batch]))
        for dc in range(8):
            batches = []
            for hb in range(3):
                comb = wdT_h.ap()[8 * hb:8 * (hb + 1), :, 128 * dc:128 * (dc + 1)]
                batches.append([((128 * (8 * hb + j), 128 * (8 * hb + j + 1)), True, comb)
                                for j in range(8)])
            wprep_jobs.append((wd_h.ap()[128 * dc:128 * (dc + 1), :], HID, batches))
        wprep_ld = [0]
        wprep_tr = [0]

        def prep_loads(n_load):
            k = 0
            while wprep_ld[0] < len(wprep_jobs) and k < n_load:
                i = wprep_ld[0]
                if i >= 3 and (i - 3) not in tw_prev:
                    break
                prep_load(i)
                wprep_ld[0] += 1
                k += 1

        def prep_trs(n_tr):
            k = 0
            while wprep_tr[0] < wprep_ld[0] and k < n_tr:
                if wprep_tr[0] + 2 > wprep_ld[0]:
                    break
                prep_transpose(wprep_tr[0])
                wprep_tr[0] += 1
                k += 1

        def emit_wg_stream(gt_last):
            es_qkv.close()
            wgT_holder[0] = es_outer.enter_context(
                nc.sbuf_tensor("wgT", [128, 24, D], BF16, side="right"))
            pw_fin = [b.tok("pw0"), b.tok("pw1")]

            @blk.sync
            def _(q, gt_last=gt_last, pw_fin=pw_fin):
                b.wait(q, "Q", qkv_mm_done[(gt_last, 2)])
                for tk in pw_fin:
                    b.wait(q, "Q", tk)
                b.inc(q.dma_start(wgT_holder[0][:, :, :],
                                  wgT_h.ap().transpose([1, 0, 2])), "wg0", 16)

        emit_xload(0)
        emit_xload(1)
        emit_xload(2)
        emit_xtr(0)
        emit_xtr(1)

        for s in range(SEQSv):
            for t in range(TB + 1):
                gt = s * TB + t
                sl = gt % 2
                sl3 = gt % 3
                have_qkv = t < TB
                have_w1 = 1 <= t <= NW
                have_w2 = t >= 2
                abase = [b.cnt["A"]]
                off_exp = 9 if have_w2 else 1

                prep_loads(2)
                if have_qkv:
                    emit_xload(gt + 3)

                def emit_qkv_pass(p, gt=gt, sl=sl):
                    prev = (gt, p - 1) if p > 0 else (gt - 1, 2)
                    qsfree = tr_banks[2][1] if p == 0 else None

                    @blk.tensor
                    def _(t_, gt=gt, sl=sl, p=p, prev=prev, qsfree=qsfree):
                        b.wait(t_, "T", xfm_done[gt])
                        b.wait(t_, "T", evac_half.get((prev, 0)))
                        b.wait(t_, "T", qsfree)
                        for c in range(8):
                            lhsT = x_fm[:, sl, c, :]
                            t_.matmul(psum_qkv[:, 0:512],
                                      lhsT, wqkv_bf[:, c, 1024 * p:1024 * p + 512],
                                      start=(c == 0), stop=False)
                            if c == 0:
                                b.wait(t_, "T", evac_half.get((prev, 1)))
                            t_.matmul(psum_qkv[:, 512:1024],
                                      lhsT, wqkv_bf[:, c, 1024 * p + 512:1024 * (p + 1)],
                                      start=(c == 0), stop=False)
                        b.inc(t_.matmul(psum_qkv[:, 0:512], ones128[:, :],
                                        bq_rep[:, 1024 * p:1024 * p + 512],
                                        start=False, stop=True), "T", 1)
                        qkv_half[(gt, p, 0)] = b.tok("T")
                        b.inc(t_.matmul(psum_qkv[:, 512:1024], ones128[:, :],
                                        bq_rep[:, 1024 * p + 512:1024 * (p + 1)],
                                        start=False, stop=True), "T", 1)
                    qkv_mm_done[(gt, p)] = b.tok("T")
                    qkv_half[(gt, p, 1)] = qkv_mm_done[(gt, p)]
                    if p == 2:
                        qkv_mm_done[gt] = b.tok("T")

                    @blk.vector
                    def _(v, gt=gt, sl3=sl3, p=p):
                        b.wait(v, "V", qkv_half[(gt, p, 0)])
                        lw = lastwin(gt - 3)
                        if p == 0:
                            b.wait(v, "V", s_mm_done.get(lw))
                            dst = q_tm
                        elif p == 1:
                            b.wait(v, "V", s_mm_done.get(lw))
                            dst = k_tm
                        else:
                            b.wait(v, "V", tv_done.get(gt - 3))
                            dst = v_tmp
                        b.inc(v.tensor_copy(dst[:, sl3, 0:512], psum_qkv[:, 0:512]), "V", 1)
                        evac_half[((gt, p), 0)] = b.tok("V")
                        b.wait(v, "V", qkv_half[(gt, p, 1)])
                        b.inc(v.tensor_copy(dst[:, sl3, 512:1024], psum_qkv[:, 512:1024]),
                              "V", 1)
                    evac_half[((gt, p), 1)] = b.tok("V")
                    evac_done[(gt, p)] = b.tok("V")

                if have_qkv:
                    emit_qkv_pass(0)
                    emit_qkv_pass(1)

                if have_w1:
                    n1 = t - 1
                    gw1 = s * NW + n1
                    s_sl0 = (s * TB + n1) % 3
                    s_sl1 = (s * TB + n1 + 1) % 3
                    gt_n1 = s * TB + n1 + 1
                    sfree = tr_banks[1][1]
                    exp_done[gw1] = ("A", abase[0] + off_exp)

                    @blk.tensor
                    def _(t_, gw1=gw1, s_sl0=s_sl0, s_sl1=s_sl1, gt_n1=gt_n1, sfree=sfree):
                        b.wait(t_, "T", evac_half[((gt_n1, 1), 0)])
                        b.wait(t_, "T", ssub_done.get(gw1 - 1))
                        b.wait(t_, "T", sfree)
                        for hp in range(8):
                            if hp == 4:
                                b.wait(t_, "T", evac_half[((gt_n1, 1), 1)])
                            for par in range(2):
                                h = 2 * hp + par
                                for j, slj in enumerate((s_sl0, s_sl1)):
                                    mm = t_.matmul(
                                        psum_S[64 * par:64 * par + 64, 64 * hp:64 * hp + 64],
                                        q_tm[:, slj, 64 * h:64 * h + 64],
                                        k_tm[:, slj, 64 * h:64 * h + 64],
                                        start=(j == 0), stop=(j == 1),
                                        skip_group_check=True)
                                    if hp == 7 and par == 1 and j == 1:
                                        b.inc(mm, "T", 1)
                    s_mm_done[gw1] = b.tok("T")

                    @blk.vector
                    def _(v, gw1=gw1):
                        b.wait(v, "V", s_mm_done[gw1])
                        b.wait(v, "V", b.tok("V"))
                        b.inc(v.reduce_max(negmax[:, :],
                                           psum_S[:, :].rearrange("p (s x) -> p s x", s=8),
                                           axis=AX.X, negate=True), "V", 1)
                        b.wait(v, "V", b.tok("V"))
                        b.wait(v, "V", exp_done.get(gw1 - 1))
                        nm = negmax[:, :].unsqueeze(2).broadcast_to([128, 8, 64])
                        b.inc(v.tensor_tensor(ssub[:, :].rearrange("p (s x) -> p s x", s=8),
                                              psum_S[:, :].rearrange("p (s x) -> p s x", s=8),
                                              nm, ALU.add), "V", 1)
                    ssub_done[gw1] = b.tok("V")
                    tr_banks[1][1] = ssub_done[gw1]

                if have_qkv:
                    emit_qkv_pass(2)
                    qkv_free[0] = evac_half[((gt, 2), 1)]
                    emit_xtr(gt + 2)

                if s == SEQSv - 1 and t == TB - 1 and wprep_tr[0] >= 24:
                    emit_wg_stream(gt)

                if have_qkv:
                    vbk, vqtok = claim_tr()
                    vbi, vfreetok = vbk[0], vbk[1]

                    @blk.tensor
                    def _(t_, gt=gt, sl3=sl3, vbi=vbi, vfreetok=vfreetok, vqtok=vqtok):
                        b.wait(t_, "T", evac_half[((gt, 2), 0)])
                        b.wait(t_, "T", vfreetok)
                        b.wait(t_, "T", vqtok)
                        for c in range(8):
                            if c == 4:
                                b.wait(t_, "T", evac_half[((gt, 2), 1)])
                            mm = t_.transpose(tr_slice(vbi, 128 * c, 128 * (c + 1)),
                                              v_tmp[:, sl3, 128 * c:128 * (c + 1)],
                                              ident_sb[:, :])
                            if c == 7:
                                b.inc(mm, "T", 1)
                    tvmm_done[gt] = b.tok("T")

                    @blk.vector
                    def _(v, gt=gt, sl3=sl3, vbi=vbi):
                        b.wait(v, "V", tvmm_done[gt])
                        b.wait(v, "V", o_mm_done.get(lastwin(gt - 3)))
                        b.inc(v.tensor_copy(v_fm[:, sl3, :, :].rearrange("p a b -> p (a b)"),
                                            tr_slice(vbi, 0, 1024)), "V", 1)
                    tv_done[gt] = b.tok("V")
                    vbk[1] = tv_done[gt]

                if have_w2:
                    n = t - 2
                    gw = s * NW + n
                    sl0 = (s * TB + n) % 3
                    sl1 = (s * TB + n + 1) % 3
                    wpar = gw % 2

                    flush_typf(which=1)

                    @blk.tensor
                    def _(t_, gw=gw, wpar=wpar):
                        b.wait(t_, "T", bd_done[gw])
                        b.wait(t_, "T", oevac_done.get(gw - 1))
                        b.wait(t_, "T", resid_done.get(2 * (gw - 1)))
                        at_dst = psum_O[:, 0:512].bitcast(BF16)
                        for hp in range(8):
                            t_.transpose(at_dst[0:64, 128 * hp:128 * (hp + 1)],
                                         a_bd[:, wpar, hp, 0:64], ident_sb[:, :])
                            mm = t_.transpose(at_dst[64:128, 128 * hp:128 * (hp + 1)],
                                              a_bd[:, wpar, hp, 64:128], ident_sb[:, :])
                            if hp == 7:
                                b.inc(mm, "T", 1)
                    at_mm_done[gw] = b.tok("T")

                    @blk.vector
                    def _(v, gw=gw):
                        b.wait(v, "V", at_mm_done[gw])
                        b.wait(v, "V", b.tok("V"))
                        b.wait(v, "V", o_mm_done.get(gw - 1))
                        b.inc(v.tensor_copy(at_bf[:, :], psum_O[:, 0:512].bitcast(BF16)),
                              "V", 1)
                    atevac_done[gw] = b.tok("V")

                    @blk.tensor
                    def _(t_, gw=gw, s=s, n=n, sl0=sl0, sl1=sl1):
                        b.wait(t_, "T", atevac_done[gw])
                        b.wait(t_, "T", tv_done[s * TB + n])
                        b.wait(t_, "T", tv_done[s * TB + n + 1])
                        b.wait(t_, "T", oevac_done.get(gw - 1))
                        b.wait(t_, "T", resid_done.get(2 * (gw - 1) + 1))
                        for hp in range(8):
                            for c4 in range(4):
                                slj = sl0 if c4 < 2 else sl1
                                off = 64 * (c4 % 2)
                                lhsT = v_fm[:, slj, hp, off:off + 64]
                                dst0 = 512 * (hp // 2) + 256 * (hp % 2) + 64 * c4
                                for par in range(2):
                                    mm = t_.matmul(
                                        psum_O[64 * par:64 * par + 64, dst0:dst0 + 64],
                                        lhsT,
                                        at_bf[:, 128 * hp + 64 * par:128 * hp + 64 * par + 64],
                                        start=True, stop=True,
                                        skip_group_check=True)
                                    if hp == 7 and c4 == 3 and par == 1:
                                        b.inc(mm, "T", 1)
                    o_mm_done[gw] = b.tok("T")

                    flush_typf(which=2)
                    prep_trs(2)

                    oevac_done[gw] = ("A", abase[0] + 8)
                    lnstart = abase[0] + (10 if have_w1 else 9)

                    for k2 in range(2):
                        rt = 2 * n + k2
                        grt = s * RTS + rt
                        rsl = grt % 2

                        @blk.sync
                        def _(q, s=s, rt=rt, grt=grt, rsl=rsl):
                            b.wait(q, "Q", resid_done.get(grt - 2))
                            b.inc(q.dma_start(xr32[:, rsl, :],
                                              x_h.ap()[s, 128 * rt:128 * (rt + 1), :]),
                                  "xr%d" % rsl, 16)
                        xr_done[grt] = b.tok("xr%d" % rsl)

                        @blk.tensor
                        def _(t_, gw=gw, n=n, k2=k2, grt=grt, s=s):
                            pj = psum_O[:, 1024 * k2:1024 * (k2 + 1)]
                            rb = 256 * (n % 2) + 128 * k2
                            for c in range(8):
                                lhsT = attn_out[:, c, rb:rb + 128]
                                b.wait(t_, "T", ("A", abase[0] + 1 + max(c, 1 + 4 * k2)))
                                t_.matmul(pj[:, 0:512], lhsT,
                                          wout_bf[:, c, 0:512],
                                          start=(c == 0), stop=False,
                                          skip_group_check=True)
                                b.wait(t_, "T", ("A", abase[0] + 1 + max(c, 3 + 4 * k2)))
                                t_.matmul(pj[:, 512:1024], lhsT,
                                          wout_bf[:, c, 512:1024],
                                          start=(c == 0), stop=False,
                                          skip_group_check=True)
                            b.inc(t_.matmul(pj[:, 0:512], ones128[:, :],
                                            bo_rep[:, 0:512], start=False, stop=True,
                                            skip_group_check=True), "T", 1)
                            b.inc(t_.matmul(pj[:, 512:1024], ones128[:, :],
                                            bo_rep[:, 512:1024], start=False, stop=True,
                                            skip_group_check=True), "T", 1)
                        proj_done[2 * gw + k2] = b.tok("T")

                        @blk.vector
                        def _(v, grt=grt, rsl=rsl, gw=gw, k2=k2):
                            b.wait(v, "V", proj_done[2 * gw + k2])
                            b.wait(v, "V", xr_done[grt])
                            b.wait(v, "V", apply_done.get(grt - 2))
                            b.wait(v, "V", b.tok("V"))
                            b.inc(v.tensor_tensor(y1pre[:, rsl, :],
                                                  psum_O[:, 1024 * k2:1024 * (k2 + 1)],
                                                  xr32[:, rsl, :], ALU.add), "V", 1)
                        resid_done[grt] = b.tok("V")
                        last_resid = resid_done[grt]

                        apply_done[grt] = ("A", lnstart + 11 + 12 * k2)

                        @blk.gpsimd
                        def _(g, s=s, rt=rt, grt=grt, rsl=rsl, lnstart=lnstart):
                            b.wait(g, "G", ("A", lnstart + 10 + 12 * k2))
                            b.inc(g.dma_start(y1f_h.ap()[s, 128 * rt:128 * (rt + 1), :],
                                              y1_f32[:, rsl, :]), "pk%d" % rsl, 16)
                        pk_done[grt] = b.tok("pk%d" % rsl)

                    pending_typf[0] = (gw, s, n)

                if have_w1 or have_w2:
                    w2ctx = (gw, s, n) if have_w2 else None
                    w1gw = gw1 if have_w1 else None

                    @blk.scalar
                    def _(a, w2ctx=w2ctx, w1gw=w1gw, abase=abase):
                        assert b.cnt["A"] == abase[0], (b.cnt["A"], abase[0])
                        if w2ctx is not None:
                            gw, s_, n = w2ctx
                            b.wait(a, "A", o_mm_done[gw])
                            b.wait(a, "A", proj_done.get(2 * (gw - 2) + 1))
                            rb = 256 * (n % 2)
                            for hp in range(8):
                                base = 512 * (hp // 2) + 256 * (hp % 2)
                                b.inc(a.activation(
                                    attn_out[:, hp, rb:rb + 256]
                                    .rearrange("p (d c) -> p d c", c=4),
                                    psum_O[:, base:base + 256]
                                    .rearrange("p (c d) -> p d c", c=4),
                                    AF.Copy), "A", 1)
                        if w1gw is not None:
                            b.wait(a, "A", ssub_done[w1gw])
                            b.wait(a, "A", scale_done.get(w1gw - 1))
                            b.inc(a.activation(aexp[:, :], ssub[:, :], AF.Exp, scale=0.125),
                                  "A", 1)
                        if w2ctx is not None:
                            gw, s_, n = w2ctx
                            for k2 in range(2):
                                grt = s_ * RTS + 2 * n + k2
                                rsl = grt % 2
                                b.wait(a, "A", resid_done[grt])
                                b.wait(a, "A", b.tok("A"))
                                b.inc(a.activation(y1sq[:, :], y1pre[:, rsl, :], AF.Identity,
                                                   accum_out=sums[:, rsl, :]), "A", 1)
                                b.wait(a, "A", b.tok("A"))
                                b.inc(a.activation(y1sq[:, :], y1pre[:, rsl, :], AF.Square,
                                                   accum_out=sumsq[:, rsl, :]), "A", 1)
                                b.wait(a, "A", b.tok("A"))
                                b.inc(a.activation(lnm[:, rsl, :], sums[:, rsl, :], AF.Identity,
                                                   scale=1.0 / D), "A", 1)
                                b.wait(a, "A", b.tok("A"))
                                b.inc(a.activation(lnmsq[:, rsl, :], lnm[:, rsl, :], AF.Square),
                                      "A", 1)
                                b.inc(a.activation(lnvpe[:, rsl, :], sumsq[:, rsl, :],
                                                   AF.Identity, scale=1.0 / D), "A", 1)
                                b.wait(a, "A", b.tok("A"))
                                b.inc(a.activation(lnvpe[:, rsl, :], lnmsq[:, rsl, :],
                                                   AF.Identity,
                                                   scale=-1.0, bias=lnvpe[:, rsl, :]), "A", 1)
                                b.wait(a, "A", b.tok("A"))
                                b.inc(a.activation(lnln[:, rsl, :], lnvpe[:, rsl, :], AF.Ln,
                                                   bias=lneps[:, :]), "A", 1)
                                b.wait(a, "A", b.tok("A"))
                                b.inc(a.activation(lnrstd[:, rsl, :], lnln[:, rsl, :], AF.Exp,
                                                   scale=-0.5), "A", 1)
                                b.wait(a, "A", b.tok("A"))
                                b.inc(a.activation(lnmb[:, rsl, :], lnm[:, rsl, :], AF.Identity,
                                                   scale=lnrstd[:, rsl, :]), "A", 1)
                                b.wait(a, "A", b.tok("A"))
                                b.inc(a.activation(lnmb[:, rsl, :], lnmb[:, rsl, :], AF.Identity,
                                                   scale=-1.0), "A", 1)
                                b.wait(a, "A", b.tok("A"))
                                b.wait(a, "A", pk_done.get(grt - 2))
                                b.wait(a, "A", tymm_done.get(grt - 2))
                                b.inc(a.activation(y1_f32[:, rsl, :], y1pre[:, rsl, :],
                                                   AF.Identity, bias=lnmb[:, rsl, :],
                                                   scale=lnrstd[:, rsl, :]), "A", 1)
                                b.inc(a.activation(y1_bf[:, rsl, :], y1pre[:, rsl, :],
                                                   AF.Identity, bias=lnmb[:, rsl, :],
                                                   scale=lnrstd[:, rsl, :]), "A", 1)

                if have_w1:
                    @blk.vector
                    def _(v, gw1=gw1):
                        b.wait(v, "V", exp_done[gw1])
                        b.wait(v, "V", b.tok("V"))
                        b.inc(v.reduce_sum(asum[:, :],
                                           aexp[:, :].rearrange("p (s x) -> p s x", s=8),
                                           axis=AX.X), "V", 1)
                        b.wait(v, "V", b.tok("V"))
                        b.inc(v.reciprocal(arecip[:, :], asum[:, :]), "V", 1)
                        b.wait(v, "V", b.tok("V"))
                        ar = arecip[:, :].unsqueeze(2).broadcast_to([128, 8, 64])
                        b.inc(v.tensor_tensor(a_bf[:, :].rearrange("p (s x) -> p s x", s=8),
                                              aexp[:, :].rearrange("p (s x) -> p s x", s=8),
                                              ar, ALU.mult), "V", 1)
                    scale_done[gw1] = b.tok("V")

                    @blk.vector
                    def _(v, gw1=gw1):
                        b.wait(v, "V", scale_done[gw1])
                        b.wait(v, "V", b.tok("V"))
                        b.wait(v, "V", at_mm_done.get(gw1 - 2))
                        b.wait(v, "V", memset_tok)
                        wp1 = gw1 % 2
                        b.inc(v.tensor_copy(
                            a_bd[0:64, wp1, :, 0:64],
                            a_bf[0:64, :].rearrange("p (h x) -> p h x", h=8)), "V", 1)
                        b.inc(v.tensor_copy(
                            a_bd[64:128, wp1, :, 64:128],
                            a_bf[64:128, :].rearrange("p (h x) -> p h x", h=8)), "V", 1)
                    bd_done[gw1] = b.tok("V")

        while wprep_tr[0] < len(wprep_jobs):
            while wprep_ld[0] < len(wprep_jobs) and wprep_ld[0] < wprep_tr[0] + 3:
                prep_load(wprep_ld[0])
                wprep_ld[0] += 1
            prep_transpose(wprep_tr[0])
            wprep_tr[0] += 1
        flush_typf()
        if wgT_holder[0] is None:
            emit_wg_stream(SEQSv * TB - 1)
        wg_tok = b.tok("wg0")

        @blk.sync
        def _(q):
            b.wait(q, "Q", b.tok("pk0"))
            b.wait(q, "Q", b.tok("pk1"))
            b.wait(q, "Q", b.tok("pf0"))
            b.wait(q, "Q", b.tok("pf1"))

    a_end = [b.tok(k) for k in ("T", "V", "A", "G")]
    a_end += [b.tok(k) for k in ("pk0", "pk1", "pf0", "pf1",
                                 "xq0", "xq1", "xq2", "xq3", "xr0", "xr1",
                                 "pw0", "pw1", "wc0", "wc1")]

    es_w.close()
    es_a.close()

    es_b = contextlib.ExitStack()
    SBB = lambda n, sh, dt, **kw: es_b.enter_context(nc.sbuf_tensor(n, sh, dt, **kw))
    wgT = wgT_holder[0]
    wuT = SBB("wuT", [128, 24, D], BF16)
    wdT = SBB("wdT", [128, 24, D], BF16)
    y1fm_sb = SBB("y1fm_sb", [128, 2, 8, 256], BF16)
    y1r32 = SBB("y1r32", [128, 2, 2, D], F32)
    sg_bf = SBB("sg_bf", [128, 2, 256], BF16)
    h_bf = SBB("h_bf", [128, 2, 256], BF16)
    y2pre = SBB("y2pre", [128, 2, D], F32)
    y2sq = SBB("y2sq", [128, D], F32)
    y2 = SBB("y2", [128, 2, 2, D], F32)
    s2s = SBB("s2s", [128, 2, 1], F32)
    s2q = SBB("s2q", [128, 2, 1], F32)
    l2m = SBB("l2m", [128, 2, 1], F32)
    l2msq = SBB("l2msq", [128, 2, 1], F32)
    l2vpe = SBB("l2vpe", [128, 2, 1], F32)
    l2rstd = SBB("l2rstd", [128, 2, 1], F32)
    l2ln = SBB("l2ln", [128, 2, 1], F32)
    l2mb = SBB("l2mb", [128, 2, 1], F32)

    gate_ps = [psum_qkv[:, 0:256], psum_qkv[:, 512:768]]
    up_ps = [psum_S[:, 0:256], psum_TR[:, 0:512].bitcast(F32)[:, 0:256]]
    down_ps = psum_O

    gate_done, silu_done, up_done, mul_done, down_done = {}, {}, {}, {}, {}
    apply2_done = {}
    mb2_toks = {}
    pending_ln2 = [None]
    pending_vln2 = [None]
    dload_done, rload_done, resid2_done, sq2_done, y2_done, store_done = {}, {}, {}, {}, {}, {}
    NGg = SEQSv * (Lv // 256)

    with nc.Block() as blk:
        @blk.tensor
        def _(t_):
            for tk in a_end:
                b.wait(t_, "T", tk)

        @blk.vector
        def _(v):
            for tk in a_end:
                b.wait(v, "V", tk)

        @blk.scalar
        def _(a):
            for tk in a_end:
                b.wait(a, "A", tk)

        @blk.gpsimd
        def _(g):
            for tk in a_end:
                b.wait(g, "G", tk)

        @blk.sync
        def _(q):
            for tk in a_end:
                b.wait(q, "Q", tk)

        wu_tok, wd_tok = {}, {}

        def emit_wu_chunk(hc):
            @blk.sync
            def _(q, hc=hc):
                b.wait(q, "Q", up_done.get((0, hc - 2)))
                b.inc(q.dma_start(wuT[:, hc, :], wuT_h.ap()[hc]),
                      "wl%d" % (hc % 2), 16)
            wu_tok[hc] = b.tok("wl%d" % (hc % 2))

        def emit_wd_chunk(hc):
            @blk.sync
            def _(q, hc=hc):
                b.wait(q, "Q", down_done.get((0, hc - 3)))
                b.inc(q.dma_start(wdT[:, hc, :], wdT_h.ap()[hc]),
                      "wl%d" % (2 + hc % 3), 16)
            wd_tok[hc] = b.tok("wl%d" % (2 + hc % 3))

        def emit_gloads(g):
            s = g // (Lv // 256)
            r0 = 256 * (g % (Lv // 256))
            gsl = g % 2

            @blk.sync
            def _(q, g=g, s=s, r0=r0, gsl=gsl):
                b.wait(q, "Q", up_done.get((g - 2, 23)))
                b.inc(q.dma_start(y1fm_sb[:, gsl, :, :],
                                  y1fm_h.ap()[s, g % (Lv // 256)]), "yf%d" % gsl, 16)
            dload_done[g] = b.tok("yf%d" % gsl)

            @blk.sync
            def _(q, g=g, s=s, r0=r0, gsl=gsl):
                b.wait(q, "Q", resid2_done.get((g - 2, 1)))
                for rs in range(2):
                    b.inc(q.dma_start(y1r32[:, gsl, rs, :],
                                      y1f_h.ap()[s, r0 + 128 * rs:r0 + 128 * (rs + 1), :]),
                          "yr%d" % gsl, 16)
            rload_done[g] = b.tok("yr%d" % gsl)

        emit_gloads(0)
        if NGg > 1:
            emit_gloads(1)
        for c0 in (0, 1):
            emit_wu_chunk(c0)
            emit_wd_chunk(c0)
        emit_wd_chunk(2)

        b2bases = {}

        def nb2(g):
            return b2bases[g] + (24 if g == 0 else 40)

        for g in range(NGg):
            s = g // (Lv // 256)
            r0 = 256 * (g % (Lv // 256))
            gsl = g % 2
            b2base = [b.cnt["A"] if g == 0 else nb2(g - 1)]
            b2bases[g] = b2base[0]

            def emit_down(hc, g=g):
                hsl = hc % 2

                @blk.tensor
                def _(t_, g=g, hc=hc, hsl=hsl):
                    b.wait(t_, "T", wd_tok[hc])
                    b.wait(t_, "T", mul_done[(g, hc)])
                    if hc == 0:
                        b.wait(t_, "T", resid2_done.get((g - 1, 1)))
                    for rs in range(2):
                        lhsT = h_bf[:, hsl, 128 * rs:128 * (rs + 1)]
                        t_.matmul(down_ps[:, 1024 * rs:1024 * rs + 512],
                                  lhsT, wdT[:, hc, 0:512],
                                  start=(hc == 0), stop=(hc == 23),
                                  skip_group_check=True)
                        mm = t_.matmul(down_ps[:, 1024 * rs + 512:1024 * (rs + 1)],
                                       lhsT, wdT[:, hc, 512:1024],
                                       start=(hc == 0), stop=(hc == 23),
                                       skip_group_check=True)
                        if rs == 1:
                            b.inc(mm, "T", 1)
                down_done[(g, hc)] = b.tok("T")

            for hc in range(24):
                hsl = hc % 2

                @blk.tensor
                def _(t_, g=g, gsl=gsl, hc=hc, hsl=hsl):
                    b.wait(t_, "T", wg_tok)
                    b.wait(t_, "T", dload_done[g])
                    b.wait(t_, "T", silu_done.get((g, hc - 2) if hc >= 2 else (g - 1, 22 + hc)))
                    for dc in range(8):
                        mm = t_.matmul(gate_ps[hsl], wgT[:, hc, 128 * dc:128 * (dc + 1)],
                                       y1fm_sb[:, gsl, dc, :],
                                       start=(dc == 0), stop=(dc == 7))
                        if dc == 7:
                            b.inc(mm, "T", 1)
                gate_done[(g, hc)] = b.tok("T")

                @blk.tensor
                def _(t_, g=g, gsl=gsl, hc=hc, hsl=hsl):
                    b.wait(t_, "T", wu_tok[hc])
                    b.wait(t_, "T", mul_done.get((g, hc - 2) if hc >= 2 else (g - 1, 22 + hc)))
                    for dc in range(8):
                        mm = t_.matmul(up_ps[hsl], wuT[:, hc, 128 * dc:128 * (dc + 1)],
                                       y1fm_sb[:, gsl, dc, :],
                                       start=(dc == 0), stop=(dc == 7))
                        if dc == 7:
                            b.inc(mm, "T", 1)
                up_done[(g, hc)] = b.tok("T")

                silu_done[(g, hc)] = ("A", b2base[0] + (
                    1 + hc if hc < 5 or g == 0 else
                    3 + hc if hc < 6 else
                    7 + hc if hc < 7 else
                    9 + hc if hc < 8 else
                    13 + hc if hc < 9 else
                    15 + hc if hc < 11 else
                    16 + hc if hc < 13 else
                    17 + hc))

                @blk.vector
                def _(v, g=g, hc=hc, hsl=hsl):
                    b.wait(v, "V", up_done[(g, hc)])
                    b.wait(v, "V", silu_done[(g, hc)])
                    b.wait(v, "V", down_done.get((g, hc - 2) if hc >= 2 else (g - 1, 22 + hc)))
                    b.wait(v, "V", b.tok("V"))
                    b.inc(v.tensor_tensor(h_bf[:, hsl, :], up_ps[hsl], sg_bf[:, hsl, :],
                                          ALU.mult), "V", 1)
                mul_done[(g, hc)] = b.tok("V")

                if hc == 9 and pending_vln2[0] is not None:
                    pending_vln2[0](0)
                if hc == 10 and pending_vln2[0] is not None:
                    pending_vln2[0](1)
                    pending_vln2[0] = None
                if hc >= 1:
                    emit_down(hc - 1)
                if g == 0 and hc + 2 <= 23:
                    emit_wu_chunk(hc + 2)
                    if hc >= 1:
                        emit_wd_chunk(hc + 2)
            emit_down(23)

            for rs in range(2):
                @blk.vector
                def _(v, g=g, gsl=gsl, rs=rs):
                    b.wait(v, "V", down_done[(g, 23)])
                    b.wait(v, "V", rload_done[g])
                    b.wait(v, "V", apply2_done.get((g - 1, rs)))
                    b.wait(v, "V", b.tok("V"))
                    b.inc(v.tensor_tensor(y2pre[:, rs, :], down_ps[:, 1024 * rs:1024 * (rs + 1)],
                                          y1r32[:, gsl, rs, :], ALU.add), "V", 1)
                resid2_done[(g, rs)] = b.tok("V")

            for rs in range(2):
                apply2_done[(g, rs)] = ("A", nb2(g) + (8 + 8 * rs if g == NGg - 1 else 26 + 3 * rs))

            if g + 2 < NGg:
                emit_gloads(g + 2)

            def emit_vln2(rs, g=g, s=s, r0=r0):
                rstd2_done = ("A", nb2(g) + (7 + 8 * rs if g == NGg - 1 else 22 + rs))

                @blk.vector
                def _(v, g=g, rs=rs, rstd2_done=rstd2_done):
                    b.wait(v, "V", rstd2_done)
                    b.wait(v, "V", b.tok("V"))
                    b.inc(v.reciprocal(l2rstd[:, rs, :], l2ln[:, rs, :]), "V", 1)
                    b.wait(v, "V", b.tok("V"))
                    b.inc(v.scalar_tensor_tensor(l2mb[:, rs, :], l2m[:, rs, :],
                                                 -1.0, l2rstd[:, rs, :],
                                                 ALU.mult, ALU.mult), "V", 1)
                mb2_toks[(g, rs)] = b.tok("V")

                @blk.gpsimd
                def _(g_, g=g, rs=rs, s=s, r0=r0):
                    b.wait(g_, "G", apply2_done[(g, rs)])
                    b.inc(g_.dma_start(out_h.ap()[s, r0 + 128 * rs:r0 + 128 * (rs + 1), :],
                                       y2[:, g % 2, rs, :]), "st%d" % (g % 2), 16)
                store_done[(g, rs)] = b.tok("st%d" % (g % 2))

            pending_vln2[0] = emit_vln2

            @blk.scalar
            def _(a, g=g, b2base=b2base, prev_ln2=pending_ln2[0]):
                assert b.cnt["A"] == b2base[0], (b.cnt["A"], b2base[0])
                for hc in range(24):
                    hsl = hc % 2
                    b.wait(a, "A", gate_done[(g, hc)])
                    b.wait(a, "A", mul_done.get((g, hc - 2) if hc >= 2 else (g - 1, 22 + hc)))
                    b.inc(a.activation(sg_bf[:, hsl, :], gate_ps[hsl], AF.Silu), "A", 1)
                    if prev_ln2 is None:
                        continue
                    gp = prev_ln2
                    if hc in (4, 6):
                        rs = 0 if hc == 4 else 1
                        b.wait(a, "A", resid2_done[(gp, rs)])
                        b.wait(a, "A", b.tok("A"))
                        b.inc(a.activation(y2sq[:, :], y2pre[:, rs, :], AF.Identity,
                                           accum_out=s2s[:, rs, :]), "A", 1)
                        b.wait(a, "A", b.tok("A"))
                        b.inc(a.activation(y2sq[:, :], y2pre[:, rs, :], AF.Square,
                                           accum_out=s2q[:, rs, :]), "A", 1)
                    if hc in (5, 7):
                        rs = 0 if hc == 5 else 1
                        b.wait(a, "A", b.tok("A"))
                        b.inc(a.activation(l2m[:, rs, :], s2s[:, rs, :], AF.Identity,
                                           scale=1.0 / D), "A", 1)
                        b.wait(a, "A", b.tok("A"))
                        b.inc(a.activation(l2msq[:, rs, :], l2m[:, rs, :], AF.Square), "A", 1)
                        b.inc(a.activation(l2vpe[:, rs, :], s2q[:, rs, :], AF.Identity,
                                           scale=1.0 / D), "A", 1)
                        b.wait(a, "A", b.tok("A"))
                        b.inc(a.activation(l2vpe[:, rs, :], l2msq[:, rs, :], AF.Identity,
                                           scale=-1.0, bias=l2vpe[:, rs, :]), "A", 1)
                    if hc == 8:
                        for rs in range(2):
                            b.wait(a, "A", b.tok("A"))
                            b.inc(a.activation(l2ln[:, rs, :], l2vpe[:, rs, :], AF.Sqrt,
                                               bias=lneps[:, :]), "A", 1)
                    if hc in (10, 12):
                        rs = 0 if hc == 10 else 1
                        b.wait(a, "A", mb2_toks[(gp, rs)])
                        b.wait(a, "A", store_done.get((gp - 2, 1)))
                        b.inc(a.activation(y2[:, gp % 2, rs, :], y2pre[:, rs, :], AF.Identity,
                                           bias=l2mb[:, rs, :], scale=l2rstd[:, rs, :]),
                              "A", 1)
            pending_ln2[0] = g

        if pending_vln2[0] is not None:
            pending_vln2[0](0)
            pending_vln2[0](1)
            pending_vln2[0] = None

        @blk.scalar
        def _(a, prev_ln2=pending_ln2[0]):
            gp = prev_ln2
            for rs in range(2):
                b.wait(a, "A", resid2_done[(gp, rs)])
                b.wait(a, "A", b.tok("A"))
                b.inc(a.activation(y2sq[:, :], y2pre[:, rs, :], AF.Identity,
                                   accum_out=s2s[:, rs, :]), "A", 1)
                b.wait(a, "A", b.tok("A"))
                b.inc(a.activation(y2sq[:, :], y2pre[:, rs, :], AF.Square,
                                   accum_out=s2q[:, rs, :]), "A", 1)
                b.wait(a, "A", b.tok("A"))
                b.inc(a.activation(l2m[:, rs, :], s2s[:, rs, :], AF.Identity,
                                   scale=1.0 / D), "A", 1)
                b.wait(a, "A", b.tok("A"))
                b.inc(a.activation(l2msq[:, rs, :], l2m[:, rs, :], AF.Square), "A", 1)
                b.inc(a.activation(l2vpe[:, rs, :], s2q[:, rs, :], AF.Identity,
                                   scale=1.0 / D), "A", 1)
                b.wait(a, "A", b.tok("A"))
                b.inc(a.activation(l2vpe[:, rs, :], l2msq[:, rs, :], AF.Identity,
                                   scale=-1.0, bias=l2vpe[:, rs, :]), "A", 1)
                b.wait(a, "A", b.tok("A"))
                b.inc(a.activation(l2ln[:, rs, :], l2vpe[:, rs, :], AF.Sqrt,
                                   bias=lneps[:, :]), "A", 1)
                b.wait(a, "A", mb2_toks[(gp, rs)])
                b.wait(a, "A", store_done.get((gp - 2, 1)))
                b.inc(a.activation(y2[:, gp % 2, rs, :], y2pre[:, rs, :], AF.Identity,
                                   bias=l2mb[:, rs, :], scale=l2rstd[:, rs, :]), "A", 1)

        @blk.gpsimd
        def _(g_):
            g_.wait_ge(b.sems["st0"], b.cnt["st0"])
            g_.wait_ge(b.sems["st1"], b.cnt["st1"])

    es_b.close()
    es_outer.close()
    return nc


_IDENT = np.eye(128, dtype=ml_dtypes.bfloat16)


def kernel(L0_attn_x, L0_attn_W_qkv, L0_attn_b_qkv, L0_attn_W_out, L0_attn_b_out,
           L0_ffn_w_gate, L0_ffn_w_up, L0_ffn_w_down):
    x = np.ascontiguousarray(np.asarray(L0_attn_x, dtype=np.float32))
    wqkv = np.ascontiguousarray(np.asarray(L0_attn_W_qkv, dtype=np.float32))
    bqkv = np.ascontiguousarray(np.asarray(L0_attn_b_qkv, dtype=np.float32)).reshape(1, 3 * D)
    wout = np.ascontiguousarray(np.asarray(L0_attn_W_out, dtype=np.float32))
    bout = np.ascontiguousarray(np.asarray(L0_attn_b_out, dtype=np.float32)).reshape(1, D)
    wg = np.ascontiguousarray(np.asarray(L0_ffn_w_gate, dtype=np.int32))
    wu = np.ascontiguousarray(np.asarray(L0_ffn_w_up, dtype=np.int32))
    wd = np.ascontiguousarray(np.asarray(L0_ffn_w_down, dtype=np.int32))

    nc = build_program()
    in_maps = []
    for i in range(NCORES):
        in_maps.append({
            "x": x[SEQS * i:SEQS * (i + 1)],
            "wqkv": wqkv, "bqkv": bqkv, "wout": wout, "bout": bout,
            "wg": wg, "wu": wu, "wd": wd, "ident": _IDENT,
        })
    res = run_bass_kernel_spmd(nc, in_maps, core_ids=list(range(NCORES)))
    return np.concatenate([r["out"] for r in res.results], axis=0)
